# version 02
# baseline (speedup 1.0000x reference)
import numpy as np
import ml_dtypes
import concourse.bass as bass
import concourse.mybir as mybir
from concourse.bass_utils import run_bass_kernel_spmd

F32 = mybir.dt.float32
BF16 = mybir.dt.bfloat16
AF = mybir.ActivationFunctionType
ALU = mybir.AluOpType
BF = ml_dtypes.bfloat16

D = 1024
NH = 16
DK = 64
DFF = 4096
RANK = 256
T = 512
NSUB = T // 128
NG = T // 8
EPS = 1e-5

ENGS = ("sync", "gpsimd", "tensor", "vector", "scalar")


class Prog:
    def __init__(self):
        self.ops = {e: [] for e in ENGS}
        self.ctr = {}
        self.seq = 0
        self.marks = {}

    def mark(self, name):
        self.marks[name] = self.seq

    def c(self, sem):
        return self.ctr.get(sem, 0)

    def truncate(self, limit_seq):
        self.ctr = {}
        for e in ENGS:
            kept = [op for op in self.ops[e] if op[3] <= limit_seq]
            self.ops[e] = kept
            for fn, waits, tok, sq in kept:
                if tok is not None:
                    self.ctr[tok[0]] = self.ctr.get(tok[0], 0) + tok[1]

    def emit(self, eng, fn, waits=(), inc=None, inc_k=1):
        tok = None
        if inc is not None:
            self.ctr[inc] = self.ctr.get(inc, 0) + inc_k
            tok = (inc, inc_k)
        self.seq += 1
        self.ops[eng].append((fn, [w for w in waits if w[1] > 0], tok, self.seq))

    def dma(self, eng, out_ap, in_ap, waits=(), inc="dma", slow=False):
        self.emit(eng, lambda e, o=out_ap, i=in_ap, sl=slow: e.dma_start(
            o, i, allow_slow_non_contiguous=sl),
            waits=waits, inc=inc, inc_k=16)

    def drain(self, eng):
        self.ops[eng].append((lambda e: e.drain(), [], None, self.seq))

    def wall(self, *sems):
        return [(s, self.c(s)) for s in sems]

    def realize(self, block, sems):
        for eng_name in ENGS:
            seq = self.ops[eng_name]
            if not seq:
                continue
            def mk(seq=seq):
                def body(eng):
                    for fn, waits, tok, _sq in seq:
                        for s, v in waits:
                            eng.wait_ge(sems[s], v)
                        instr = fn(eng)
                        if tok is not None:
                            instr.then_inc(sems[tok[0]], tok[1])
                return body
            getattr(block, eng_name)(mk())


def ap(t, offset, dims):
    if isinstance(t, bass.AP):
        t = t.tensor
    return bass.AP(t, offset, [list(d) for d in dims])


def build_core_program(n_tiles, debug=(), phase=None, skip=()):
    nc = bass.Bass("TRN2", target_bir_lowering=False)
    NTOK = n_tiles * T

    x_h = nc.dram_tensor("x", [NTOK, D], F32, kind="ExternalInput").ap()
    y_h = nc.dram_tensor("y", [NTOK, D], F32, kind="ExternalOutput").ap()
    wq_h = nc.dram_tensor("wq", [D, D], F32, kind="ExternalInput").ap()
    wk_h = nc.dram_tensor("wk", [D, D], F32, kind="ExternalInput").ap()
    wv_h = nc.dram_tensor("wv", [D, D], F32, kind="ExternalInput").ap()
    bq_h = nc.dram_tensor("bq", [1, D], F32, kind="ExternalInput").ap()
    bk_h = nc.dram_tensor("bk", [1, D], F32, kind="ExternalInput").ap()
    bv_h = nc.dram_tensor("bv", [1, D], F32, kind="ExternalInput").ap()
    u1_h = nc.dram_tensor("u1", [D, RANK], F32, kind="ExternalInput").ap()
    v1_h = nc.dram_tensor("v1", [RANK, DFF], F32, kind="ExternalInput").ap()
    b1_h = nc.dram_tensor("b1", [1, DFF], F32, kind="ExternalInput").ap()
    u2_h = nc.dram_tensor("u2", [DFF, RANK], F32, kind="ExternalInput").ap()
    v2_h = nc.dram_tensor("v2", [RANK, D], F32, kind="ExternalInput").ap()
    b2_h = nc.dram_tensor("b2", [1, D], F32, kind="ExternalInput").ap()
    c_h = nc.dram_tensor("cvec", [1, D], F32, kind="ExternalInput").ap()
    wmod_h = nc.dram_tensor("wmod", [D, 6 * D], F32, kind="ExternalInput").ap()
    ident_h = nc.dram_tensor("ident", [128, 128], BF16, kind="ExternalInput").ap()
    i64d_h = nc.dram_tensor("i64d", [128, 64], BF16, kind="ExternalInput").ap()
    maskq_h = nc.dram_tensor("maskq", [9, 8192], BF16, kind="ExternalInput").ap()
    maskk_h = nc.dram_tensor("maskk", [9, 8192], BF16, kind="ExternalInput").ap()
    dbg = {}
    for name, shape, dt in debug:
        dbg[name] = nc.dram_tensor(name, list(shape), dt, kind="ExternalOutput").ap()

    p = Prog()
    SEMS = ["dma", "cast", "tmm", "vec", "sca", "xin", "xout", "shuf"]

    from contextlib import ExitStack
    with ExitStack() as ctx:
        wq_sb = ctx.enter_context(nc.sbuf_tensor("wq_sb", [128, 8192], BF16))
        wk_sb = ctx.enter_context(nc.sbuf_tensor("wk_sb", [128, 8192], BF16))
        wv_sb = ctx.enter_context(nc.sbuf_tensor("wv_sb", [128, 8192], BF16))
        u1_sb = ctx.enter_context(nc.sbuf_tensor("u1_sb", [128, 2048], BF16))
        v1_sb = ctx.enter_context(nc.sbuf_tensor("v1_sb", [128, 8192], BF16))
        u2_sb = ctx.enter_context(nc.sbuf_tensor("u2_sb", [128, 8192], BF16))
        v2_sb = ctx.enter_context(nc.sbuf_tensor("v2_sb", [128, 2048], BF16))
        ident = ctx.enter_context(nc.sbuf_tensor("ident_sb", [128, 128], BF16))
        i64d = ctx.enter_context(nc.sbuf_tensor("i64d_sb", [128, 64], BF16))
        qop = ctx.enter_context(nc.sbuf_tensor("qop", [73, 8192], BF16))
        kop = ctx.enter_context(nc.sbuf_tensor("kop", [73, 8192], BF16))
        vop = ctx.enter_context(nc.sbuf_tensor("vop", [128, 4160], BF16))
        bcols = ctx.enter_context(nc.sbuf_tensor("bcols", [128, 80], F32))
        modc = ctx.enter_context(nc.sbuf_tensor("modc", [128, 56], F32))
        rowbuf = ctx.enter_context(nc.sbuf_tensor("rowbuf", [1, 512], F32))
        rowbf = ctx.enter_context(nc.sbuf_tensor("rowbf", [1, 1024], BF16))
        scolb = ctx.enter_context(nc.sbuf_tensor("scolb", [128, 24], BF16))
        onescal = ctx.enter_context(nc.sbuf_tensor("onescal", [1, 8], BF16))
        small = ctx.enter_context(nc.sbuf_tensor("small", [128, 64], F32))
        x_t = ctx.enter_context(nc.sbuf_tensor("x_t", [128, 4096], F32))
        x2_t = ctx.enter_context(nc.sbuf_tensor("x2_t", [128, 4096], F32))
        arena = ctx.enter_context(nc.sbuf_tensor("arena", [128, 4096], BF16))
        xhatT = ctx.enter_context(nc.sbuf_tensor("xhatT", [128, 4096], BF16))
        qtmp = ctx.enter_context(nc.sbuf_tensor("qtmp", [128, 4096], BF16))
        oddscr = ctx.enter_context(nc.sbuf_tensor("oddscr", [64, 1024], BF16))
        vint = ctx.enter_context(nc.sbuf_tensor("vint", [128, 4096], BF16))
        eop = ctx.enter_context(nc.sbuf_tensor("eop", [128, 1024], BF16))
        den = ctx.enter_context(nc.sbuf_tensor("den", [128, 16], F32))
        ptsb = ctx.enter_context(nc.sbuf_tensor("ptsb", [128, 1024], BF16))
        htring = ctx.enter_context(nc.sbuf_tensor("htring", [128, 1024], BF16))
        q2sb = ctx.enter_context(nc.sbuf_tensor("q2sb", [128, 1024], BF16))
        pb0 = ctx.enter_context(nc.psum_tensor("pb0", [128, 512], F32))
        pb1 = ctx.enter_context(nc.psum_tensor("pb1", [128, 512], F32))
        pb2 = ctx.enter_context(nc.psum_tensor("pb2", [128, 512], F32))
        pb3 = ctx.enter_context(nc.psum_tensor("pb3", [128, 512], F32))
        pb4 = ctx.enter_context(nc.psum_tensor("pb4", [128, 512], F32))
        pb5 = ctx.enter_context(nc.psum_tensor("pb5", [128, 512], F32))
        pb6 = ctx.enter_context(nc.psum_tensor("pb6", [128, 512], F32))
        pb7 = ctx.enter_context(nc.psum_tensor("pb7", [128, 512], F32))
        s_dma = ctx.enter_context(nc.semaphore("s_dma"))
        s_cast = ctx.enter_context(nc.semaphore("s_cast"))
        s_tmm = ctx.enter_context(nc.semaphore("s_tmm"))
        s_vec = ctx.enter_context(nc.semaphore("s_vec"))
        s_sca = ctx.enter_context(nc.semaphore("s_sca"))
        s_xin = ctx.enter_context(nc.semaphore("s_xin"))
        s_xout = ctx.enter_context(nc.semaphore("s_xout"))
        s_shuf = ctx.enter_context(nc.semaphore("s_shuf"))
        sems = {"dma": s_dma, "cast": s_cast, "tmm": s_tmm, "vec": s_vec,
                "sca": s_sca, "xin": s_xin, "xout": s_xout, "shuf": s_shuf}
        PB = [pb0, pb1, pb2, pb3, pb4, pb5, pb6, pb7]

        p.dma("sync", ident[:], ident_h)
        p.dma("sync", i64d[:], i64d_h)
        p.dma("sync", qop[64:73, :], maskq_h)
        p.dma("sync", kop[64:73, :], maskk_h)
        p.dma("sync", rowbuf[0:1, 0:512], ap(c_h, 0, [[1, 1], [1, 512]]))
        p.dma("sync", ap(bcols, 32, [[80, 128], [1, 32]]),
              ap(b1_h, 0, [[1, 128], [128, 32]]), slow=True)
        p.dma("sync", ap(bcols, 64, [[80, 128], [1, 8]]),
              ap(b2_h, 0, [[1, 128], [128, 8]]), slow=True)
        p.dma("sync", ap(small, 0, [[64, 128], [1, 8]]),
              ap(bq_h, 0, [[1, 128], [128, 8]]), slow=True)
        p.dma("sync", ap(small, 8, [[64, 128], [1, 8]]),
              ap(bk_h, 0, [[1, 128], [128, 8]]), slow=True)
        p.dma("sync", ap(small, 16, [[64, 128], [1, 8]]),
              ap(bv_h, 0, [[1, 128], [128, 8]]), slow=True)
        consts_done = p.c("dma")

        for w_sb, w_hh in ((wq_sb, wq_h), (wk_sb, wk_h), (wv_sb, wv_h)):
            p.dma("gpsimd", w_sb[:],
                  ap(w_hh, 0, [[1024, 128], [128 * 1024, 8], [1, 1024]]), inc="cast")
        p.dma("gpsimd", u1_sb[:],
              ap(u1_h, 0, [[256, 128], [128 * 256, 8], [1, 256]]), inc="cast")
        p.dma("gpsimd", v1_sb[:],
              ap(v1_h, 0, [[4096, 128], [128 * 4096, 2], [1, 4096]]), inc="cast")
        p.dma("gpsimd", u2_sb[:],
              ap(u2_h, 0, [[256, 128], [128 * 256, 32], [1, 256]]), inc="cast")
        p.dma("gpsimd", v2_sb[:],
              ap(v2_h, 0, [[1024, 128], [128 * 1024, 2], [1, 1024]]), inc="cast")
        w_done = p.c("cast")

        p.emit("vector", lambda e: e.memset(onescal[:], 1.0), inc="vec")
        p.emit("vector", lambda e: e.memset(
            ap(vop, 64, [[4160, 128], [65, 64]]), 1.0), inc="vec")
        v_ones = p.c("vec")

        p.emit("scalar", lambda e: e.activation(rowbf[0:1, 0:512], rowbuf[0:1, 0:512],
                                                AF.Silu),
               waits=[("dma", consts_done)], inc="sca")
        s_h1 = p.c("sca")
        p.dma("sync", rowbuf[0:1, 0:512], ap(c_h, 512, [[1, 1], [1, 512]]),
              waits=[("sca", s_h1)])
        c2_done = p.c("dma")
        p.emit("scalar", lambda e: e.activation(rowbf[0:1, 512:1024], rowbuf[0:1, 0:512],
                                                AF.Silu),
               waits=[("dma", c2_done)], inc="sca")
        s_silu = p.c("sca")
        for k in range(8):
            p.emit("tensor", lambda e, k=k: e.matmul(
                pb6[:, k:k + 1], rowbf[0:1, k * 128:(k + 1) * 128], onescal[0:1, 0:1],
                start=True, stop=True),
                waits=(p.wall("sca", "vec") if k == 0 else []), inc="tmm")
        t_scol = p.c("tmm")
        p.emit("vector", lambda e: e.tensor_copy(scolb[:, 0:8], pb6[:, 0:8]),
               waits=[("tmm", t_scol)], inc="vec")
        v_scolb = p.c("vec")

        for n in range(12):
            ring = vint if n % 2 == 0 else qtmp
            p.dma("gpsimd", ap(ring, 0, [[4096, 128], [512, 8], [1, 512]]),
                  ap(wmod_h, n * 512, [[6144, 128], [128 * 6144, 8], [1, 512]]),
                  waits=(p.wall("tmm") if n >= 2 else []),
                  inc="cast")
            cd = p.c("cast")
            for k in range(8):
                p.emit("tensor", lambda e, k=k, ring=ring: e.matmul(
                    pb7[0:1, 0:512], scolb[:, k:k + 1],
                    ring[:, k * 512:(k + 1) * 512],
                    start=(k == 0), stop=(k == 7)),
                    waits=(([("cast", cd), ("vec", v_scolb)] + p.wall("sca"))
                           if k == 0 else []),
                    inc="tmm")
            t_mod = p.c("tmm")
            p.emit("scalar", lambda e: e.copy(rowbuf[0:1, 0:512], pb7[0:1, 0:512]),
                   waits=[("tmm", t_mod)], inc="sca")
            p.drain("scalar")
            p.emit("scalar", lambda e: e.copy(rowbf[0:1, 0:512], rowbuf[0:1, 0:512]),
                   inc="sca")
            s_rb = p.c("sca")
            for j in range(4):
                col = n * 4 + j
                p.emit("tensor", lambda e, j=j: e.matmul(
                    pb6[:, 8:9], rowbf[0:1, j * 128:(j + 1) * 128],
                    onescal[0:1, 0:1], start=True, stop=True),
                    waits=[("sca", s_rb)] + p.wall("vec"), inc="tmm")
                t_c = p.c("tmm")
                p.emit("vector", lambda e, col=col: e.tensor_copy(
                    modc[:, col:col + 1], pb6[:, 8:9]),
                    waits=[("tmm", t_c)], inc="vec")
        v_modc = p.c("vec")

        def emit_gate(c0, sg0, mg0, prev_v):
            p.emit("scalar", lambda e, c0=c0, sg0=sg0: e.sign(
                small[:, sg0:sg0 + 8], modc[:, c0:c0 + 8]),
                waits=[("vec", v_modc), ("vec", prev_v)], inc="sca")
            p.emit("scalar", lambda e, c0=c0, mg0=mg0: e.activation(
                small[:, mg0:mg0 + 8], modc[:, c0:c0 + 8], AF.Abs), inc="sca")
            s_g = p.c("sca")
            p.emit("vector", lambda e, mg0=mg0: e.tensor_scalar_max(
                small[:, mg0:mg0 + 8], small[:, mg0:mg0 + 8], 0.25),
                waits=[("sca", s_g), ("vec", v_modc)], inc="vec")
            p.drain("vector")
            p.emit("vector", lambda e, c0=c0, sg0=sg0, mg0=mg0: e.tensor_mul(
                modc[:, c0:c0 + 8], small[:, sg0:sg0 + 8], small[:, mg0:mg0 + 8]),
                inc="vec")
            return p.c("vec")
        print("PLAN: before gates seq=", p.seq)
        g1 = emit_gate(16, 32, 40, 0)
        emit_gate(40, 56, 32, g1)
        print("PLAN: after gates seq=", p.seq)
        p.emit("vector", lambda e: e.tensor_scalar_add(modc[:, 48:56], modc[:, 8:16], 1.0),
               inc="vec")
        p.emit("vector", lambda e: e.tensor_scalar_add(small[:, 48:56], modc[:, 32:40], 1.0),
               inc="vec")
        p.emit("vector", lambda e: e.tensor_copy(scolb[:, 8:16], modc[:, 0:8]), inc="vec")
        p.emit("vector", lambda e: e.tensor_copy(scolb[:, 16:24], modc[:, 24:32]), inc="vec")
        v_gates = p.c("vec")

        def emit_bprime(w_sb, scol_off, raw_off, dst_off, width, wstride):
            nch = max(1, width // 512)
            for n in range(nch):
                for k in range(8):
                    p.emit("tensor", lambda e, n=n, k=k, w_sb=w_sb, scol_off=scol_off,
                           wstride=wstride: e.matmul(
                        pb7[0:1, 0:512] if wstride >= 512 else pb7[0:1, 0:wstride],
                        scolb[:, scol_off + k:scol_off + k + 1],
                        w_sb[:, k * wstride + n * 512:
                             k * wstride + n * 512 + min(512, wstride)],
                        start=(k == 0), stop=(k == 7)),
                        waits=(([("vec", v_gates)] + p.wall("sca")) if (n == 0 and k == 0)
                               else []),
                        inc="tmm")
                t_b = p.c("tmm")
                p.emit("scalar", lambda e, wstride=wstride: e.copy(
                    rowbf[0:1, 0:min(512, wstride)], pb7[0:1, 0:min(512, wstride)]),
                    waits=[("tmm", t_b)], inc="sca")
                s_b = p.c("sca")
                ncols = min(512, wstride) // 128
                for j in range(ncols):
                    cc = n * 4 + j
                    p.emit("tensor", lambda e, j=j: e.matmul(
                        pb6[:, 8:9], rowbf[0:1, j * 128:(j + 1) * 128],
                        onescal[0:1, 0:1], start=True, stop=True),
                        waits=[("sca", s_b)] + p.wall("vec"), inc="tmm")
                    t_c = p.c("tmm")
                    p.emit("vector", lambda e, cc=cc, dst_off=dst_off: e.tensor_copy(
                        bcols[:, dst_off + cc:dst_off + cc + 1], pb6[:, 8:9]),
                        waits=[("tmm", t_c)], inc="vec")
            if raw_off is not None:
                p.drain("vector")
                p.emit("vector", lambda e, dst_off=dst_off, raw_off=raw_off: e.tensor_add(
                    bcols[:, dst_off:dst_off + width // 128],
                    bcols[:, dst_off:dst_off + width // 128],
                    small[:, raw_off:raw_off + width // 128]),
                    waits=[("dma", consts_done)], inc="vec")

        emit_bprime(wq_sb, 8, 0, 0, 1024, 1024)
        emit_bprime(wk_sb, 8, 8, 8, 1024, 1024)
        emit_bprime(wv_sb, 8, 16, 16, 1024, 1024)
        emit_bprime(u1_sb, 16, None, 24, 256, 256)
        v_bcols = p.c("vec")

        for w_sb in (wq_sb, wk_sb, wv_sb):
            for k in range(8):
                p.emit("vector", lambda e, w_sb=w_sb, k=k: e.tensor_scalar_mul(
                    w_sb[:, k * 1024:(k + 1) * 1024], w_sb[:, k * 1024:(k + 1) * 1024],
                    modc[:, 48 + k:49 + k]),
                    waits=([("vec", v_bcols)] if k == 0 and w_sb is wq_sb else []),
                    inc="vec")
        for k in range(8):
            p.emit("vector", lambda e, k=k: e.tensor_scalar_mul(
                u1_sb[:, k * 256:(k + 1) * 256], u1_sb[:, k * 256:(k + 1) * 256],
                small[:, 48 + k:49 + k]), inc="vec")
        p.emit("vector", lambda e: e.memset(small[:, 32:33], EPS), inc="vec")
        v_prep = p.c("vec")
        t_prep = p.c("tmm")
        s_prep = p.c("sca")
        p.mark("prep")

        x_free = 0
        x2_free_out = 0
        fin_v_prev = 0
        for tile in range(n_tiles):
            p.dma("sync", x_t[:],
                  ap(x_h, tile * T * D, [[D, 128], [128 * D, NSUB], [1, D]]),
                  waits=([("vec", x_free)] if tile > 0 else []), inc="xin")
            xin_done = p.c("xin")

            for s in range(NSUB):
                p.emit("vector", lambda e, s=s: e.reduce_sum(
                    small[:, 16 + s:17 + s], x_t[:, s * 1024:(s + 1) * 1024],
                    axis=mybir.AxisListType.X),
                    waits=([("xin", xin_done), ("vec", v_prep)] if s == 0 else []),
                    inc="vec")
                p.emit("scalar", lambda e, s=s: e.activation(
                    qtmp[:, 0:512], x_t[:, s * 1024:s * 1024 + 512], AF.Square,
                    accum_out=small[:, 20 + s:21 + s]),
                    waits=(([("xin", xin_done)] +
                            ([("tmm", t_prep)] if tile == 0 else [])) if s == 0 else []),
                    inc="sca")
                p.emit("scalar", lambda e, s=s: e.activation(
                    qtmp[:, 512:1024], x_t[:, s * 1024 + 512:(s + 1) * 1024], AF.Square,
                    accum_out=small[:, 24 + s:25 + s]),
                    inc="sca")
            s_sq = p.c("sca")
            v_sums = p.c("vec")

            def emit_ln_chain(sum_off, sq1_off, sq2_off, two_sq):
                if two_sq:
                    p.emit("vector", lambda e: e.tensor_add(
                        small[:, sq1_off:sq1_off + 4], small[:, sq1_off:sq1_off + 4],
                        small[:, sq2_off:sq2_off + 4]),
                        waits=[("sca", p.c("sca")), ("vec", p.c("vec"))], inc="vec")
                p.drain("vector")
                p.emit("vector", lambda e: e.tensor_scalar_mul(
                    small[:, sum_off:sum_off + 4], small[:, sum_off:sum_off + 4],
                    1.0 / 1024), inc="vec")
                p.drain("vector")
                p.emit("vector", lambda e: e.tensor_mul(
                    small[:, 28:32], small[:, sum_off:sum_off + 4],
                    small[:, sum_off:sum_off + 4]), inc="vec")
                p.emit("vector", lambda e: e.tensor_scalar_mul(
                    small[:, sq1_off:sq1_off + 4], small[:, sq1_off:sq1_off + 4],
                    1.0 / 1024), inc="vec")
                p.drain("vector")
                p.emit("vector", lambda e: e.tensor_sub(
                    small[:, sq1_off:sq1_off + 4], small[:, sq1_off:sq1_off + 4],
                    small[:, 28:32]), inc="vec")
                v_var = p.c("vec")
                p.emit("scalar", lambda e: e.activation(
                    small[:, 28:32], small[:, sq1_off:sq1_off + 4], AF.Ln,
                    bias=small[:, 32:33]),
                    waits=[("vec", v_var)], inc="sca")
                p.drain("scalar")
                p.emit("scalar", lambda e: e.activation(
                    small[:, 0:4], small[:, 28:32], AF.Exp, scale=-0.5), inc="sca")
                s_r = p.c("sca")
                p.emit("vector", lambda e: e.tensor_mul(
                    small[:, 8:12], small[:, sum_off:sum_off + 4], small[:, 0:4]),
                    waits=[("sca", s_r)], inc="vec")
                p.drain("vector")
                p.emit("vector", lambda e: e.tensor_scalar_mul(
                    small[:, 8:12], small[:, 8:12], -1.0), inc="vec")
                return p.c("vec")

            v_nmr = emit_ln_chain(16, 20, 24, True)
            for s in range(NSUB):
                p.emit("scalar", lambda e, s=s: e.activation(
                    arena[:, s * 1024:(s + 1) * 1024], x_t[:, s * 1024:(s + 1) * 1024],
                    AF.Identity, bias=small[:, 8 + s:9 + s], scale=small[:, s:s + 1]),
                    waits=([("vec", v_nmr)] if s == 0 else []), inc="sca")
            s_xhat = p.c("sca")
            p.mark("ln1")

            def emit_transposes(src_sb, s_ready, extra_first):
                ev = []
                for s in range(NSUB):
                    b = (s % 2) * 2
                    for fc in range(8):
                        waits = []
                        if s == 0 and fc == 0:
                            waits = [("sca", s_ready)] + extra_first
                        if s >= 2 and fc == 0:
                            waits = list(ev[s - 2])
                        p.emit("tensor", lambda e, s=s, fc=fc, b=b, src_sb=src_sb: e.matmul(
                            bass.AP(PB[b + fc // 4], (fc % 4) * 128, [[512, 128], [1, 128]]),
                            src_sb[:, s * 1024 + fc * 128: s * 1024 + (fc + 1) * 128],
                            ident[:], start=True, stop=True),
                            waits=waits, inc="tmm")
                    t_tr = p.c("tmm")
                    if s % 2 == 0:
                        p.emit("vector", lambda e, s=s, b=b: e.tensor_copy(
                            ap(xhatT, s * 128, [[4096, 128], [512, 4], [1, 128]]),
                            ap(PB[b], 0, [[512, 128], [128, 4], [1, 128]])),
                            waits=[("tmm", t_tr)], inc="vec")
                        p.emit("vector", lambda e, s=s, b=b: e.tensor_copy(
                            ap(xhatT, 2048 + s * 128, [[4096, 128], [512, 4], [1, 128]]),
                            ap(PB[b + 1], 0, [[512, 128], [128, 4], [1, 128]])),
                            inc="vec")
                        ev.append([("vec", p.c("vec"))])
                    else:
                        p.emit("scalar", lambda e, s=s, b=b: e.copy(
                            ap(xhatT, s * 128, [[4096, 128], [512, 4], [1, 128]]),
                            ap(PB[b], 0, [[512, 128], [128, 4], [1, 128]])),
                            waits=[("tmm", t_tr)], inc="sca")
                        p.emit("scalar", lambda e, s=s, b=b: e.copy(
                            ap(xhatT, 2048 + s * 128, [[4096, 128], [512, 4], [1, 128]]),
                            ap(PB[b + 1], 0, [[512, 128], [128, 4], [1, 128]])),
                            inc="sca")
                        ev.append([("sca", p.c("sca"))])
                return p.c("vec"), p.c("sca")

            xT_v, xT_s = emit_transposes(arena, s_xhat,
                                         [("vec", fin_v_prev)] if tile > 0 else [])
            p.mark("xT")

            def proj_qk(w_sb, op, bcol_off, first_waits, stage):
                ev = []
                for cc in range(8):
                    bank = cc % 4
                    for kc in range(8):
                        waits = []
                        if cc == 0 and kc == 0:
                            waits = list(first_waits)
                        if cc >= 4 and kc == 0:
                            waits = list(ev[cc - 4])
                        p.emit("tensor", lambda e, cc=cc, kc=kc, bank=bank, w_sb=w_sb: e.matmul(
                            PB[bank][:, 0:512],
                            w_sb[:, kc * 1024 + cc * 128: kc * 1024 + (cc + 1) * 128],
                            xhatT[:, kc * 512:(kc + 1) * 512],
                            start=(kc == 0), stop=(kc == 7)),
                            waits=waits, inc="tmm")
                    t_p = p.c("tmm")
                    even_dst = ap(op, 2 * cc * 8, [[8192, 64], [128, 64], [1, 8]])
                    even_src = bass.AP(PB[bank], 0, [[512, 64], [8, 64], [1, 8]])
                    if cc % 2 == 0:
                        p.emit("vector", lambda e, d=even_dst, s_=even_src, cc=cc:
                               e.tensor_scalar_add(
                            d, s_, bcols[0:64, bcol_off + cc:bcol_off + cc + 1]),
                            waits=[("tmm", t_p)], inc="vec")
                        p.emit("vector", lambda e, cc=cc, bank=bank, stage=stage:
                               e.tensor_scalar_add(
                            stage[64:128, cc * 512:(cc + 1) * 512],
                            PB[bank][64:128, 0:512],
                            bcols[64:128, bcol_off + cc:bcol_off + cc + 1]),
                            inc="vec")
                        ev.append([("vec", p.c("vec"))])
                    else:
                        p.emit("scalar", lambda e, d=even_dst, s_=even_src, cc=cc:
                               e.activation(
                            d, s_, AF.Identity,
                            bias=bcols[0:64, bcol_off + cc:bcol_off + cc + 1]),
                            waits=[("tmm", t_p)], inc="sca")
                        p.emit("scalar", lambda e, cc=cc, bank=bank, stage=stage: e.activation(
                            stage[64:128, cc * 512:(cc + 1) * 512],
                            PB[bank][64:128, 0:512],
                            AF.Identity,
                            bias=bcols[64:128, bcol_off + cc:bcol_off + cc + 1]),
                            inc="sca")
                        ev.append([("sca", p.c("sca"))])
                return p.c("vec"), p.c("sca")

            vq, sq_ = proj_qk(wq_sb, qop, 0,
                              [("vec", xT_v), ("sca", xT_s)] + p.wall("shuf"), qtmp)
            for cc in range(8):
                p.dma("sync",
                      ap(qop, (2 * cc + 1) * 8, [[8192, 64], [128, 64], [1, 8]]),
                      ap(qtmp, 64 * 4096 + cc * 512, [[4096, 64], [8, 64], [1, 8]]),
                      waits=([("sca", sq_), ("vec", vq)] if cc == 0 else []), inc="shuf")
            q_odd = p.c("shuf")
            vk, sk_ = proj_qk(wk_sb, kop, 8, [], arena)
            for cc in range(8):
                p.dma("sync",
                      ap(kop, (2 * cc + 1) * 8, [[8192, 64], [128, 64], [1, 8]]),
                      ap(arena, 64 * 4096 + cc * 512, [[4096, 64], [8, 64], [1, 8]]),
                      waits=([("sca", sk_), ("vec", vk)] if cc == 0 else []), inc="shuf")
            k_odd = p.c("shuf")

            ev_v = []
            for cc in range(8):
                bank = cc % 4
                for kc in range(8):
                    waits = []
                    if cc == 0 and kc == 0:
                        waits = [("shuf", q_odd), ("vec", vk), ("sca", sk_)]
                    if cc >= 4 and kc == 0:
                        waits = list(ev_v[cc - 4])
                    p.emit("tensor", lambda e, cc=cc, kc=kc, bank=bank: e.matmul(
                        PB[bank][:, 0:512],
                        wv_sb[:, kc * 1024 + cc * 128: kc * 1024 + (cc + 1) * 128],
                        xhatT[:, kc * 512:(kc + 1) * 512],
                        start=(kc == 0), stop=(kc == 7)),
                        waits=waits, inc="tmm")
                t_v = p.c("tmm")
                if cc % 2 == 0:
                    p.emit("vector", lambda e, cc=cc, bank=bank: e.tensor_scalar_add(
                        qtmp[:, cc * 512:(cc + 1) * 512], PB[bank][:, 0:512],
                        bcols[:, 16 + cc:17 + cc]),
                        waits=[("tmm", t_v)], inc="vec")
                    ev_v.append([("vec", p.c("vec"))])
                else:
                    p.emit("scalar", lambda e, cc=cc, bank=bank: e.activation(
                        qtmp[:, cc * 512:(cc + 1) * 512], PB[bank][:, 0:512],
                        AF.Identity, bias=bcols[:, 16 + cc:17 + cc]),
                        waits=[("tmm", t_v)], inc="sca")
                    ev_v.append([("sca", p.c("sca"))])
            vfm_v, vfm_s = p.c("vec"), p.c("sca")
            p.mark("qkv")

            vint_w = [("vec", vfm_v), ("sca", vfm_s)]
            first_v = True
            first_d = True
            for j in range(16):
                src_band = (j % 2) * 64
                for half in range(2):
                    dst_band = half * 64
                    src_off = src_band * 4096 + (j // 2) * 512 + half * 256
                    dst_off = dst_band * 4096 + j * 8
                    if src_band == dst_band:
                        p.emit("vector", lambda e, src_off=src_off, dst_off=dst_off:
                               e.tensor_copy(
                            ap(vint, dst_off, [[4096, 64], [128, 32], [1, 8]]),
                            ap(qtmp, src_off, [[4096, 64], [8, 32], [1, 8]])),
                            waits=(vint_w if first_v else []), inc="vec")
                        first_v = False
                    else:
                        p.dma("sync",
                              ap(vint, dst_off, [[4096, 64], [128, 32], [1, 8]]),
                              ap(qtmp, src_off, [[4096, 64], [8, 32], [1, 8]]),
                              waits=(vint_w if first_d else []), inc="shuf")
                        first_d = False
            vint_v, vint_dma = p.c("vec"), p.c("shuf")

            ev_vt = []
            for gb in range(8):
                bank = 2 + gb % 2
                for gl in range(8):
                    g = gb * 8 + gl
                    half = g // 32
                    base = half * 64
                    waits = []
                    if gb == 0 and gl == 0:
                        waits = [("vec", vint_v), ("shuf", vint_dma)]
                    if gb >= 2 and gl == 0:
                        waits = list(ev_vt[gb - 2])
                    p.emit("tensor", lambda e, g=g, gl=gl, bank=bank, base=base: e.matmul(
                        bass.AP(PB[bank], gl * 64, [[512, 128], [1, 64]]),
                        vint[base:base + 64, (g % 32) * 128:(g % 32) * 128 + 128],
                        i64d[base:base + 64, 0:64],
                        start=True, stop=True),
                        waits=waits, inc="tmm")
                t_vt = p.c("tmm")
                if gb % 2 == 0:
                    p.emit("vector", lambda e, gb=gb, bank=bank: e.tensor_copy(
                        ap(vop, gb * 8 * 65, [[4160, 128], [65, 8], [1, 64]]),
                        ap(PB[bank], 0, [[512, 128], [64, 8], [1, 64]])),
                        waits=[("tmm", t_vt)], inc="vec")
                    ev_vt.append([("vec", p.c("vec"))])
                else:
                    p.emit("scalar", lambda e, gb=gb, bank=bank: e.copy(
                        ap(vop, gb * 8 * 65, [[4160, 128], [65, 8], [1, 64]]),
                        ap(PB[bank], 0, [[512, 128], [64, 8], [1, 64]])),
                        waits=[("tmm", t_vt)], inc="sca")
                    ev_vt.append([("sca", p.c("sca"))])
            vop_v, vop_s = p.c("vec"), p.c("sca")
            p.mark("vop")

            att_w = [("shuf", q_odd), ("shuf", k_odd), ("vec", vq), ("vec", vk),
                     ("sca", sq_), ("sca", sk_), ("vec", vop_v), ("sca", vop_s)]
            exp_done = []
            attnv_done = []
            norm_done = []
            NB4 = NG // 4

            def emit_score(b):
                sbank = b % 2
                waits = (list(att_w) if b == 0 else [])
                if b >= 2:
                    waits = waits + [("sca", exp_done[b - 2])]
                for gl in range(4):
                    g = 4 * b + gl
                    p.emit("tensor", lambda e, g=g, gl=gl, sbank=sbank: e.matmul(
                        bass.AP(PB[sbank], gl * 128, [[512, 128], [1, 128]]),
                        kop[0:73, g * 128:(g + 1) * 128],
                        qop[0:73, g * 128:(g + 1) * 128],
                        start=True, stop=True),
                        waits=(waits if gl == 0 else []), inc="tmm")
                t_sc = p.c("tmm")
                ew = [("tmm", t_sc)]
                if b >= 2:
                    ew.append(("tmm", attnv_done[b - 2]))
                p.emit("scalar", lambda e, b=b, sbank=sbank: e.activation(
                    eop[:, (b % 2) * 512:(b % 2) * 512 + 512], PB[sbank][:, 0:512],
                    AF.Exp, scale=0.125),
                    waits=ew, inc="sca")
                exp_done.append(p.c("sca"))

            def emit_attnv(b):
                obank = 4 + b % 2
                aw = [("sca", exp_done[b])]
                if b >= 2:
                    aw.append(("vec", norm_done[b - 2]))
                for gl in range(4):
                    g = 4 * b + gl
                    p.emit("tensor", lambda e, g=g, gl=gl, b=b, obank=obank: e.matmul(
                        bass.AP(PB[obank], gl * 66, [[512, 128], [1, 65]]),
                        eop[:, (b % 2) * 512 + gl * 128:(b % 2) * 512 + (gl + 1) * 128],
                        vop[:, g * 65:(g + 1) * 65],
                        start=True, stop=True),
                        waits=(aw if gl == 0 else []), inc="tmm")
                attnv_done.append(p.c("tmm"))
                p.emit("vector", lambda e, b=b, obank=obank: e.tensor_copy(
                    den[:, (b % 2) * 8:(b % 2) * 8 + 4],
                    ap(PB[obank], 64, [[512, 128], [66, 4]])),
                    waits=[("tmm", attnv_done[b])], inc="vec")
                p.drain("vector")
                p.emit("vector", lambda e, b=b: e.reciprocal(
                    den[:, (b % 2) * 8 + 4:(b % 2) * 8 + 8],
                    den[:, (b % 2) * 8:(b % 2) * 8 + 4]), inc="vec")
                p.drain("vector")
                p.emit("vector", lambda e, b=b, obank=obank: e.tensor_tensor(
                    ap(qtmp, b * 256, [[4096, 128], [64, 4], [1, 64]]),
                    ap(PB[obank], 0, [[512, 128], [66, 4], [1, 64]]),
                    ap(den, (b % 2) * 8 + 4, [[16, 128], [1, 4], [0, 64]]),
                    op=ALU.mult), inc="vec")
                norm_done.append(p.c("vec"))

            emit_score(0)
            emit_score(1)
            for b in range(NB4):
                emit_attnv(b)
                if b + 2 < NB4:
                    emit_score(b + 2)
            p.mark("attcore")

            ofm_v = 0
            ofm_s = 0
            oq_dma = p.c("shuf")
            for b in range(NG // 4):
                bank = 6 + b % 2
                eng = "vector" if b % 2 == 0 else "scalar"
                bw = [("vec", norm_done[b])]
                if b >= 2:
                    bw += [(("vec", ofm_v) if b % 2 == 0 else ("sca", ofm_s))]
                for gl in range(4):
                    g = 4 * b + gl
                    p.emit("tensor", lambda e, g=g, gl=gl, bank=bank: e.matmul(
                        bass.AP(PB[bank], gl * 128, [[512, 64], [1, 128]]),
                        qtmp[:, g * 64:(g + 1) * 64],
                        ident[:], start=True, stop=True),
                        waits=(bw if gl == 0 else []), inc="tmm")
                t_oi = p.c("tmm")
                sw = [("tmm", t_oi)]
                if b % 4 in (0, 1) and b > 1:
                    sw.append(("shuf", oq_dma))
                for i in range(16):
                    if i % 2 == 0:
                        dst = ap(arena, (i // 2) * 512 + b * 32,
                                 [[4096, 64], [8, 4], [1, 8]])
                    else:
                        dst = ap(oddscr, (i // 2) * 128 + (b % 4) * 32,
                                 [[1024, 64], [8, 4], [1, 8]])
                    srcp = ap(PB[bank], i * 8, [[512, 64], [128, 4], [1, 8]])
                    if eng == "vector":
                        p.emit("vector", lambda e, d=dst, s_=srcp: e.tensor_copy(d, s_),
                               waits=(sw if i == 0 else []), inc="vec")
                    else:
                        p.emit("scalar", lambda e, d=dst, s_=srcp: e.copy(d, s_),
                               waits=(sw if i == 0 else []), inc="sca")
                if eng == "vector":
                    ofm_v = p.c("vec")
                else:
                    ofm_s = p.c("sca")
                if b % 4 == 3:
                    qb = b // 4
                    p.dma("sync",
                          ap(arena, 64 * 4096 + qb * 128, [[4096, 64], [512, 8], [1, 128]]),
                          ap(oddscr, 0, [[1024, 64], [128, 8], [1, 128]]),
                          waits=[("sca", p.c("sca")), ("vec", p.c("vec"))], inc="shuf")
                    oq_dma = p.c("shuf")
            ofm_dma = p.c("shuf")
            p.mark("attn")

            for cc in range(8):
                p.emit("vector", lambda e, cc=cc: e.tensor_scalar_mul(
                    eop[:, cc * 128:(cc + 1) * 128], ident[:], modc[:, 16 + cc:17 + cc]),
                    waits=([("tmm", attnv_done[-1])] if cc == 0 else []), inc="vec")
            dga_v = p.c("vec")
            res_v = 0
            for s in range(NSUB):
                b2k = (s % 2) * 2
                for cc in range(8):
                    waits = []
                    if s == 0 and cc == 0:
                        waits = [("vec", ofm_v), ("sca", ofm_s), ("shuf", ofm_dma),
                                 ("vec", dga_v)]
                    if s >= 2 and cc == 0:
                        waits = [("vec", res_v)]
                    p.emit("tensor", lambda e, s=s, cc=cc, b2k=b2k: e.matmul(
                        bass.AP(PB[b2k + cc // 4], (cc % 4) * 128, [[512, 128], [1, 128]]),
                        arena[:, cc * 512 + s * 128: cc * 512 + s * 128 + 128],
                        eop[:, cc * 128:(cc + 1) * 128],
                        start=True, stop=True),
                        waits=waits, inc="tmm")
                t_ga = p.c("tmm")
                p.emit("vector", lambda e, s=s, b2k=b2k: e.scalar_tensor_tensor(
                    x2_t[:, s * 1024:s * 1024 + 512], x_t[:, s * 1024:s * 1024 + 512],
                    1.0, PB[b2k][:, 0:512], ALU.mult, ALU.add,
                    accum_out=small[:, 16 + s:17 + s]),
                    waits=[("tmm", t_ga)] +
                          ([("xout", x2_free_out)] if s == 0 and tile > 0 else []),
                    inc="vec")
                p.emit("vector", lambda e, s=s, b2k=b2k: e.scalar_tensor_tensor(
                    x2_t[:, s * 1024 + 512:(s + 1) * 1024],
                    x_t[:, s * 1024 + 512:(s + 1) * 1024],
                    1.0, PB[b2k + 1][:, 0:512], ALU.mult, ALU.add,
                    accum_out=small[:, 20 + s:21 + s]),
                    inc="vec")
                res_v = p.c("vec")
            x_free = res_v
            p.mark("res1")

            for s in range(NSUB):
                p.emit("scalar", lambda e, s=s: e.activation(
                    pb6[:, 0:512], x2_t[:, s * 1024:s * 1024 + 512], AF.Square,
                    accum_out=small[:, 24 + s:25 + s]),
                    waits=([("vec", res_v)] if s == 0 else []), inc="sca")
                p.emit("scalar", lambda e, s=s: e.activation(
                    pb7[:, 0:512], x2_t[:, s * 1024 + 512:(s + 1) * 1024], AF.Square,
                    accum_out=small[:, 28 + s:29 + s]),
                    inc="sca")
            p.emit("vector", lambda e: e.tensor_add(small[:, 16:20], small[:, 16:20],
                                                    small[:, 20:24]),
                   waits=[("vec", res_v)], inc="vec")
            v_nmr2 = emit_ln_chain(16, 24, 28, True)
            for s in range(NSUB):
                p.emit("scalar", lambda e, s=s: e.activation(
                    arena[:, s * 1024:(s + 1) * 1024], x2_t[:, s * 1024:(s + 1) * 1024],
                    AF.Identity, bias=small[:, 8 + s:9 + s], scale=small[:, s:s + 1]),
                    waits=([("vec", v_nmr2)] if s == 0 else []), inc="sca")
            s_xhat2 = p.c("sca")

            xT2_v, xT2_s = emit_transposes(arena, s_xhat2, [])

            for c2 in range(2):
                bank = 4 + c2
                for kc in range(8):
                    p.emit("tensor", lambda e, c2=c2, kc=kc, bank=bank: e.matmul(
                        PB[bank][:, 0:512],
                        u1_sb[:, kc * 256 + c2 * 128: kc * 256 + (c2 + 1) * 128],
                        xhatT[:, kc * 512:(kc + 1) * 512],
                        start=(kc == 0), stop=(kc == 7)),
                        waits=([("vec", xT2_v), ("sca", xT2_s)]
                               if (c2 == 0 and kc == 0) else []),
                        inc="tmm")
                t_pt = p.c("tmm")
                p.emit("vector", lambda e, c2=c2, bank=bank: e.tensor_scalar_add(
                    ptsb[:, c2 * 512:(c2 + 1) * 512], PB[bank][:, 0:512],
                    bcols[:, 24 + c2:25 + c2]),
                    waits=[("tmm", t_pt)], inc="vec")
            pt_v = p.c("vec")
            p.mark("pt")

            gelu_done = []

            def emit_h(fc):
                bank = fc % 4
                for rc in range(2):
                    waits = []
                    if fc == 0 and rc == 0:
                        waits = [("vec", pt_v)]
                    if fc >= 4 and rc == 0:
                        waits = [("sca", gelu_done[fc - 4])]
                    p.emit("tensor", lambda e, fc=fc, rc=rc, bank=bank: e.matmul(
                        PB[bank][:, 0:512],
                        v1_sb[:, rc * 4096 + fc * 128: rc * 4096 + (fc + 1) * 128],
                        ptsb[:, rc * 512:(rc + 1) * 512],
                        start=(rc == 0), stop=(rc == 1)),
                        waits=waits, inc="tmm")
                t_h = p.c("tmm")
                p.emit("scalar", lambda e, fc=fc, bank=bank: e.activation(
                    htring[:, (fc % 2) * 512:(fc % 2) * 512 + 512], PB[bank][:, 0:512],
                    AF.Gelu_apprx_tanh, bias=bcols[:, 32 + fc:33 + fc]),
                    waits=[("tmm", t_h)], inc="sca")
                gelu_done.append(p.c("sca"))

            def emit_u2(fc):
                for c2 in range(2):
                    p.emit("tensor", lambda e, fc=fc, c2=c2: e.matmul(
                        PB[6 + c2][:, 0:512],
                        u2_sb[:, fc * 256 + c2 * 128: fc * 256 + (c2 + 1) * 128],
                        htring[:, (fc % 2) * 512:(fc % 2) * 512 + 512],
                        start=(fc == 0), stop=(fc == 31), skip_group_check=True),
                        waits=([("sca", gelu_done[fc])] if c2 == 0 else []), inc="tmm")

            emit_h(0)
            emit_h(1)
            for fc in range(32):
                emit_u2(fc)
                if fc + 2 < 32:
                    emit_h(fc + 2)
            t_q2 = p.c("tmm")
            for c2 in range(2):
                p.emit("vector", lambda e, c2=c2: e.tensor_copy(
                    q2sb[:, c2 * 512:(c2 + 1) * 512], PB[6 + c2][:, 0:512]),
                    waits=[("tmm", t_q2)], inc="vec")
            q2_v = p.c("vec")
            p.mark("q2")

            fprime_s = []
            for cc in range(8):
                bank = 4 + cc % 2
                for rc in range(2):
                    waits = []
                    if cc == 0 and rc == 0:
                        waits = [("vec", q2_v)]
                    if cc >= 2 and rc == 0:
                        waits = [("sca", fprime_s[cc - 2])]
                    p.emit("tensor", lambda e, cc=cc, rc=rc, bank=bank: e.matmul(
                        PB[bank][:, 0:512],
                        v2_sb[:, rc * 1024 + cc * 128: rc * 1024 + (cc + 1) * 128],
                        q2sb[:, rc * 512:(rc + 1) * 512],
                        start=(rc == 0), stop=(rc == 1)),
                        waits=waits, inc="tmm")
                t_f = p.c("tmm")
                p.emit("scalar", lambda e, cc=cc, bank=bank: e.activation(
                    vint[:, cc * 512:(cc + 1) * 512], PB[bank][:, 0:512],
                    AF.Identity, bias=bcols[:, 64 + cc:65 + cc]),
                    waits=[("tmm", t_f)], inc="sca")
                fprime_s.append(p.c("sca"))
            fp_s = p.c("sca")
            p.mark("fp")

            for cc in range(8):
                p.emit("vector", lambda e, cc=cc: e.tensor_scalar_mul(
                    eop[:, cc * 128:(cc + 1) * 128], ident[:], modc[:, 40 + cc:41 + cc]),
                    waits=([("tmm", p.c("tmm"))] if cc == 0 else []), inc="vec")
            dgb_v = p.c("vec")
            fin_v = 0
            for s in range(NSUB):
                b2k = (s % 2) * 2
                for cc in range(8):
                    waits = []
                    if s == 0 and cc == 0:
                        waits = [("sca", fp_s), ("vec", dgb_v)]
                    if s >= 2 and cc == 0:
                        waits = [("vec", fin_v)]
                    p.emit("tensor", lambda e, s=s, cc=cc, b2k=b2k: e.matmul(
                        bass.AP(PB[b2k + cc // 4], (cc % 4) * 128, [[512, 128], [1, 128]]),
                        vint[:, cc * 512 + s * 128: cc * 512 + s * 128 + 128],
                        eop[:, cc * 128:(cc + 1) * 128],
                        start=True, stop=True),
                        waits=waits, inc="tmm")
                t_gb = p.c("tmm")
                p.emit("vector", lambda e, s=s, b2k=b2k: e.tensor_add(
                    x2_t[:, s * 1024:s * 1024 + 512], x2_t[:, s * 1024:s * 1024 + 512],
                    PB[b2k][:, 0:512]),
                    waits=[("tmm", t_gb)], inc="vec")
                p.emit("vector", lambda e, s=s, b2k=b2k: e.tensor_add(
                    x2_t[:, s * 1024 + 512:(s + 1) * 1024],
                    x2_t[:, s * 1024 + 512:(s + 1) * 1024],
                    PB[b2k + 1][:, 0:512]),
                    inc="vec")
                fin_v = p.c("vec")
                p.dma("sync",
                      ap(y_h, tile * T * D + s * 128 * D, [[D, 128], [1, D]]),
                      x2_t[:, s * 1024:(s + 1) * 1024],
                      waits=[("vec", fin_v)], inc="xout")
            fin_v_prev = fin_v
            x2_free_out = p.c("xout")

        if phase is not None:
            print("PLAN: truncating at", phase, "=", p.marks[phase], "of", p.seq)
            p.truncate(p.marks[phase])
        if dbg:
            last = p.wall("vec", "sca", "tmm", "xout", "shuf")
            dump_map = {
                "d_qop": qop[:], "d_kop": kop[:], "d_vop": vop[:], "d_eop": eop[:],
                "d_otmp": qtmp[:], "d_ofm": arena[:], "d_xhatT": xhatT[:],
                "d_x2": x2_t[:], "d_modc": modc[:], "d_bcols": bcols[:],
                "d_vint": vint[:], "d_ptsb": ptsb[:], "d_q2sb": q2sb[:],
                "d_small": small[:], "d_scolb": scolb[:],
            }
            first = True
            for name in dbg:
                p.dma("gpsimd", dbg[name], dump_map[name],
                      waits=(last if first else []), inc="dma")
                first = False

        p.emit("sync", lambda e: e.nop(),
               waits=p.wall("xout", "dma", "shuf", "cast"))

        with nc.Block() as block:
            p.realize(block, sems)

    return nc


def host_consts():
    ident = np.eye(128, dtype=BF)
    i64d = np.zeros((128, 64), dtype=BF)
    i64d[0:64] = np.eye(64, dtype=BF)
    i64d[64:128] = np.eye(64, dtype=BF)
    base = np.zeros((9, 128), dtype=np.float32)
    for r in range(8):
        for t in range(8):
            for h in range(16):
                base[r, h * 8 + t] = 32.0 if t == r else 0.0
    maskk = np.tile(base, (1, 64)).astype(BF)
    maskq = np.tile(base, (1, 64)).astype(BF)
    maskk[8, :] = np.float32(-32.0)
    maskq[8, :] = np.float32(32.0)
    return ident, i64d, maskq, maskk


_NC_CACHE = {}


def kernel(L0_attn_x, L0_attn_Wq, L0_attn_bq, L0_attn_Wk, L0_attn_bk, L0_attn_Wv,
           L0_attn_bv, L0_ffn_U1, L0_ffn_V1, L0_ffn_b1, L0_ffn_U2, L0_ffn_V2,
           L0_ffn_b2, L0_c, L0_wmod):
    x = np.ascontiguousarray(np.asarray(L0_attn_x, dtype=np.float32))
    B, S, Dm = x.shape
    half = S // 2
    ident, i64d, maskq, maskk = host_consts()
    shared = {
        "wq": np.ascontiguousarray(L0_attn_Wq, dtype=np.float32),
        "wk": np.ascontiguousarray(L0_attn_Wk, dtype=np.float32),
        "wv": np.ascontiguousarray(L0_attn_Wv, dtype=np.float32),
        "bq": np.asarray(L0_attn_bq, dtype=np.float32).reshape(1, Dm),
        "bk": np.asarray(L0_attn_bk, dtype=np.float32).reshape(1, Dm),
        "bv": np.asarray(L0_attn_bv, dtype=np.float32).reshape(1, Dm),
        "u1": np.ascontiguousarray(L0_ffn_U1, dtype=np.float32),
        "v1": np.ascontiguousarray(L0_ffn_V1, dtype=np.float32),
        "b1": np.asarray(L0_ffn_b1, dtype=np.float32).reshape(1, DFF),
        "u2": np.ascontiguousarray(L0_ffn_U2, dtype=np.float32),
        "v2": np.ascontiguousarray(L0_ffn_V2, dtype=np.float32),
        "b2": np.asarray(L0_ffn_b2, dtype=np.float32).reshape(1, Dm),
        "wmod": np.ascontiguousarray(L0_wmod, dtype=np.float32),
        "ident": ident, "i64d": i64d, "maskq": maskq, "maskk": maskk,
    }
    cvec = np.asarray(L0_c, dtype=np.float32)
    n_tiles = (half + T - 1) // T
    if n_tiles not in _NC_CACHE:
        _NC_CACHE[n_tiles] = build_core_program(n_tiles)
    nc = _NC_CACHE[n_tiles]
    in_maps = []
    for core in range(8):
        b = core // 2
        h0 = (core % 2) * half
        m = dict(shared)
        m["x"] = np.ascontiguousarray(x[b, h0:h0 + half])
        m["cvec"] = np.ascontiguousarray(cvec[b:b + 1])
        in_maps.append(m)
    res = run_bass_kernel_spmd(nc, in_maps, core_ids=list(range(8)))
    out = np.empty((B, S, Dm), dtype=np.float32)
    for core in range(8):
        b = core // 2
        h0 = (core % 2) * half
        out[b, h0:h0 + half] = res.results[core]["y"]
    return out


# version 03
# speedup vs baseline: 1.0111x; 1.0111x over previous
import numpy as np
import ml_dtypes
import concourse.bass as bass
import concourse.mybir as mybir
from concourse.bass_utils import run_bass_kernel_spmd

F32 = mybir.dt.float32
BF16 = mybir.dt.bfloat16
AF = mybir.ActivationFunctionType
ALU = mybir.AluOpType
BF = ml_dtypes.bfloat16

D = 1024
NH = 16
DK = 64
DFF = 4096
RANK = 256
T = 512
NSUB = T // 128
NG = T // 8
EPS = 1e-5

ENGS = ("sync", "gpsimd", "tensor", "vector", "scalar")


class Prog:
    def __init__(self):
        self.ops = {e: [] for e in ENGS}
        self.ctr = {}
        self.seq = 0
        self.marks = {}

    def mark(self, name):
        self.marks[name] = self.seq

    def c(self, sem):
        return self.ctr.get(sem, 0)

    def truncate(self, limit_seq):
        self.ctr = {}
        for e in ENGS:
            kept = [op for op in self.ops[e] if op[3] <= limit_seq]
            self.ops[e] = kept
            for fn, waits, tok, sq in kept:
                if tok is not None:
                    self.ctr[tok[0]] = self.ctr.get(tok[0], 0) + tok[1]

    def emit(self, eng, fn, waits=(), inc=None, inc_k=1):
        tok = None
        if inc is not None:
            self.ctr[inc] = self.ctr.get(inc, 0) + inc_k
            tok = (inc, inc_k)
        self.seq += 1
        self.ops[eng].append((fn, [w for w in waits if w[1] > 0], tok, self.seq))

    def dma(self, eng, out_ap, in_ap, waits=(), inc="dma", slow=False):
        self.emit(eng, lambda e, o=out_ap, i=in_ap, sl=slow: e.dma_start(
            o, i, allow_slow_non_contiguous=sl),
            waits=waits, inc=inc, inc_k=16)

    def drain(self, eng):
        self.ops[eng].append((lambda e: e.drain(), [], None, self.seq))

    def wall(self, *sems):
        return [(s, self.c(s)) for s in sems]

    def realize(self, block, sems):
        for eng_name in ENGS:
            seq = self.ops[eng_name]
            if not seq:
                continue
            def mk(seq=seq):
                def body(eng):
                    for fn, waits, tok, _sq in seq:
                        for s, v in waits:
                            eng.wait_ge(sems[s], v)
                        instr = fn(eng)
                        if tok is not None:
                            instr.then_inc(sems[tok[0]], tok[1])
                return body
            getattr(block, eng_name)(mk())


def ap(t, offset, dims):
    if isinstance(t, bass.AP):
        t = t.tensor
    return bass.AP(t, offset, [list(d) for d in dims])


def build_core_program(n_tiles, debug=(), phase=None, skip=()):
    nc = bass.Bass("TRN2", target_bir_lowering=False)
    NTOK = n_tiles * T

    x_h = nc.dram_tensor("x", [NTOK, D], F32, kind="ExternalInput").ap()
    y_h = nc.dram_tensor("y", [NTOK, D], F32, kind="ExternalOutput").ap()
    wq_h = nc.dram_tensor("wq", [D, D], F32, kind="ExternalInput").ap()
    wk_h = nc.dram_tensor("wk", [D, D], F32, kind="ExternalInput").ap()
    wv_h = nc.dram_tensor("wv", [D, D], F32, kind="ExternalInput").ap()
    bq_h = nc.dram_tensor("bq", [1, D], F32, kind="ExternalInput").ap()
    bk_h = nc.dram_tensor("bk", [1, D], F32, kind="ExternalInput").ap()
    bv_h = nc.dram_tensor("bv", [1, D], F32, kind="ExternalInput").ap()
    u1_h = nc.dram_tensor("u1", [D, RANK], F32, kind="ExternalInput").ap()
    v1_h = nc.dram_tensor("v1", [RANK, DFF], F32, kind="ExternalInput").ap()
    b1_h = nc.dram_tensor("b1", [1, DFF], F32, kind="ExternalInput").ap()
    u2_h = nc.dram_tensor("u2", [DFF, RANK], F32, kind="ExternalInput").ap()
    v2_h = nc.dram_tensor("v2", [RANK, D], F32, kind="ExternalInput").ap()
    b2_h = nc.dram_tensor("b2", [1, D], F32, kind="ExternalInput").ap()
    c_h = nc.dram_tensor("cvec", [1, D], F32, kind="ExternalInput").ap()
    wmod_h = nc.dram_tensor("wmod", [D, 6 * D], F32, kind="ExternalInput").ap()
    ident_h = nc.dram_tensor("ident", [128, 128], BF16, kind="ExternalInput").ap()
    i64d_h = nc.dram_tensor("i64d", [128, 64], BF16, kind="ExternalInput").ap()
    maskq_h = nc.dram_tensor("maskq", [9, 8192], BF16, kind="ExternalInput").ap()
    maskk_h = nc.dram_tensor("maskk", [9, 8192], BF16, kind="ExternalInput").ap()
    dbg = {}
    for name, shape, dt in debug:
        dbg[name] = nc.dram_tensor(name, list(shape), dt, kind="ExternalOutput").ap()

    p = Prog()
    SEMS = ["dma", "cast", "tmm", "vec", "sca", "xin", "xout", "shuf"]

    from contextlib import ExitStack
    with ExitStack() as ctx:
        wq_sb = ctx.enter_context(nc.sbuf_tensor("wq_sb", [128, 8192], BF16))
        wk_sb = ctx.enter_context(nc.sbuf_tensor("wk_sb", [128, 8192], BF16))
        wv_sb = ctx.enter_context(nc.sbuf_tensor("wv_sb", [128, 8192], BF16))
        u1_sb = ctx.enter_context(nc.sbuf_tensor("u1_sb", [128, 2048], BF16))
        v1_sb = ctx.enter_context(nc.sbuf_tensor("v1_sb", [128, 8192], BF16))
        u2_sb = ctx.enter_context(nc.sbuf_tensor("u2_sb", [128, 8192], BF16))
        v2_sb = ctx.enter_context(nc.sbuf_tensor("v2_sb", [128, 2048], BF16))
        ident = ctx.enter_context(nc.sbuf_tensor("ident_sb", [128, 128], BF16))
        i64d = ctx.enter_context(nc.sbuf_tensor("i64d_sb", [128, 64], BF16))
        qop = ctx.enter_context(nc.sbuf_tensor("qop", [73, 8192], BF16))
        kop = ctx.enter_context(nc.sbuf_tensor("kop", [73, 8192], BF16))
        vop = ctx.enter_context(nc.sbuf_tensor("vop", [128, 4160], BF16))
        bcols = ctx.enter_context(nc.sbuf_tensor("bcols", [128, 80], F32))
        modc = ctx.enter_context(nc.sbuf_tensor("modc", [128, 56], F32))
        rowbuf = ctx.enter_context(nc.sbuf_tensor("rowbuf", [1, 512], F32))
        rowbf = ctx.enter_context(nc.sbuf_tensor("rowbf", [1, 1024], BF16))
        scolb = ctx.enter_context(nc.sbuf_tensor("scolb", [128, 24], BF16))
        onescal = ctx.enter_context(nc.sbuf_tensor("onescal", [1, 8], BF16))
        small = ctx.enter_context(nc.sbuf_tensor("small", [128, 64], F32))
        x_t = ctx.enter_context(nc.sbuf_tensor("x_t", [128, 4096], F32))
        x2_t = ctx.enter_context(nc.sbuf_tensor("x2_t", [128, 4096], F32))
        arena = ctx.enter_context(nc.sbuf_tensor("arena", [128, 4096], BF16))
        xhatT = ctx.enter_context(nc.sbuf_tensor("xhatT", [128, 4096], BF16))
        qtmp = ctx.enter_context(nc.sbuf_tensor("qtmp", [128, 4096], BF16))
        oddscr = ctx.enter_context(nc.sbuf_tensor("oddscr", [64, 1024], BF16))
        vint = ctx.enter_context(nc.sbuf_tensor("vint", [128, 4096], BF16))
        eop = ctx.enter_context(nc.sbuf_tensor("eop", [128, 1024], BF16))
        den = ctx.enter_context(nc.sbuf_tensor("den", [128, 16], F32))
        ptsb = ctx.enter_context(nc.sbuf_tensor("ptsb", [128, 1024], BF16))
        htring = ctx.enter_context(nc.sbuf_tensor("htring", [128, 1024], BF16))
        q2sb = ctx.enter_context(nc.sbuf_tensor("q2sb", [128, 1024], BF16))
        pb0 = ctx.enter_context(nc.psum_tensor("pb0", [128, 512], F32))
        pb1 = ctx.enter_context(nc.psum_tensor("pb1", [128, 512], F32))
        pb2 = ctx.enter_context(nc.psum_tensor("pb2", [128, 512], F32))
        pb3 = ctx.enter_context(nc.psum_tensor("pb3", [128, 512], F32))
        pb4 = ctx.enter_context(nc.psum_tensor("pb4", [128, 512], F32))
        pb5 = ctx.enter_context(nc.psum_tensor("pb5", [128, 512], F32))
        pb6 = ctx.enter_context(nc.psum_tensor("pb6", [128, 512], F32))
        pb7 = ctx.enter_context(nc.psum_tensor("pb7", [128, 512], F32))
        s_dma = ctx.enter_context(nc.semaphore("s_dma"))
        s_cast = ctx.enter_context(nc.semaphore("s_cast"))
        s_tmm = ctx.enter_context(nc.semaphore("s_tmm"))
        s_vec = ctx.enter_context(nc.semaphore("s_vec"))
        s_sca = ctx.enter_context(nc.semaphore("s_sca"))
        s_xin = ctx.enter_context(nc.semaphore("s_xin"))
        s_xout = ctx.enter_context(nc.semaphore("s_xout"))
        s_shuf = ctx.enter_context(nc.semaphore("s_shuf"))
        sems = {"dma": s_dma, "cast": s_cast, "tmm": s_tmm, "vec": s_vec,
                "sca": s_sca, "xin": s_xin, "xout": s_xout, "shuf": s_shuf}
        PB = [pb0, pb1, pb2, pb3, pb4, pb5, pb6, pb7]

        p.dma("sync", ident[:], ident_h)
        p.dma("sync", i64d[:], i64d_h)
        p.dma("sync", qop[64:73, :], maskq_h)
        p.dma("sync", kop[64:73, :], maskk_h)
        p.dma("sync", rowbuf[0:1, 0:512], ap(c_h, 0, [[1, 1], [1, 512]]))
        p.dma("sync", ap(bcols, 32, [[80, 128], [1, 32]]),
              ap(b1_h, 0, [[1, 128], [128, 32]]), slow=True)
        p.dma("sync", ap(bcols, 64, [[80, 128], [1, 8]]),
              ap(b2_h, 0, [[1, 128], [128, 8]]), slow=True)
        p.dma("sync", ap(small, 0, [[64, 128], [1, 8]]),
              ap(bq_h, 0, [[1, 128], [128, 8]]), slow=True)
        p.dma("sync", ap(small, 8, [[64, 128], [1, 8]]),
              ap(bk_h, 0, [[1, 128], [128, 8]]), slow=True)
        p.dma("sync", ap(small, 16, [[64, 128], [1, 8]]),
              ap(bv_h, 0, [[1, 128], [128, 8]]), slow=True)
        consts_done = p.c("dma")

        for w_sb, w_hh in ((wq_sb, wq_h), (wk_sb, wk_h), (wv_sb, wv_h)):
            p.dma("gpsimd", w_sb[:],
                  ap(w_hh, 0, [[1024, 128], [128 * 1024, 8], [1, 1024]]), inc="cast")
        p.dma("gpsimd", u1_sb[:],
              ap(u1_h, 0, [[256, 128], [128 * 256, 8], [1, 256]]), inc="cast")
        p.dma("gpsimd", v1_sb[:],
              ap(v1_h, 0, [[4096, 128], [128 * 4096, 2], [1, 4096]]), inc="cast")
        p.dma("gpsimd", u2_sb[:],
              ap(u2_h, 0, [[256, 128], [128 * 256, 32], [1, 256]]), inc="cast")
        p.dma("gpsimd", v2_sb[:],
              ap(v2_h, 0, [[1024, 128], [128 * 1024, 2], [1, 1024]]), inc="cast")
        w_done = p.c("cast")

        p.emit("vector", lambda e: e.memset(onescal[:], 1.0), inc="vec")
        p.emit("vector", lambda e: e.memset(
            ap(vop, 64, [[4160, 128], [65, 64]]), 1.0), inc="vec")
        v_ones = p.c("vec")

        p.emit("scalar", lambda e: e.activation(rowbf[0:1, 0:512], rowbuf[0:1, 0:512],
                                                AF.Silu),
               waits=[("dma", consts_done)], inc="sca")
        s_h1 = p.c("sca")
        p.dma("sync", rowbuf[0:1, 0:512], ap(c_h, 512, [[1, 1], [1, 512]]),
              waits=[("sca", s_h1)])
        c2_done = p.c("dma")
        p.emit("scalar", lambda e: e.activation(rowbf[0:1, 512:1024], rowbuf[0:1, 0:512],
                                                AF.Silu),
               waits=[("dma", c2_done)], inc="sca")
        s_silu = p.c("sca")
        for k in range(8):
            p.emit("tensor", lambda e, k=k: e.matmul(
                pb6[:, k:k + 1], rowbf[0:1, k * 128:(k + 1) * 128], onescal[0:1, 0:1],
                start=True, stop=True),
                waits=(p.wall("sca", "vec") if k == 0 else []), inc="tmm")
        t_scol = p.c("tmm")
        p.emit("vector", lambda e: e.tensor_copy(scolb[:, 0:8], pb6[:, 0:8]),
               waits=[("tmm", t_scol)], inc="vec")
        v_scolb = p.c("vec")

        for n in range(12):
            ring = vint if n % 2 == 0 else qtmp
            p.dma("gpsimd", ap(ring, 0, [[4096, 128], [512, 8], [1, 512]]),
                  ap(wmod_h, n * 512, [[6144, 128], [128 * 6144, 8], [1, 512]]),
                  waits=(p.wall("tmm") if n >= 2 else []),
                  inc="cast")
            cd = p.c("cast")
            for k in range(8):
                p.emit("tensor", lambda e, k=k, ring=ring: e.matmul(
                    pb7[0:1, 0:512], scolb[:, k:k + 1],
                    ring[:, k * 512:(k + 1) * 512],
                    start=(k == 0), stop=(k == 7)),
                    waits=(([("cast", cd), ("vec", v_scolb)] + p.wall("sca"))
                           if k == 0 else []),
                    inc="tmm")
            t_mod = p.c("tmm")
            p.emit("scalar", lambda e: e.copy(rowbuf[0:1, 0:512], pb7[0:1, 0:512]),
                   waits=[("tmm", t_mod)], inc="sca")
            p.drain("scalar")
            p.emit("scalar", lambda e: e.copy(rowbf[0:1, 0:512], rowbuf[0:1, 0:512]),
                   inc="sca")
            s_rb = p.c("sca")
            for j in range(4):
                col = n * 4 + j
                p.emit("tensor", lambda e, j=j: e.matmul(
                    pb6[:, 8:9], rowbf[0:1, j * 128:(j + 1) * 128],
                    onescal[0:1, 0:1], start=True, stop=True),
                    waits=[("sca", s_rb)] + p.wall("vec"), inc="tmm")
                t_c = p.c("tmm")
                p.emit("vector", lambda e, col=col: e.tensor_copy(
                    modc[:, col:col + 1], pb6[:, 8:9]),
                    waits=[("tmm", t_c)], inc="vec")
        v_modc = p.c("vec")

        def emit_gate(c0, sg0, mg0, prev_v):
            p.emit("scalar", lambda e, c0=c0, sg0=sg0: e.sign(
                small[:, sg0:sg0 + 8], modc[:, c0:c0 + 8]),
                waits=[("vec", v_modc), ("vec", prev_v)], inc="sca")
            p.emit("scalar", lambda e, c0=c0, mg0=mg0: e.activation(
                small[:, mg0:mg0 + 8], modc[:, c0:c0 + 8], AF.Abs), inc="sca")
            s_g = p.c("sca")
            p.emit("vector", lambda e, mg0=mg0: e.tensor_scalar_max(
                small[:, mg0:mg0 + 8], small[:, mg0:mg0 + 8], 0.25),
                waits=[("sca", s_g), ("vec", v_modc)], inc="vec")
            p.drain("vector")
            p.emit("vector", lambda e, c0=c0, sg0=sg0, mg0=mg0: e.tensor_mul(
                modc[:, c0:c0 + 8], small[:, sg0:sg0 + 8], small[:, mg0:mg0 + 8]),
                inc="vec")
            return p.c("vec")
        print("PLAN: before gates seq=", p.seq)
        g1 = emit_gate(16, 32, 40, 0)
        emit_gate(40, 56, 32, g1)
        print("PLAN: after gates seq=", p.seq)
        p.emit("vector", lambda e: e.tensor_scalar_add(modc[:, 48:56], modc[:, 8:16], 1.0),
               inc="vec")
        p.emit("vector", lambda e: e.tensor_scalar_add(small[:, 48:56], modc[:, 32:40], 1.0),
               inc="vec")
        p.emit("vector", lambda e: e.tensor_copy(scolb[:, 8:16], modc[:, 0:8]), inc="vec")
        p.emit("vector", lambda e: e.tensor_copy(scolb[:, 16:24], modc[:, 24:32]), inc="vec")
        v_gates = p.c("vec")

        def emit_bprime(w_sb, scol_off, raw_off, dst_off, width, wstride):
            nch = max(1, width // 512)
            for n in range(nch):
                for k in range(8):
                    p.emit("tensor", lambda e, n=n, k=k, w_sb=w_sb, scol_off=scol_off,
                           wstride=wstride: e.matmul(
                        pb7[0:1, 0:512] if wstride >= 512 else pb7[0:1, 0:wstride],
                        scolb[:, scol_off + k:scol_off + k + 1],
                        w_sb[:, k * wstride + n * 512:
                             k * wstride + n * 512 + min(512, wstride)],
                        start=(k == 0), stop=(k == 7)),
                        waits=(([("vec", v_gates)] + p.wall("sca")) if (n == 0 and k == 0)
                               else []),
                        inc="tmm")
                t_b = p.c("tmm")
                p.emit("scalar", lambda e, wstride=wstride: e.copy(
                    rowbf[0:1, 0:min(512, wstride)], pb7[0:1, 0:min(512, wstride)]),
                    waits=[("tmm", t_b)], inc="sca")
                s_b = p.c("sca")
                ncols = min(512, wstride) // 128
                for j in range(ncols):
                    cc = n * 4 + j
                    p.emit("tensor", lambda e, j=j: e.matmul(
                        pb6[:, 8:9], rowbf[0:1, j * 128:(j + 1) * 128],
                        onescal[0:1, 0:1], start=True, stop=True),
                        waits=[("sca", s_b)] + p.wall("vec"), inc="tmm")
                    t_c = p.c("tmm")
                    p.emit("vector", lambda e, cc=cc, dst_off=dst_off: e.tensor_copy(
                        bcols[:, dst_off + cc:dst_off + cc + 1], pb6[:, 8:9]),
                        waits=[("tmm", t_c)], inc="vec")
            if raw_off is not None:
                p.drain("vector")
                p.emit("vector", lambda e, dst_off=dst_off, raw_off=raw_off: e.tensor_add(
                    bcols[:, dst_off:dst_off + width // 128],
                    bcols[:, dst_off:dst_off + width // 128],
                    small[:, raw_off:raw_off + width // 128]),
                    waits=[("dma", consts_done)], inc="vec")

        emit_bprime(wq_sb, 8, 0, 0, 1024, 1024)
        emit_bprime(wk_sb, 8, 8, 8, 1024, 1024)
        emit_bprime(wv_sb, 8, 16, 16, 1024, 1024)
        emit_bprime(u1_sb, 16, None, 24, 256, 256)
        v_bcols = p.c("vec")

        for w_sb in (wq_sb, wk_sb, wv_sb):
            for k in range(8):
                p.emit("vector", lambda e, w_sb=w_sb, k=k: e.tensor_scalar_mul(
                    w_sb[:, k * 1024:(k + 1) * 1024], w_sb[:, k * 1024:(k + 1) * 1024],
                    modc[:, 48 + k:49 + k]),
                    waits=([("vec", v_bcols)] if k == 0 and w_sb is wq_sb else []),
                    inc="vec")
        for k in range(8):
            p.emit("vector", lambda e, k=k: e.tensor_scalar_mul(
                u1_sb[:, k * 256:(k + 1) * 256], u1_sb[:, k * 256:(k + 1) * 256],
                small[:, 48 + k:49 + k]), inc="vec")
        p.emit("vector", lambda e: e.memset(small[:, 32:33], EPS), inc="vec")
        v_prep = p.c("vec")
        t_prep = p.c("tmm")
        s_prep = p.c("sca")
        p.mark("prep")

        x_free = 0
        x2_free_out = 0
        fin_v_prev = 0
        def emit_ln_chain(sum_off, sq1_off, sq2_off, two_sq):
            if two_sq:
                p.emit("vector", lambda e: e.tensor_add(
                    small[:, sq1_off:sq1_off + 4], small[:, sq1_off:sq1_off + 4],
                    small[:, sq2_off:sq2_off + 4]),
                    waits=[("sca", p.c("sca")), ("vec", p.c("vec"))], inc="vec")
            p.drain("vector")
            p.emit("vector", lambda e: e.tensor_scalar_mul(
                small[:, sum_off:sum_off + 4], small[:, sum_off:sum_off + 4],
                1.0 / 1024), inc="vec")
            p.drain("vector")
            p.emit("vector", lambda e: e.tensor_mul(
                small[:, 28:32], small[:, sum_off:sum_off + 4],
                small[:, sum_off:sum_off + 4]), inc="vec")
            p.emit("vector", lambda e: e.tensor_scalar_mul(
                small[:, sq1_off:sq1_off + 4], small[:, sq1_off:sq1_off + 4],
                1.0 / 1024), inc="vec")
            p.drain("vector")
            p.emit("vector", lambda e: e.tensor_sub(
                small[:, sq1_off:sq1_off + 4], small[:, sq1_off:sq1_off + 4],
                small[:, 28:32]), inc="vec")
            v_var = p.c("vec")
            p.emit("scalar", lambda e: e.activation(
                small[:, 28:32], small[:, sq1_off:sq1_off + 4], AF.Ln,
                bias=small[:, 32:33]),
                waits=[("vec", v_var)], inc="sca")
            p.drain("scalar")
            p.emit("scalar", lambda e: e.activation(
                small[:, 0:4], small[:, 28:32], AF.Exp, scale=-0.5), inc="sca")
            s_r = p.c("sca")
            p.emit("vector", lambda e: e.tensor_mul(
                small[:, 8:12], small[:, sum_off:sum_off + 4], small[:, 0:4]),
                waits=[("sca", s_r)], inc="vec")
            p.drain("vector")
            p.emit("vector", lambda e: e.tensor_scalar_mul(
                small[:, 8:12], small[:, 8:12], -1.0), inc="vec")
            return p.c("vec")

        def emit_ln1(tile, extra_w):
            p.dma("sync", x_t[:],
                  ap(x_h, tile * T * D, [[D, 128], [128 * D, NSUB], [1, D]]),
                  waits=list(extra_w), inc="xin")
            xin_done = p.c("xin")
            for s in range(NSUB):
                p.emit("vector", lambda e, s=s: e.reduce_sum(
                    small[:, 16 + s:17 + s], x_t[:, s * 1024:(s + 1) * 1024],
                    axis=mybir.AxisListType.X),
                    waits=([("xin", xin_done)] + (list(extra_w) if s == 0 else []))
                          if s == 0 else [],
                    inc="vec")
                p.emit("scalar", lambda e, s=s: e.activation(
                    qtmp[:, 0:512], x_t[:, s * 1024:s * 1024 + 512], AF.Square,
                    accum_out=small[:, 20 + s:21 + s]),
                    waits=(([("xin", xin_done)] + list(extra_w)) if s == 0 else []),
                    inc="sca")
                p.emit("scalar", lambda e, s=s: e.activation(
                    qtmp[:, 512:1024], x_t[:, s * 1024 + 512:(s + 1) * 1024], AF.Square,
                    accum_out=small[:, 24 + s:25 + s]),
                    inc="sca")
            v_nmr = emit_ln_chain(16, 20, 24, True)
            out = []
            for s in range(NSUB):
                p.emit("scalar", lambda e, s=s: e.activation(
                    arena[:, s * 1024:(s + 1) * 1024], x_t[:, s * 1024:(s + 1) * 1024],
                    AF.Identity, bias=small[:, 8 + s:9 + s], scale=small[:, s:s + 1]),
                    waits=([("vec", v_nmr)] if s == 0 else []), inc="sca")
                out.append(p.c("sca"))
            return out

        s_xhat_s = None
        for tile in range(n_tiles):
            if tile == 0:
                s_xhat_s = emit_ln1(0, [("vec", v_prep), ("tmm", t_prep)])
            p.mark("ln1")

            def emit_transposes(src_sb, s_ready, extra_first):
                ev = []
                for s in range(NSUB):
                    b = (s % 2) * 2
                    for fc in range(8):
                        waits = []
                        if fc == 0:
                            waits = [("sca", s_ready[s])]
                        if s == 0 and fc == 0:
                            waits += extra_first
                        if s >= 2 and fc == 0:
                            waits += list(ev[s - 2])
                        p.emit("tensor", lambda e, s=s, fc=fc, b=b, src_sb=src_sb: e.matmul(
                            bass.AP(PB[b + fc // 4], (fc % 4) * 128, [[512, 128], [1, 128]]),
                            src_sb[:, s * 1024 + fc * 128: s * 1024 + (fc + 1) * 128],
                            ident[:], start=True, stop=True),
                            waits=waits, inc="tmm")
                    t_tr = p.c("tmm")
                    if s % 2 == 0:
                        p.emit("vector", lambda e, s=s, b=b: e.tensor_copy(
                            ap(xhatT, s * 128, [[4096, 128], [512, 4], [1, 128]]),
                            ap(PB[b], 0, [[512, 128], [128, 4], [1, 128]])),
                            waits=[("tmm", t_tr)], inc="vec")
                        p.emit("vector", lambda e, s=s, b=b: e.tensor_copy(
                            ap(xhatT, 2048 + s * 128, [[4096, 128], [512, 4], [1, 128]]),
                            ap(PB[b + 1], 0, [[512, 128], [128, 4], [1, 128]])),
                            inc="vec")
                        ev.append([("vec", p.c("vec"))])
                    else:
                        p.emit("scalar", lambda e, s=s, b=b: e.copy(
                            ap(xhatT, s * 128, [[4096, 128], [512, 4], [1, 128]]),
                            ap(PB[b], 0, [[512, 128], [128, 4], [1, 128]])),
                            waits=[("tmm", t_tr)], inc="sca")
                        p.emit("scalar", lambda e, s=s, b=b: e.copy(
                            ap(xhatT, 2048 + s * 128, [[4096, 128], [512, 4], [1, 128]]),
                            ap(PB[b + 1], 0, [[512, 128], [128, 4], [1, 128]])),
                            inc="sca")
                        ev.append([("sca", p.c("sca"))])
                return p.c("vec"), p.c("sca")

            xT_v, xT_s = emit_transposes(arena, s_xhat_s,
                                         [("vec", fin_v_prev)] if tile > 0 else [])
            p.mark("xT")

            def proj_qk(w_sb, op, bcol_off, first_waits, stage):
                ev = []
                for cc in range(8):
                    bank = cc % 4
                    for kc in range(8):
                        waits = []
                        if cc == 0 and kc == 0:
                            waits = list(first_waits)
                        if cc >= 4 and kc == 0:
                            waits = list(ev[cc - 4])
                        p.emit("tensor", lambda e, cc=cc, kc=kc, bank=bank, w_sb=w_sb: e.matmul(
                            PB[bank][:, 0:512],
                            w_sb[:, kc * 1024 + cc * 128: kc * 1024 + (cc + 1) * 128],
                            xhatT[:, kc * 512:(kc + 1) * 512],
                            start=(kc == 0), stop=(kc == 7)),
                            waits=waits, inc="tmm")
                    t_p = p.c("tmm")
                    even_dst = ap(op, 2 * cc * 8, [[8192, 64], [128, 64], [1, 8]])
                    even_src = bass.AP(PB[bank], 0, [[512, 64], [8, 64], [1, 8]])
                    if cc % 2 == 0:
                        p.emit("vector", lambda e, d=even_dst, s_=even_src, cc=cc:
                               e.tensor_scalar_add(
                            d, s_, bcols[0:64, bcol_off + cc:bcol_off + cc + 1]),
                            waits=[("tmm", t_p)], inc="vec")
                        p.emit("vector", lambda e, cc=cc, bank=bank, stage=stage:
                               e.tensor_scalar_add(
                            stage[64:128, cc * 512:(cc + 1) * 512],
                            PB[bank][64:128, 0:512],
                            bcols[64:128, bcol_off + cc:bcol_off + cc + 1]),
                            inc="vec")
                        ev.append([("vec", p.c("vec"))])
                    else:
                        p.emit("scalar", lambda e, d=even_dst, s_=even_src, cc=cc:
                               e.activation(
                            d, s_, AF.Identity,
                            bias=bcols[0:64, bcol_off + cc:bcol_off + cc + 1]),
                            waits=[("tmm", t_p)], inc="sca")
                        p.emit("scalar", lambda e, cc=cc, bank=bank, stage=stage: e.activation(
                            stage[64:128, cc * 512:(cc + 1) * 512],
                            PB[bank][64:128, 0:512],
                            AF.Identity,
                            bias=bcols[64:128, bcol_off + cc:bcol_off + cc + 1]),
                            inc="sca")
                        ev.append([("sca", p.c("sca"))])
                return p.c("vec"), p.c("sca")

            ev_v = []
            for cc in range(8):
                bank = cc % 4
                for kc in range(8):
                    waits = []
                    if cc == 0 and kc == 0:
                        waits = [("vec", xT_v), ("sca", xT_s)] + p.wall("shuf")
                    if cc >= 4 and kc == 0:
                        waits = list(ev_v[cc - 4])
                    p.emit("tensor", lambda e, cc=cc, kc=kc, bank=bank: e.matmul(
                        PB[bank][:, 0:512],
                        wv_sb[:, kc * 1024 + cc * 128: kc * 1024 + (cc + 1) * 128],
                        xhatT[:, kc * 512:(kc + 1) * 512],
                        start=(kc == 0), stop=(kc == 7)),
                        waits=waits, inc="tmm")
                t_v = p.c("tmm")
                if cc % 2 == 0:
                    p.emit("vector", lambda e, cc=cc, bank=bank: e.tensor_scalar_add(
                        qtmp[:, cc * 512:(cc + 1) * 512], PB[bank][:, 0:512],
                        bcols[:, 16 + cc:17 + cc]),
                        waits=[("tmm", t_v)], inc="vec")
                    ev_v.append([("vec", p.c("vec"))])
                else:
                    p.emit("scalar", lambda e, cc=cc, bank=bank: e.activation(
                        qtmp[:, cc * 512:(cc + 1) * 512], PB[bank][:, 0:512],
                        AF.Identity, bias=bcols[:, 16 + cc:17 + cc]),
                        waits=[("tmm", t_v)], inc="sca")
                    ev_v.append([("sca", p.c("sca"))])
            vfm_v, vfm_s = p.c("vec"), p.c("sca")
            p.mark("qkv")

            vint_w = [("vec", vfm_v), ("sca", vfm_s)]
            first_v = True
            first_d = True
            dq = 0
            for j in range(16):
                src_band = (j % 2) * 64
                for half in range(2):
                    dst_band = half * 64
                    src_off = src_band * 4096 + (j // 2) * 512 + half * 256
                    dst_off = dst_band * 4096 + j * 8
                    if src_band == dst_band:
                        p.emit("vector", lambda e, src_off=src_off, dst_off=dst_off:
                               e.tensor_copy(
                            ap(vint, dst_off, [[4096, 64], [128, 32], [1, 8]]),
                            ap(qtmp, src_off, [[4096, 64], [8, 32], [1, 8]])),
                            waits=(vint_w if first_v else []), inc="vec")
                        first_v = False
                    else:
                        qeng = "sync" if dq % 2 == 0 else "scalar"
                        p.dma(qeng,
                              ap(vint, dst_off, [[4096, 64], [128, 32], [1, 8]]),
                              ap(qtmp, src_off, [[4096, 64], [8, 32], [1, 8]]),
                              waits=(vint_w if first_d else []), inc="shuf")
                        first_d = False
                        dq += 1
            vint_v, vint_dma = p.c("vec"), p.c("shuf")

            vq, sq_ = proj_qk(wq_sb, qop, 0,
                              [("vec", vint_v), ("shuf", vint_dma)], qtmp)
            for cc in range(8):
                p.dma("sync",
                      ap(qop, (2 * cc + 1) * 8, [[8192, 64], [128, 64], [1, 8]]),
                      ap(qtmp, 64 * 4096 + cc * 512, [[4096, 64], [8, 64], [1, 8]]),
                      waits=([("sca", sq_), ("vec", vq)] if cc == 0 else []), inc="shuf")
            q_odd = p.c("shuf")
            vk, sk_ = proj_qk(wk_sb, kop, 8, [("vec", vq), ("sca", sq_)], arena)
            for cc in range(8):
                p.dma("scalar",
                      ap(kop, (2 * cc + 1) * 8, [[8192, 64], [128, 64], [1, 8]]),
                      ap(arena, 64 * 4096 + cc * 512, [[4096, 64], [8, 64], [1, 8]]),
                      waits=([("sca", sk_), ("vec", vk)] if cc == 0 else []), inc="shuf")
            k_odd = p.c("shuf")

            ev_vt = []
            for gb in range(8):
                bank = 2 + gb % 2
                for gl in range(8):
                    g = gb * 8 + gl
                    half = g // 32
                    base = half * 64
                    waits = []
                    if gb == 0 and gl == 0:
                        waits = [("vec", vint_v), ("shuf", vint_dma),
                                 ("vec", vk), ("sca", sk_)]
                    if gb >= 2 and gl == 0:
                        waits = list(ev_vt[gb - 2])
                    p.emit("tensor", lambda e, g=g, gl=gl, bank=bank, base=base: e.matmul(
                        bass.AP(PB[bank], gl * 64, [[512, 128], [1, 64]]),
                        vint[base:base + 64, (g % 32) * 128:(g % 32) * 128 + 128],
                        i64d[base:base + 64, 0:64],
                        start=True, stop=True),
                        waits=waits, inc="tmm")
                t_vt = p.c("tmm")
                if gb % 2 == 0:
                    p.emit("vector", lambda e, gb=gb, bank=bank: e.tensor_copy(
                        ap(vop, gb * 8 * 65, [[4160, 128], [65, 8], [1, 64]]),
                        ap(PB[bank], 0, [[512, 128], [64, 8], [1, 64]])),
                        waits=[("tmm", t_vt)], inc="vec")
                    ev_vt.append([("vec", p.c("vec"))])
                else:
                    p.emit("scalar", lambda e, gb=gb, bank=bank: e.copy(
                        ap(vop, gb * 8 * 65, [[4160, 128], [65, 8], [1, 64]]),
                        ap(PB[bank], 0, [[512, 128], [64, 8], [1, 64]])),
                        waits=[("tmm", t_vt)], inc="sca")
                    ev_vt.append([("sca", p.c("sca"))])
            vop_v, vop_s = p.c("vec"), p.c("sca")
            p.mark("vop")

            att_w = [("shuf", q_odd), ("shuf", k_odd), ("vec", vq), ("vec", vk),
                     ("sca", sq_), ("sca", sk_), ("vec", vop_v), ("sca", vop_s)]
            exp_done = []
            attnv_done = []
            norm_done = []
            NB4 = NG // 4

            def emit_score(b):
                sbank = b % 2
                waits = (list(att_w) if b == 0 else [])
                if b >= 2:
                    waits = waits + [("sca", exp_done[b - 2])]
                for gl in range(4):
                    g = 4 * b + gl
                    p.emit("tensor", lambda e, g=g, gl=gl, sbank=sbank: e.matmul(
                        bass.AP(PB[sbank], gl * 128, [[512, 128], [1, 128]]),
                        kop[0:73, g * 128:(g + 1) * 128],
                        qop[0:73, g * 128:(g + 1) * 128],
                        start=True, stop=True),
                        waits=(waits if gl == 0 else []), inc="tmm")
                t_sc = p.c("tmm")
                ew = [("tmm", t_sc)]
                if b >= 2:
                    ew.append(("tmm", attnv_done[b - 2]))
                p.emit("scalar", lambda e, b=b, sbank=sbank: e.activation(
                    eop[:, (b % 2) * 512:(b % 2) * 512 + 512], PB[sbank][:, 0:512],
                    AF.Exp, scale=0.125),
                    waits=ew, inc="sca")
                exp_done.append(p.c("sca"))

            def emit_attnv(b):
                obank = 4 + b % 2
                aw = [("sca", exp_done[b])]
                if b >= 2:
                    aw.append(("vec", norm_done[b - 2]))
                for gl in range(4):
                    g = 4 * b + gl
                    p.emit("tensor", lambda e, g=g, gl=gl, b=b, obank=obank: e.matmul(
                        bass.AP(PB[obank], gl * 66, [[512, 128], [1, 65]]),
                        eop[:, (b % 2) * 512 + gl * 128:(b % 2) * 512 + (gl + 1) * 128],
                        vop[:, g * 65:(g + 1) * 65],
                        start=True, stop=True),
                        waits=(aw if gl == 0 else []), inc="tmm")
                attnv_done.append(p.c("tmm"))
                p.emit("vector", lambda e, b=b, obank=obank: e.tensor_copy(
                    den[:, (b % 2) * 8:(b % 2) * 8 + 4],
                    ap(PB[obank], 64, [[512, 128], [66, 4]])),
                    waits=[("tmm", attnv_done[b])], inc="vec")
                p.drain("vector")
                p.emit("vector", lambda e, b=b: e.reciprocal(
                    den[:, (b % 2) * 8 + 4:(b % 2) * 8 + 8],
                    den[:, (b % 2) * 8:(b % 2) * 8 + 4]), inc="vec")
                p.drain("vector")
                p.emit("vector", lambda e, b=b, obank=obank: e.tensor_tensor(
                    ap(qtmp, b * 256, [[4096, 128], [64, 4], [1, 64]]),
                    ap(PB[obank], 0, [[512, 128], [66, 4], [1, 64]]),
                    ap(den, (b % 2) * 8 + 4, [[16, 128], [1, 4], [0, 64]]),
                    op=ALU.mult), inc="vec")
                norm_done.append(p.c("vec"))

            emit_score(0)
            emit_score(1)
            for b in range(NB4):
                emit_attnv(b)
                if b + 2 < NB4:
                    emit_score(b + 2)
            p.mark("attcore")

            ofm_v = 0
            ofm_s = 0
            oq_dma = p.c("shuf")
            for b in range(NG // 4):
                bank = 6 + b % 2
                eng = "vector" if b % 2 == 0 else "scalar"
                bw = [("vec", norm_done[b])]
                if b >= 2:
                    bw += [(("vec", ofm_v) if b % 2 == 0 else ("sca", ofm_s))]
                for gl in range(4):
                    g = 4 * b + gl
                    p.emit("tensor", lambda e, g=g, gl=gl, bank=bank: e.matmul(
                        bass.AP(PB[bank], gl * 128, [[512, 64], [1, 128]]),
                        qtmp[:, g * 64:(g + 1) * 64],
                        ident[:], start=True, stop=True),
                        waits=(bw if gl == 0 else []), inc="tmm")
                t_oi = p.c("tmm")
                sw = [("tmm", t_oi)]
                if b % 4 in (0, 1) and b > 1:
                    sw.append(("shuf", oq_dma))
                for i in range(16):
                    if i % 2 == 0:
                        dst = ap(arena, (i // 2) * 512 + b * 32,
                                 [[4096, 64], [8, 4], [1, 8]])
                    else:
                        dst = ap(oddscr, (i // 2) * 128 + (b % 4) * 32,
                                 [[1024, 64], [8, 4], [1, 8]])
                    srcp = ap(PB[bank], i * 8, [[512, 64], [128, 4], [1, 8]])
                    if eng == "vector":
                        p.emit("vector", lambda e, d=dst, s_=srcp: e.tensor_copy(d, s_),
                               waits=(sw if i == 0 else []), inc="vec")
                    else:
                        p.emit("scalar", lambda e, d=dst, s_=srcp: e.copy(d, s_),
                               waits=(sw if i == 0 else []), inc="sca")
                if eng == "vector":
                    ofm_v = p.c("vec")
                else:
                    ofm_s = p.c("sca")
                if b % 4 == 3:
                    qb = b // 4
                    p.dma("sync",
                          ap(arena, 64 * 4096 + qb * 128, [[4096, 64], [512, 8], [1, 128]]),
                          ap(oddscr, 0, [[1024, 64], [128, 8], [1, 128]]),
                          waits=[("sca", p.c("sca")), ("vec", p.c("vec"))], inc="shuf")
                    oq_dma = p.c("shuf")
            ofm_dma = p.c("shuf")
            p.mark("attn")

            for cc in range(8):
                p.emit("vector", lambda e, cc=cc: e.tensor_scalar_mul(
                    eop[:, cc * 128:(cc + 1) * 128], ident[:], modc[:, 16 + cc:17 + cc]),
                    waits=([("tmm", attnv_done[-1])] if cc == 0 else []), inc="vec")
            dga_v = p.c("vec")
            res_v = 0
            res_v_s = []
            for s in range(NSUB):
                b2k = (s % 2) * 2
                for cc in range(8):
                    waits = []
                    if s == 0 and cc == 0:
                        waits = [("vec", ofm_v), ("sca", ofm_s), ("shuf", ofm_dma),
                                 ("vec", dga_v)]
                    if s >= 2 and cc == 0:
                        waits = [("vec", res_v)]
                    p.emit("tensor", lambda e, s=s, cc=cc, b2k=b2k: e.matmul(
                        bass.AP(PB[b2k + cc // 4], (cc % 4) * 128, [[512, 128], [1, 128]]),
                        arena[:, cc * 512 + s * 128: cc * 512 + s * 128 + 128],
                        eop[:, cc * 128:(cc + 1) * 128],
                        start=True, stop=True),
                        waits=waits, inc="tmm")
                t_ga = p.c("tmm")
                p.emit("vector", lambda e, s=s, b2k=b2k: e.scalar_tensor_tensor(
                    x2_t[:, s * 1024:s * 1024 + 512], x_t[:, s * 1024:s * 1024 + 512],
                    1.0, PB[b2k][:, 0:512], ALU.mult, ALU.add,
                    accum_out=small[:, 16 + s:17 + s]),
                    waits=[("tmm", t_ga)] +
                          ([("xout", x2_free_out)] if s == 0 and tile > 0 else []),
                    inc="vec")
                p.emit("vector", lambda e, s=s, b2k=b2k: e.scalar_tensor_tensor(
                    x2_t[:, s * 1024 + 512:(s + 1) * 1024],
                    x_t[:, s * 1024 + 512:(s + 1) * 1024],
                    1.0, PB[b2k + 1][:, 0:512], ALU.mult, ALU.add,
                    accum_out=small[:, 20 + s:21 + s]),
                    inc="vec")
                res_v = p.c("vec")
                res_v_s.append(res_v)
            x_free = res_v
            p.mark("res1")

            for s in range(NSUB):
                p.emit("scalar", lambda e, s=s: e.activation(
                    pb6[:, 0:512], x2_t[:, s * 1024:s * 1024 + 512], AF.Square,
                    accum_out=small[:, 24 + s:25 + s]),
                    waits=[("vec", res_v_s[s])], inc="sca")
                p.emit("scalar", lambda e, s=s: e.activation(
                    pb7[:, 0:512], x2_t[:, s * 1024 + 512:(s + 1) * 1024], AF.Square,
                    accum_out=small[:, 28 + s:29 + s]),
                    inc="sca")
            p.emit("vector", lambda e: e.tensor_add(small[:, 16:20], small[:, 16:20],
                                                    small[:, 20:24]),
                   waits=[("vec", res_v)], inc="vec")
            v_nmr2 = emit_ln_chain(16, 24, 28, True)
            s_xhat2_s = []
            for s in range(NSUB):
                p.emit("scalar", lambda e, s=s: e.activation(
                    arena[:, s * 1024:(s + 1) * 1024], x2_t[:, s * 1024:(s + 1) * 1024],
                    AF.Identity, bias=small[:, 8 + s:9 + s], scale=small[:, s:s + 1]),
                    waits=([("vec", v_nmr2)] if s == 0 else []), inc="sca")
                s_xhat2_s.append(p.c("sca"))

            xT2_v, xT2_s = emit_transposes(arena, s_xhat2_s, [])

            for c2 in range(2):
                bank = 4 + c2
                for kc in range(8):
                    p.emit("tensor", lambda e, c2=c2, kc=kc, bank=bank: e.matmul(
                        PB[bank][:, 0:512],
                        u1_sb[:, kc * 256 + c2 * 128: kc * 256 + (c2 + 1) * 128],
                        xhatT[:, kc * 512:(kc + 1) * 512],
                        start=(kc == 0), stop=(kc == 7)),
                        waits=([("vec", xT2_v), ("sca", xT2_s)]
                               if (c2 == 0 and kc == 0) else []),
                        inc="tmm")
                t_pt = p.c("tmm")
                p.emit("vector", lambda e, c2=c2, bank=bank: e.tensor_scalar_add(
                    ptsb[:, c2 * 512:(c2 + 1) * 512], PB[bank][:, 0:512],
                    bcols[:, 24 + c2:25 + c2]),
                    waits=[("tmm", t_pt)], inc="vec")
            pt_v = p.c("vec")
            p.mark("pt")

            gelu_done = []

            def emit_h(fc):
                bank = fc % 4
                for rc in range(2):
                    waits = []
                    if fc == 0 and rc == 0:
                        waits = [("vec", pt_v)]
                    if fc >= 4 and rc == 0:
                        waits = [("sca", gelu_done[fc - 4])]
                    p.emit("tensor", lambda e, fc=fc, rc=rc, bank=bank: e.matmul(
                        PB[bank][:, 0:512],
                        v1_sb[:, rc * 4096 + fc * 128: rc * 4096 + (fc + 1) * 128],
                        ptsb[:, rc * 512:(rc + 1) * 512],
                        start=(rc == 0), stop=(rc == 1)),
                        waits=waits, inc="tmm")
                t_h = p.c("tmm")
                p.emit("scalar", lambda e, fc=fc, bank=bank: e.activation(
                    htring[:, (fc % 2) * 512:(fc % 2) * 512 + 512], PB[bank][:, 0:512],
                    AF.Gelu_apprx_tanh, bias=bcols[:, 32 + fc:33 + fc]),
                    waits=[("tmm", t_h)], inc="sca")
                gelu_done.append(p.c("sca"))

            def emit_u2(fc):
                for c2 in range(2):
                    p.emit("tensor", lambda e, fc=fc, c2=c2: e.matmul(
                        PB[6 + c2][:, 0:512],
                        u2_sb[:, fc * 256 + c2 * 128: fc * 256 + (c2 + 1) * 128],
                        htring[:, (fc % 2) * 512:(fc % 2) * 512 + 512],
                        start=(fc == 0), stop=(fc == 31), skip_group_check=True),
                        waits=([("sca", gelu_done[fc])] if c2 == 0 else []), inc="tmm")

            emit_h(0)
            emit_h(1)
            for fc in range(32):
                emit_u2(fc)
                if fc + 2 < 32:
                    emit_h(fc + 2)
            t_q2 = p.c("tmm")
            for c2 in range(2):
                p.emit("vector", lambda e, c2=c2: e.tensor_copy(
                    q2sb[:, c2 * 512:(c2 + 1) * 512], PB[6 + c2][:, 0:512]),
                    waits=[("tmm", t_q2)], inc="vec")
            q2_v = p.c("vec")
            p.mark("q2")

            fprime_s = []
            for cc in range(8):
                bank = 4 + cc % 2
                for rc in range(2):
                    waits = []
                    if cc == 0 and rc == 0:
                        waits = [("vec", q2_v)]
                    if cc >= 2 and rc == 0:
                        waits = [("sca", fprime_s[cc - 2])]
                    p.emit("tensor", lambda e, cc=cc, rc=rc, bank=bank: e.matmul(
                        PB[bank][:, 0:512],
                        v2_sb[:, rc * 1024 + cc * 128: rc * 1024 + (cc + 1) * 128],
                        q2sb[:, rc * 512:(rc + 1) * 512],
                        start=(rc == 0), stop=(rc == 1)),
                        waits=waits, inc="tmm")
                t_f = p.c("tmm")
                p.emit("scalar", lambda e, cc=cc, bank=bank: e.activation(
                    vint[:, cc * 512:(cc + 1) * 512], PB[bank][:, 0:512],
                    AF.Identity, bias=bcols[:, 64 + cc:65 + cc]),
                    waits=[("tmm", t_f)], inc="sca")
                fprime_s.append(p.c("sca"))
            fp_s = p.c("sca")
            p.mark("fp")

            if tile + 1 < n_tiles:
                t_ga_all = t_ga
                s_xhat_s_next = emit_ln1(tile + 1,
                                         [("vec", x_free), ("tmm", t_ga_all)])
            for cc in range(8):
                p.emit("vector", lambda e, cc=cc: e.tensor_scalar_mul(
                    eop[:, cc * 128:(cc + 1) * 128], ident[:], modc[:, 40 + cc:41 + cc]),
                    waits=([("tmm", p.c("tmm"))] if cc == 0 else []), inc="vec")
            dgb_v = p.c("vec")
            fin_v = 0
            for s in range(NSUB):
                b2k = (s % 2) * 2
                for cc in range(8):
                    waits = []
                    if s == 0 and cc == 0:
                        waits = [("sca", fp_s), ("vec", dgb_v)]
                    if s >= 2 and cc == 0:
                        waits = [("vec", fin_v)]
                    p.emit("tensor", lambda e, s=s, cc=cc, b2k=b2k: e.matmul(
                        bass.AP(PB[b2k + cc // 4], (cc % 4) * 128, [[512, 128], [1, 128]]),
                        vint[:, cc * 512 + s * 128: cc * 512 + s * 128 + 128],
                        eop[:, cc * 128:(cc + 1) * 128],
                        start=True, stop=True),
                        waits=waits, inc="tmm")
                t_gb = p.c("tmm")
                p.emit("vector", lambda e, s=s, b2k=b2k: e.tensor_add(
                    x2_t[:, s * 1024:s * 1024 + 512], x2_t[:, s * 1024:s * 1024 + 512],
                    PB[b2k][:, 0:512]),
                    waits=[("tmm", t_gb)], inc="vec")
                p.emit("vector", lambda e, s=s, b2k=b2k: e.tensor_add(
                    x2_t[:, s * 1024 + 512:(s + 1) * 1024],
                    x2_t[:, s * 1024 + 512:(s + 1) * 1024],
                    PB[b2k + 1][:, 0:512]),
                    inc="vec")
                fin_v = p.c("vec")
                p.dma("sync",
                      ap(y_h, tile * T * D + s * 128 * D, [[D, 128], [1, D]]),
                      x2_t[:, s * 1024:(s + 1) * 1024],
                      waits=[("vec", fin_v)], inc="xout")
            fin_v_prev = fin_v
            x2_free_out = p.c("xout")
            if tile + 1 < n_tiles:
                s_xhat_s = s_xhat_s_next

        if phase is not None:
            print("PLAN: truncating at", phase, "=", p.marks[phase], "of", p.seq)
            p.truncate(p.marks[phase])
        if dbg:
            last = p.wall("vec", "sca", "tmm", "xout", "shuf")
            dump_map = {
                "d_qop": qop[:], "d_kop": kop[:], "d_vop": vop[:], "d_eop": eop[:],
                "d_otmp": qtmp[:], "d_ofm": arena[:], "d_xhatT": xhatT[:],
                "d_x2": x2_t[:], "d_modc": modc[:], "d_bcols": bcols[:],
                "d_vint": vint[:], "d_ptsb": ptsb[:], "d_q2sb": q2sb[:],
                "d_small": small[:], "d_scolb": scolb[:],
            }
            first = True
            for name in dbg:
                p.dma("gpsimd", dbg[name], dump_map[name],
                      waits=(last if first else []), inc="dma")
                first = False

        p.emit("sync", lambda e: e.nop(),
               waits=p.wall("xout", "dma", "shuf", "cast"))

        with nc.Block() as block:
            p.realize(block, sems)

    return nc


def host_consts():
    ident = np.eye(128, dtype=BF)
    i64d = np.zeros((128, 64), dtype=BF)
    i64d[0:64] = np.eye(64, dtype=BF)
    i64d[64:128] = np.eye(64, dtype=BF)
    base = np.zeros((9, 128), dtype=np.float32)
    for r in range(8):
        for t in range(8):
            for h in range(16):
                base[r, h * 8 + t] = 32.0 if t == r else 0.0
    maskk = np.tile(base, (1, 64)).astype(BF)
    maskq = np.tile(base, (1, 64)).astype(BF)
    maskk[8, :] = np.float32(-32.0)
    maskq[8, :] = np.float32(32.0)
    return ident, i64d, maskq, maskk


_NC_CACHE = {}


def kernel(L0_attn_x, L0_attn_Wq, L0_attn_bq, L0_attn_Wk, L0_attn_bk, L0_attn_Wv,
           L0_attn_bv, L0_ffn_U1, L0_ffn_V1, L0_ffn_b1, L0_ffn_U2, L0_ffn_V2,
           L0_ffn_b2, L0_c, L0_wmod):
    x = np.ascontiguousarray(np.asarray(L0_attn_x, dtype=np.float32))
    B, S, Dm = x.shape
    half = S // 2
    ident, i64d, maskq, maskk = host_consts()
    shared = {
        "wq": np.ascontiguousarray(L0_attn_Wq, dtype=np.float32),
        "wk": np.ascontiguousarray(L0_attn_Wk, dtype=np.float32),
        "wv": np.ascontiguousarray(L0_attn_Wv, dtype=np.float32),
        "bq": np.asarray(L0_attn_bq, dtype=np.float32).reshape(1, Dm),
        "bk": np.asarray(L0_attn_bk, dtype=np.float32).reshape(1, Dm),
        "bv": np.asarray(L0_attn_bv, dtype=np.float32).reshape(1, Dm),
        "u1": np.ascontiguousarray(L0_ffn_U1, dtype=np.float32),
        "v1": np.ascontiguousarray(L0_ffn_V1, dtype=np.float32),
        "b1": np.asarray(L0_ffn_b1, dtype=np.float32).reshape(1, DFF),
        "u2": np.ascontiguousarray(L0_ffn_U2, dtype=np.float32),
        "v2": np.ascontiguousarray(L0_ffn_V2, dtype=np.float32),
        "b2": np.asarray(L0_ffn_b2, dtype=np.float32).reshape(1, Dm),
        "wmod": np.ascontiguousarray(L0_wmod, dtype=np.float32),
        "ident": ident, "i64d": i64d, "maskq": maskq, "maskk": maskk,
    }
    cvec = np.asarray(L0_c, dtype=np.float32)
    n_tiles = (half + T - 1) // T
    if n_tiles not in _NC_CACHE:
        _NC_CACHE[n_tiles] = build_core_program(n_tiles)
    nc = _NC_CACHE[n_tiles]
    in_maps = []
    for core in range(8):
        b = core // 2
        h0 = (core % 2) * half
        m = dict(shared)
        m["x"] = np.ascontiguousarray(x[b, h0:h0 + half])
        m["cvec"] = np.ascontiguousarray(cvec[b:b + 1])
        in_maps.append(m)
    res = run_bass_kernel_spmd(nc, in_maps, core_ids=list(range(8)))
    out = np.empty((B, S, Dm), dtype=np.float32)
    for core in range(8):
        b = core // 2
        h0 = (core % 2) * half
        out[b, h0:h0 + half] = res.results[core]["y"]
    return out


# version 04
# speedup vs baseline: 1.0275x; 1.0162x over previous
import numpy as np
import ml_dtypes
import concourse.bass as bass
import concourse.mybir as mybir
from concourse.bass_utils import run_bass_kernel_spmd

F32 = mybir.dt.float32
BF16 = mybir.dt.bfloat16
AF = mybir.ActivationFunctionType
ALU = mybir.AluOpType
BF = ml_dtypes.bfloat16

D = 1024
NH = 16
DK = 64
DFF = 4096
RANK = 256
T = 512
NSUB = T // 128
NG = T // 8
EPS = 1e-5

ENGS = ("sync", "gpsimd", "tensor", "vector", "scalar")


class Prog:
    def __init__(self):
        self.ops = {e: [] for e in ENGS}
        self.ctr = {}
        self.seq = 0
        self.marks = {}

    def mark(self, name):
        self.marks[name] = self.seq

    def c(self, sem):
        return self.ctr.get(sem, 0)

    def truncate(self, limit_seq):
        self.ctr = {}
        for e in ENGS:
            kept = [op for op in self.ops[e] if op[3] <= limit_seq]
            self.ops[e] = kept
            for fn, waits, tok, sq in kept:
                if tok is not None:
                    self.ctr[tok[0]] = self.ctr.get(tok[0], 0) + tok[1]

    def emit(self, eng, fn, waits=(), inc=None, inc_k=1):
        tok = None
        if inc is not None:
            self.ctr[inc] = self.ctr.get(inc, 0) + inc_k
            tok = (inc, inc_k)
        self.seq += 1
        self.ops[eng].append((fn, [w for w in waits if w[1] > 0], tok, self.seq))

    def dma(self, eng, out_ap, in_ap, waits=(), inc="dma", slow=False):
        self.emit(eng, lambda e, o=out_ap, i=in_ap, sl=slow: e.dma_start(
            o, i, allow_slow_non_contiguous=sl),
            waits=waits, inc=inc, inc_k=16)

    def drain(self, eng):
        self.ops[eng].append((lambda e: e.drain(), [], None, self.seq))

    def wall(self, *sems):
        return [(s, self.c(s)) for s in sems]

    def realize(self, block, sems):
        for eng_name in ENGS:
            seq = self.ops[eng_name]
            if not seq:
                continue
            def mk(seq=seq):
                def body(eng):
                    for fn, waits, tok, _sq in seq:
                        for s, v in waits:
                            eng.wait_ge(sems[s], v)
                        instr = fn(eng)
                        if tok is not None:
                            instr.then_inc(sems[tok[0]], tok[1])
                return body
            getattr(block, eng_name)(mk())


def ap(t, offset, dims):
    if isinstance(t, bass.AP):
        t = t.tensor
    return bass.AP(t, offset, [list(d) for d in dims])


def build_core_program(n_tiles, debug=(), phase=None, skip=()):
    nc = bass.Bass("TRN2", target_bir_lowering=False)
    NTOK = n_tiles * T

    x_h = nc.dram_tensor("x", [NTOK, D], F32, kind="ExternalInput").ap()
    y_h = nc.dram_tensor("y", [NTOK, D], F32, kind="ExternalOutput").ap()
    wq_h = nc.dram_tensor("wq", [D, D], F32, kind="ExternalInput").ap()
    wk_h = nc.dram_tensor("wk", [D, D], F32, kind="ExternalInput").ap()
    wv_h = nc.dram_tensor("wv", [D, D], F32, kind="ExternalInput").ap()
    bq_h = nc.dram_tensor("bq", [1, D], F32, kind="ExternalInput").ap()
    bk_h = nc.dram_tensor("bk", [1, D], F32, kind="ExternalInput").ap()
    bv_h = nc.dram_tensor("bv", [1, D], F32, kind="ExternalInput").ap()
    u1_h = nc.dram_tensor("u1", [D, RANK], F32, kind="ExternalInput").ap()
    v1_h = nc.dram_tensor("v1", [RANK, DFF], F32, kind="ExternalInput").ap()
    b1_h = nc.dram_tensor("b1", [1, DFF], F32, kind="ExternalInput").ap()
    u2_h = nc.dram_tensor("u2", [DFF, RANK], F32, kind="ExternalInput").ap()
    v2_h = nc.dram_tensor("v2", [RANK, D], F32, kind="ExternalInput").ap()
    b2_h = nc.dram_tensor("b2", [1, D], F32, kind="ExternalInput").ap()
    c_h = nc.dram_tensor("cvec", [1, D], F32, kind="ExternalInput").ap()
    wmod_h = nc.dram_tensor("wmod", [D, 6 * D], F32, kind="ExternalInput").ap()
    ident_h = nc.dram_tensor("ident", [128, 128], BF16, kind="ExternalInput").ap()
    i64d_h = nc.dram_tensor("i64d", [128, 64], BF16, kind="ExternalInput").ap()
    maskq_h = nc.dram_tensor("maskq", [9, 8192], BF16, kind="ExternalInput").ap()
    maskk_h = nc.dram_tensor("maskk", [9, 8192], BF16, kind="ExternalInput").ap()
    dbg = {}
    for name, shape, dt in debug:
        dbg[name] = nc.dram_tensor(name, list(shape), dt, kind="ExternalOutput").ap()

    p = Prog()
    SEMS = ["dma", "cast", "tmm", "vec", "sca", "xin", "xout", "shuf"]

    from contextlib import ExitStack
    with ExitStack() as ctx:
        wq_sb = ctx.enter_context(nc.sbuf_tensor("wq_sb", [128, 8192], BF16))
        wk_sb = ctx.enter_context(nc.sbuf_tensor("wk_sb", [128, 8192], BF16))
        wv_sb = ctx.enter_context(nc.sbuf_tensor("wv_sb", [128, 8192], BF16))
        u1_sb = ctx.enter_context(nc.sbuf_tensor("u1_sb", [128, 2048], BF16))
        v1_sb = ctx.enter_context(nc.sbuf_tensor("v1_sb", [128, 8192], BF16))
        u2_sb = ctx.enter_context(nc.sbuf_tensor("u2_sb", [128, 8192], BF16))
        v2_sb = ctx.enter_context(nc.sbuf_tensor("v2_sb", [128, 2048], BF16))
        ident = ctx.enter_context(nc.sbuf_tensor("ident_sb", [128, 128], BF16))
        i64d = ctx.enter_context(nc.sbuf_tensor("i64d_sb", [128, 64], BF16))
        qop = ctx.enter_context(nc.sbuf_tensor("qop", [73, 8192], BF16))
        kop = ctx.enter_context(nc.sbuf_tensor("kop", [73, 8192], BF16))
        vop = ctx.enter_context(nc.sbuf_tensor("vop", [128, 4160], BF16))
        bcols = ctx.enter_context(nc.sbuf_tensor("bcols", [128, 80], F32))
        modc = ctx.enter_context(nc.sbuf_tensor("modc", [128, 56], F32))
        rowbuf = ctx.enter_context(nc.sbuf_tensor("rowbuf", [1, 512], F32))
        rowbf = ctx.enter_context(nc.sbuf_tensor("rowbf", [1, 1024], BF16))
        scolb = ctx.enter_context(nc.sbuf_tensor("scolb", [128, 24], BF16))
        onescal = ctx.enter_context(nc.sbuf_tensor("onescal", [1, 8], BF16))
        small = ctx.enter_context(nc.sbuf_tensor("small", [128, 64], F32))
        x_t = ctx.enter_context(nc.sbuf_tensor("x_t", [128, 4096], F32))
        x2_t = ctx.enter_context(nc.sbuf_tensor("x2_t", [128, 4096], F32))
        arena = ctx.enter_context(nc.sbuf_tensor("arena", [128, 4096], BF16))
        xhatT = ctx.enter_context(nc.sbuf_tensor("xhatT", [128, 4096], BF16))
        qtmp = ctx.enter_context(nc.sbuf_tensor("qtmp", [128, 4096], BF16))
        oddscr = ctx.enter_context(nc.sbuf_tensor("oddscr", [64, 1024], BF16))
        vint = ctx.enter_context(nc.sbuf_tensor("vint", [128, 4096], BF16))
        eop = ctx.enter_context(nc.sbuf_tensor("eop", [128, 1024], BF16))
        den = ctx.enter_context(nc.sbuf_tensor("den", [128, 16], F32))
        ptsb = ctx.enter_context(nc.sbuf_tensor("ptsb", [128, 1024], BF16))
        htring = ctx.enter_context(nc.sbuf_tensor("htring", [128, 1024], BF16))
        q2sb = ctx.enter_context(nc.sbuf_tensor("q2sb", [128, 1024], BF16))
        pb0 = ctx.enter_context(nc.psum_tensor("pb0", [128, 512], F32))
        pb1 = ctx.enter_context(nc.psum_tensor("pb1", [128, 512], F32))
        pb2 = ctx.enter_context(nc.psum_tensor("pb2", [128, 512], F32))
        pb3 = ctx.enter_context(nc.psum_tensor("pb3", [128, 512], F32))
        pb4 = ctx.enter_context(nc.psum_tensor("pb4", [128, 512], F32))
        pb5 = ctx.enter_context(nc.psum_tensor("pb5", [128, 512], F32))
        pb6 = ctx.enter_context(nc.psum_tensor("pb6", [128, 512], F32))
        pb7 = ctx.enter_context(nc.psum_tensor("pb7", [128, 512], F32))
        s_dma = ctx.enter_context(nc.semaphore("s_dma"))
        s_cast = ctx.enter_context(nc.semaphore("s_cast"))
        s_tmm = ctx.enter_context(nc.semaphore("s_tmm"))
        s_vec = ctx.enter_context(nc.semaphore("s_vec"))
        s_sca = ctx.enter_context(nc.semaphore("s_sca"))
        s_xin = ctx.enter_context(nc.semaphore("s_xin"))
        s_xout = ctx.enter_context(nc.semaphore("s_xout"))
        s_shuf = ctx.enter_context(nc.semaphore("s_shuf"))
        sems = {"dma": s_dma, "cast": s_cast, "tmm": s_tmm, "vec": s_vec,
                "sca": s_sca, "xin": s_xin, "xout": s_xout, "shuf": s_shuf}
        PB = [pb0, pb1, pb2, pb3, pb4, pb5, pb6, pb7]

        p.dma("sync", ident[:], ident_h)
        p.dma("sync", i64d[:], i64d_h)
        p.dma("sync", qop[64:73, :], maskq_h)
        p.dma("sync", kop[64:73, :], maskk_h)
        p.dma("sync", rowbuf[0:1, 0:512], ap(c_h, 0, [[1, 1], [1, 512]]))
        p.dma("sync", ap(bcols, 32, [[80, 128], [1, 32]]),
              ap(b1_h, 0, [[1, 128], [128, 32]]), slow=True)
        p.dma("sync", ap(bcols, 64, [[80, 128], [1, 8]]),
              ap(b2_h, 0, [[1, 128], [128, 8]]), slow=True)
        p.dma("sync", ap(small, 0, [[64, 128], [1, 8]]),
              ap(bq_h, 0, [[1, 128], [128, 8]]), slow=True)
        p.dma("sync", ap(small, 8, [[64, 128], [1, 8]]),
              ap(bk_h, 0, [[1, 128], [128, 8]]), slow=True)
        p.dma("sync", ap(small, 16, [[64, 128], [1, 8]]),
              ap(bv_h, 0, [[1, 128], [128, 8]]), slow=True)
        consts_done = p.c("dma")

        for w_sb, w_hh in ((wq_sb, wq_h), (wk_sb, wk_h), (wv_sb, wv_h)):
            p.dma("gpsimd", w_sb[:],
                  ap(w_hh, 0, [[1024, 128], [128 * 1024, 8], [1, 1024]]), inc="cast")
        p.dma("gpsimd", u1_sb[:],
              ap(u1_h, 0, [[256, 128], [128 * 256, 8], [1, 256]]), inc="cast")
        p.dma("gpsimd", v1_sb[:],
              ap(v1_h, 0, [[4096, 128], [128 * 4096, 2], [1, 4096]]), inc="cast")
        p.dma("gpsimd", u2_sb[:],
              ap(u2_h, 0, [[256, 128], [128 * 256, 32], [1, 256]]), inc="cast")
        p.dma("gpsimd", v2_sb[:],
              ap(v2_h, 0, [[1024, 128], [128 * 1024, 2], [1, 1024]]), inc="cast")
        w_done = p.c("cast")

        p.emit("vector", lambda e: e.memset(onescal[:], 1.0), inc="vec")
        p.emit("vector", lambda e: e.memset(
            ap(vop, 64, [[4160, 128], [65, 64]]), 1.0), inc="vec")
        v_ones = p.c("vec")

        p.emit("scalar", lambda e: e.activation(rowbf[0:1, 0:512], rowbuf[0:1, 0:512],
                                                AF.Silu),
               waits=[("dma", consts_done)], inc="sca")
        s_h1 = p.c("sca")
        p.dma("sync", rowbuf[0:1, 0:512], ap(c_h, 512, [[1, 1], [1, 512]]),
              waits=[("sca", s_h1)])
        c2_done = p.c("dma")
        p.emit("scalar", lambda e: e.activation(rowbf[0:1, 512:1024], rowbuf[0:1, 0:512],
                                                AF.Silu),
               waits=[("dma", c2_done)], inc="sca")
        s_silu = p.c("sca")
        for k in range(8):
            p.emit("tensor", lambda e, k=k: e.matmul(
                pb6[:, k:k + 1], rowbf[0:1, k * 128:(k + 1) * 128], onescal[0:1, 0:1],
                start=True, stop=True),
                waits=(p.wall("sca", "vec") if k == 0 else []), inc="tmm")
        t_scol = p.c("tmm")
        p.emit("vector", lambda e: e.tensor_copy(scolb[:, 0:8], pb6[:, 0:8]),
               waits=[("tmm", t_scol)], inc="vec")
        v_scolb = p.c("vec")

        for n in range(12):
            ring = vint if n % 2 == 0 else qtmp
            p.dma("gpsimd", ap(ring, 0, [[4096, 128], [512, 8], [1, 512]]),
                  ap(wmod_h, n * 512, [[6144, 128], [128 * 6144, 8], [1, 512]]),
                  waits=(p.wall("tmm") if n >= 2 else []),
                  inc="cast")
            cd = p.c("cast")
            for k in range(8):
                p.emit("tensor", lambda e, k=k, ring=ring: e.matmul(
                    pb7[0:1, 0:512], scolb[:, k:k + 1],
                    ring[:, k * 512:(k + 1) * 512],
                    start=(k == 0), stop=(k == 7)),
                    waits=(([("cast", cd), ("vec", v_scolb)] + p.wall("sca"))
                           if k == 0 else []),
                    inc="tmm")
            t_mod = p.c("tmm")
            p.emit("scalar", lambda e: e.copy(rowbuf[0:1, 0:512], pb7[0:1, 0:512]),
                   waits=[("tmm", t_mod)], inc="sca")
            p.drain("scalar")
            p.emit("scalar", lambda e: e.copy(rowbf[0:1, 0:512], rowbuf[0:1, 0:512]),
                   inc="sca")
            s_rb = p.c("sca")
            for j in range(4):
                col = n * 4 + j
                p.emit("tensor", lambda e, j=j: e.matmul(
                    pb6[:, 8:9], rowbf[0:1, j * 128:(j + 1) * 128],
                    onescal[0:1, 0:1], start=True, stop=True),
                    waits=[("sca", s_rb)] + p.wall("vec"), inc="tmm")
                t_c = p.c("tmm")
                p.emit("vector", lambda e, col=col: e.tensor_copy(
                    modc[:, col:col + 1], pb6[:, 8:9]),
                    waits=[("tmm", t_c)], inc="vec")
        v_modc = p.c("vec")

        def emit_gate(c0, sg0, mg0, prev_v):
            p.emit("scalar", lambda e, c0=c0, sg0=sg0: e.sign(
                small[:, sg0:sg0 + 8], modc[:, c0:c0 + 8]),
                waits=[("vec", v_modc), ("vec", prev_v)], inc="sca")
            p.emit("scalar", lambda e, c0=c0, mg0=mg0: e.activation(
                small[:, mg0:mg0 + 8], modc[:, c0:c0 + 8], AF.Abs), inc="sca")
            s_g = p.c("sca")
            p.emit("vector", lambda e, mg0=mg0: e.tensor_scalar_max(
                small[:, mg0:mg0 + 8], small[:, mg0:mg0 + 8], 0.25),
                waits=[("sca", s_g), ("vec", v_modc)], inc="vec")
            p.drain("vector")
            p.emit("vector", lambda e, c0=c0, sg0=sg0, mg0=mg0: e.tensor_mul(
                modc[:, c0:c0 + 8], small[:, sg0:sg0 + 8], small[:, mg0:mg0 + 8]),
                inc="vec")
            return p.c("vec")
        print("PLAN: before gates seq=", p.seq)
        g1 = emit_gate(16, 32, 40, 0)
        emit_gate(40, 56, 32, g1)
        print("PLAN: after gates seq=", p.seq)
        p.emit("vector", lambda e: e.tensor_scalar_add(modc[:, 48:56], modc[:, 8:16], 1.0),
               inc="vec")
        p.emit("vector", lambda e: e.tensor_scalar_add(small[:, 48:56], modc[:, 32:40], 1.0),
               inc="vec")
        p.emit("vector", lambda e: e.tensor_copy(scolb[:, 8:16], modc[:, 0:8]), inc="vec")
        p.emit("vector", lambda e: e.tensor_copy(scolb[:, 16:24], modc[:, 24:32]), inc="vec")
        v_gates = p.c("vec")

        def emit_bprime(w_sb, scol_off, raw_off, dst_off, width, wstride):
            nch = max(1, width // 512)
            for n in range(nch):
                for k in range(8):
                    p.emit("tensor", lambda e, n=n, k=k, w_sb=w_sb, scol_off=scol_off,
                           wstride=wstride: e.matmul(
                        pb7[0:1, 0:512] if wstride >= 512 else pb7[0:1, 0:wstride],
                        scolb[:, scol_off + k:scol_off + k + 1],
                        w_sb[:, k * wstride + n * 512:
                             k * wstride + n * 512 + min(512, wstride)],
                        start=(k == 0), stop=(k == 7)),
                        waits=(([("vec", v_gates)] + p.wall("sca")) if (n == 0 and k == 0)
                               else []),
                        inc="tmm")
                t_b = p.c("tmm")
                p.emit("scalar", lambda e, wstride=wstride: e.copy(
                    rowbf[0:1, 0:min(512, wstride)], pb7[0:1, 0:min(512, wstride)]),
                    waits=[("tmm", t_b)], inc="sca")
                s_b = p.c("sca")
                ncols = min(512, wstride) // 128
                for j in range(ncols):
                    cc = n * 4 + j
                    p.emit("tensor", lambda e, j=j: e.matmul(
                        pb6[:, 8:9], rowbf[0:1, j * 128:(j + 1) * 128],
                        onescal[0:1, 0:1], start=True, stop=True),
                        waits=[("sca", s_b)] + p.wall("vec"), inc="tmm")
                    t_c = p.c("tmm")
                    p.emit("vector", lambda e, cc=cc, dst_off=dst_off: e.tensor_copy(
                        bcols[:, dst_off + cc:dst_off + cc + 1], pb6[:, 8:9]),
                        waits=[("tmm", t_c)], inc="vec")
            if raw_off is not None:
                p.drain("vector")
                p.emit("vector", lambda e, dst_off=dst_off, raw_off=raw_off: e.tensor_add(
                    bcols[:, dst_off:dst_off + width // 128],
                    bcols[:, dst_off:dst_off + width // 128],
                    small[:, raw_off:raw_off + width // 128]),
                    waits=[("dma", consts_done)], inc="vec")

        emit_bprime(wq_sb, 8, 0, 0, 1024, 1024)
        emit_bprime(wk_sb, 8, 8, 8, 1024, 1024)
        emit_bprime(wv_sb, 8, 16, 16, 1024, 1024)
        emit_bprime(u1_sb, 16, None, 24, 256, 256)
        v_bcols = p.c("vec")

        for w_sb in (wq_sb, wk_sb, wv_sb):
            for k in range(8):
                p.emit("vector", lambda e, w_sb=w_sb, k=k: e.tensor_scalar_mul(
                    w_sb[:, k * 1024:(k + 1) * 1024], w_sb[:, k * 1024:(k + 1) * 1024],
                    modc[:, 48 + k:49 + k]),
                    waits=([("vec", v_bcols)] if k == 0 and w_sb is wq_sb else []),
                    inc="vec")
        for k in range(8):
            p.emit("vector", lambda e, k=k: e.tensor_scalar_mul(
                u1_sb[:, k * 256:(k + 1) * 256], u1_sb[:, k * 256:(k + 1) * 256],
                small[:, 48 + k:49 + k]), inc="vec")
        p.emit("vector", lambda e: e.memset(small[:, 32:33], EPS), inc="vec")
        v_prep = p.c("vec")
        t_prep = p.c("tmm")
        s_prep = p.c("sca")
        p.mark("prep")

        x_free = 0
        x2_free_out = 0
        fin_v_prev = 0
        def emit_ln_chain(sum_off, sq1_off, sq2_off, two_sq):
            if two_sq:
                p.emit("vector", lambda e: e.tensor_add(
                    small[:, sq1_off:sq1_off + 4], small[:, sq1_off:sq1_off + 4],
                    small[:, sq2_off:sq2_off + 4]),
                    waits=[("sca", p.c("sca")), ("vec", p.c("vec"))], inc="vec")
            p.drain("vector")
            p.emit("vector", lambda e: e.tensor_scalar_mul(
                small[:, sum_off:sum_off + 4], small[:, sum_off:sum_off + 4],
                1.0 / 1024), inc="vec")
            p.drain("vector")
            p.emit("vector", lambda e: e.tensor_mul(
                small[:, 28:32], small[:, sum_off:sum_off + 4],
                small[:, sum_off:sum_off + 4]), inc="vec")
            p.emit("vector", lambda e: e.tensor_scalar_mul(
                small[:, sq1_off:sq1_off + 4], small[:, sq1_off:sq1_off + 4],
                1.0 / 1024), inc="vec")
            p.drain("vector")
            p.emit("vector", lambda e: e.tensor_sub(
                small[:, sq1_off:sq1_off + 4], small[:, sq1_off:sq1_off + 4],
                small[:, 28:32]), inc="vec")
            v_var = p.c("vec")
            p.emit("scalar", lambda e: e.activation(
                small[:, 28:32], small[:, sq1_off:sq1_off + 4], AF.Ln,
                bias=small[:, 32:33]),
                waits=[("vec", v_var)], inc="sca")
            p.drain("scalar")
            p.emit("scalar", lambda e: e.activation(
                small[:, 0:4], small[:, 28:32], AF.Exp, scale=-0.5), inc="sca")
            s_r = p.c("sca")
            p.emit("vector", lambda e: e.tensor_mul(
                small[:, 8:12], small[:, sum_off:sum_off + 4], small[:, 0:4]),
                waits=[("sca", s_r)], inc="vec")
            p.drain("vector")
            p.emit("vector", lambda e: e.tensor_scalar_mul(
                small[:, 8:12], small[:, 8:12], -1.0), inc="vec")
            return p.c("vec")

        def emit_ln1(tile, extra_w):
            p.dma("sync", x_t[:],
                  ap(x_h, tile * T * D, [[D, 128], [128 * D, NSUB], [1, D]]),
                  waits=list(extra_w), inc="xin")
            xin_done = p.c("xin")
            for s in range(NSUB):
                p.emit("vector", lambda e, s=s: e.reduce_sum(
                    small[:, 16 + s:17 + s], x_t[:, s * 1024:(s + 1) * 1024],
                    axis=mybir.AxisListType.X),
                    waits=([("xin", xin_done)] + (list(extra_w) if s == 0 else []))
                          if s == 0 else [],
                    inc="vec")
                p.emit("scalar", lambda e, s=s: e.activation(
                    qtmp[:, 0:512], x_t[:, s * 1024:s * 1024 + 512], AF.Square,
                    accum_out=small[:, 20 + s:21 + s]),
                    waits=(([("xin", xin_done)] + list(extra_w)) if s == 0 else []),
                    inc="sca")
                p.emit("scalar", lambda e, s=s: e.activation(
                    qtmp[:, 512:1024], x_t[:, s * 1024 + 512:(s + 1) * 1024], AF.Square,
                    accum_out=small[:, 24 + s:25 + s]),
                    inc="sca")
            v_nmr = emit_ln_chain(16, 20, 24, True)
            out = []
            for s in range(NSUB):
                p.emit("scalar", lambda e, s=s: e.activation(
                    arena[:, s * 1024:(s + 1) * 1024], x_t[:, s * 1024:(s + 1) * 1024],
                    AF.Identity, bias=small[:, 8 + s:9 + s], scale=small[:, s:s + 1]),
                    waits=([("vec", v_nmr)] if s == 0 else []), inc="sca")
                out.append(p.c("sca"))
            return out

        s_xhat_s = None
        for tile in range(n_tiles):
            if tile == 0:
                s_xhat_s = emit_ln1(0, [("vec", v_prep), ("tmm", t_prep)])
            p.mark("ln1")

            def emit_transposes(src_sb, s_ready, extra_first):
                ev = []
                for s in range(NSUB):
                    b = (s % 2) * 2
                    for fc in range(8):
                        waits = []
                        if fc == 0:
                            waits = [("sca", s_ready[s])]
                        if s == 0 and fc == 0:
                            waits += extra_first
                        if s >= 2 and fc == 0:
                            waits += list(ev[s - 2])
                        p.emit("tensor", lambda e, s=s, fc=fc, b=b, src_sb=src_sb: e.matmul(
                            bass.AP(PB[b + fc // 4], (fc % 4) * 128, [[512, 128], [1, 128]]),
                            src_sb[:, s * 1024 + fc * 128: s * 1024 + (fc + 1) * 128],
                            ident[:], start=True, stop=True),
                            waits=waits, inc="tmm")
                    t_tr = p.c("tmm")
                    if s % 2 == 0:
                        p.emit("vector", lambda e, s=s, b=b: e.tensor_copy(
                            ap(xhatT, s * 128, [[4096, 128], [512, 4], [1, 128]]),
                            ap(PB[b], 0, [[512, 128], [128, 4], [1, 128]])),
                            waits=[("tmm", t_tr)], inc="vec")
                        p.emit("vector", lambda e, s=s, b=b: e.tensor_copy(
                            ap(xhatT, 2048 + s * 128, [[4096, 128], [512, 4], [1, 128]]),
                            ap(PB[b + 1], 0, [[512, 128], [128, 4], [1, 128]])),
                            inc="vec")
                        ev.append([("vec", p.c("vec"))])
                    else:
                        p.emit("scalar", lambda e, s=s, b=b: e.copy(
                            ap(xhatT, s * 128, [[4096, 128], [512, 4], [1, 128]]),
                            ap(PB[b], 0, [[512, 128], [128, 4], [1, 128]])),
                            waits=[("tmm", t_tr)], inc="sca")
                        p.emit("scalar", lambda e, s=s, b=b: e.copy(
                            ap(xhatT, 2048 + s * 128, [[4096, 128], [512, 4], [1, 128]]),
                            ap(PB[b + 1], 0, [[512, 128], [128, 4], [1, 128]])),
                            inc="sca")
                        ev.append([("sca", p.c("sca"))])
                return p.c("vec"), p.c("sca")

            xT_v, xT_s = emit_transposes(arena, s_xhat_s,
                                         [("vec", fin_v_prev)] if tile > 0 else [])
            p.mark("xT")

            def proj_qk(w_sb, op, bcol_off, first_waits, stage, stage_w=()):
                ev = []
                for cc in range(8):
                    bank = cc % 4
                    for kc in range(8):
                        waits = []
                        if cc == 0 and kc == 0:
                            waits = list(first_waits)
                        if cc >= 4 and kc == 0:
                            waits = list(ev[cc - 4])
                        p.emit("tensor", lambda e, cc=cc, kc=kc, bank=bank, w_sb=w_sb: e.matmul(
                            PB[bank][:, 0:512],
                            w_sb[:, kc * 1024 + cc * 128: kc * 1024 + (cc + 1) * 128],
                            xhatT[:, kc * 512:(kc + 1) * 512],
                            start=(kc == 0), stop=(kc == 7)),
                            waits=waits, inc="tmm")
                    t_p = p.c("tmm")
                    even_dst = ap(op, 2 * cc * 8, [[8192, 64], [128, 64], [1, 8]])
                    even_src = bass.AP(PB[bank], 0, [[512, 64], [8, 64], [1, 8]])
                    if cc % 2 == 0:
                        p.emit("vector", lambda e, d=even_dst, s_=even_src, cc=cc:
                               e.tensor_scalar_add(
                            d, s_, bcols[0:64, bcol_off + cc:bcol_off + cc + 1]),
                            waits=[("tmm", t_p)], inc="vec")
                        p.emit("vector", lambda e, cc=cc, bank=bank, stage=stage:
                               e.tensor_scalar_add(
                            stage[64:128, cc * 512:(cc + 1) * 512],
                            PB[bank][64:128, 0:512],
                            bcols[64:128, bcol_off + cc:bcol_off + cc + 1]),
                            waits=(list(stage_w) if cc == 0 else []), inc="vec")
                        ev.append([("vec", p.c("vec"))])
                    else:
                        p.emit("scalar", lambda e, d=even_dst, s_=even_src, cc=cc:
                               e.activation(
                            d, s_, AF.Identity,
                            bias=bcols[0:64, bcol_off + cc:bcol_off + cc + 1]),
                            waits=[("tmm", t_p)], inc="sca")
                        p.emit("scalar", lambda e, cc=cc, bank=bank, stage=stage: e.activation(
                            stage[64:128, cc * 512:(cc + 1) * 512],
                            PB[bank][64:128, 0:512],
                            AF.Identity,
                            bias=bcols[64:128, bcol_off + cc:bcol_off + cc + 1]),
                            waits=(list(stage_w) if cc == 1 else []), inc="sca")
                        ev.append([("sca", p.c("sca"))])
                return p.c("vec"), p.c("sca")

            ev_v = []
            for cc in range(8):
                bank = cc % 4
                for kc in range(8):
                    waits = []
                    if cc == 0 and kc == 0:
                        waits = [("vec", xT_v), ("sca", xT_s)] + p.wall("shuf")
                    if cc >= 4 and kc == 0:
                        waits = list(ev_v[cc - 4])
                    p.emit("tensor", lambda e, cc=cc, kc=kc, bank=bank: e.matmul(
                        PB[bank][:, 0:512],
                        wv_sb[:, kc * 1024 + cc * 128: kc * 1024 + (cc + 1) * 128],
                        xhatT[:, kc * 512:(kc + 1) * 512],
                        start=(kc == 0), stop=(kc == 7)),
                        waits=waits, inc="tmm")
                t_v = p.c("tmm")
                if cc % 2 == 0:
                    p.emit("vector", lambda e, cc=cc, bank=bank: e.tensor_scalar_add(
                        qtmp[:, cc * 512:(cc + 1) * 512], PB[bank][:, 0:512],
                        bcols[:, 16 + cc:17 + cc]),
                        waits=[("tmm", t_v)], inc="vec")
                    ev_v.append([("vec", p.c("vec"))])
                else:
                    p.emit("scalar", lambda e, cc=cc, bank=bank: e.activation(
                        qtmp[:, cc * 512:(cc + 1) * 512], PB[bank][:, 0:512],
                        AF.Identity, bias=bcols[:, 16 + cc:17 + cc]),
                        waits=[("tmm", t_v)], inc="sca")
                    ev_v.append([("sca", p.c("sca"))])
            vfm_v, vfm_s = p.c("vec"), p.c("sca")
            p.mark("qkv")

            vint_w = [("vec", vfm_v), ("sca", vfm_s)]
            first_v = True
            first_d = True
            dq = 0
            for j in range(16):
                src_band = (j % 2) * 64
                for half in range(2):
                    dst_band = half * 64
                    src_off = src_band * 4096 + (j // 2) * 512 + half * 256
                    dst_off = dst_band * 4096 + j * 8
                    if src_band == dst_band:
                        p.emit("vector", lambda e, src_off=src_off, dst_off=dst_off:
                               e.tensor_copy(
                            ap(vint, dst_off, [[4096, 64], [128, 32], [1, 8]]),
                            ap(qtmp, src_off, [[4096, 64], [8, 32], [1, 8]])),
                            waits=(vint_w if first_v else []), inc="vec")
                        first_v = False
                    else:
                        qeng = "sync" if dq % 2 == 0 else "scalar"
                        p.dma(qeng,
                              ap(vint, dst_off, [[4096, 64], [128, 32], [1, 8]]),
                              ap(qtmp, src_off, [[4096, 64], [8, 32], [1, 8]]),
                              waits=(vint_w if first_d else []), inc="shuf")
                        first_d = False
                        dq += 1
            vint_v, vint_dma = p.c("vec"), p.c("shuf")

            vq, sq_ = proj_qk(wq_sb, qop, 0,
                              [("vec", vfm_v), ("sca", vfm_s)], qtmp,
                              stage_w=[("vec", vint_v), ("shuf", vint_dma)])
            for cc in range(8):
                p.dma("sync",
                      ap(qop, (2 * cc + 1) * 8, [[8192, 64], [128, 64], [1, 8]]),
                      ap(qtmp, 64 * 4096 + cc * 512, [[4096, 64], [8, 64], [1, 8]]),
                      waits=([("sca", sq_), ("vec", vq)] if cc == 0 else []), inc="shuf")
            q_odd = p.c("shuf")
            vk, sk_ = proj_qk(wk_sb, kop, 8, [("vec", vq), ("sca", sq_)], arena)
            for cc in range(8):
                p.dma("scalar",
                      ap(kop, (2 * cc + 1) * 8, [[8192, 64], [128, 64], [1, 8]]),
                      ap(arena, 64 * 4096 + cc * 512, [[4096, 64], [8, 64], [1, 8]]),
                      waits=([("sca", sk_), ("vec", vk)] if cc == 0 else []), inc="shuf")
            k_odd = p.c("shuf")

            ev_vt = []
            for gb in range(8):
                bank = 2 + gb % 2
                for gl in range(8):
                    g = gb * 8 + gl
                    half = g // 32
                    base = half * 64
                    waits = []
                    if gb == 0 and gl == 0:
                        waits = [("vec", vint_v), ("shuf", vint_dma),
                                 ("vec", vk), ("sca", sk_)]
                    if gb >= 2 and gl == 0:
                        waits = list(ev_vt[gb - 2])
                    p.emit("tensor", lambda e, g=g, gl=gl, bank=bank, base=base: e.matmul(
                        bass.AP(PB[bank], gl * 64, [[512, 128], [1, 64]]),
                        vint[base:base + 64, (g % 32) * 128:(g % 32) * 128 + 128],
                        i64d[base:base + 64, 0:64],
                        start=True, stop=True),
                        waits=waits, inc="tmm")
                t_vt = p.c("tmm")
                if gb % 2 == 0:
                    p.emit("vector", lambda e, gb=gb, bank=bank: e.tensor_copy(
                        ap(vop, gb * 8 * 65, [[4160, 128], [65, 8], [1, 64]]),
                        ap(PB[bank], 0, [[512, 128], [64, 8], [1, 64]])),
                        waits=[("tmm", t_vt)], inc="vec")
                    ev_vt.append([("vec", p.c("vec"))])
                else:
                    p.emit("scalar", lambda e, gb=gb, bank=bank: e.copy(
                        ap(vop, gb * 8 * 65, [[4160, 128], [65, 8], [1, 64]]),
                        ap(PB[bank], 0, [[512, 128], [64, 8], [1, 64]])),
                        waits=[("tmm", t_vt)], inc="sca")
                    ev_vt.append([("sca", p.c("sca"))])
            vop_v, vop_s = p.c("vec"), p.c("sca")
            p.mark("vop")

            att_w = [("shuf", q_odd), ("shuf", k_odd), ("vec", vq), ("vec", vk),
                     ("sca", sq_), ("sca", sk_), ("vec", vop_v), ("sca", vop_s)]
            exp_done = []
            attnv_done = []
            norm_done = []
            NB4 = NG // 4

            def emit_score(b):
                sbank = b % 2
                waits = (list(att_w) if b == 0 else [])
                if b >= 2:
                    waits = waits + [("sca", exp_done[b - 2])]
                for gl in range(4):
                    g = 4 * b + gl
                    p.emit("tensor", lambda e, g=g, gl=gl, sbank=sbank: e.matmul(
                        bass.AP(PB[sbank], gl * 128, [[512, 128], [1, 128]]),
                        kop[0:73, g * 128:(g + 1) * 128],
                        qop[0:73, g * 128:(g + 1) * 128],
                        start=True, stop=True),
                        waits=(waits if gl == 0 else []), inc="tmm")
                t_sc = p.c("tmm")
                ew = [("tmm", t_sc)]
                if b >= 2:
                    ew.append(("tmm", attnv_done[b - 2]))
                p.emit("scalar", lambda e, b=b, sbank=sbank: e.activation(
                    eop[:, (b % 2) * 512:(b % 2) * 512 + 512], PB[sbank][:, 0:512],
                    AF.Exp, scale=0.125),
                    waits=ew, inc="sca")
                exp_done.append(p.c("sca"))

            def emit_attnv(b):
                obank = 4 + b % 2
                aw = [("sca", exp_done[b])]
                if b >= 2:
                    aw.append(("vec", norm_done[b - 2]))
                for gl in range(4):
                    g = 4 * b + gl
                    p.emit("tensor", lambda e, g=g, gl=gl, b=b, obank=obank: e.matmul(
                        bass.AP(PB[obank], gl * 66, [[512, 128], [1, 65]]),
                        eop[:, (b % 2) * 512 + gl * 128:(b % 2) * 512 + (gl + 1) * 128],
                        vop[:, g * 65:(g + 1) * 65],
                        start=True, stop=True),
                        waits=(aw if gl == 0 else []), inc="tmm")
                attnv_done.append(p.c("tmm"))
                p.emit("vector", lambda e, b=b, obank=obank: e.tensor_copy(
                    den[:, (b % 2) * 8:(b % 2) * 8 + 4],
                    ap(PB[obank], 64, [[512, 128], [66, 4]])),
                    waits=[("tmm", attnv_done[b])], inc="vec")
                p.drain("vector")
                p.emit("vector", lambda e, b=b: e.reciprocal(
                    den[:, (b % 2) * 8 + 4:(b % 2) * 8 + 8],
                    den[:, (b % 2) * 8:(b % 2) * 8 + 4]), inc="vec")
                p.drain("vector")
                p.emit("vector", lambda e, b=b, obank=obank: e.tensor_tensor(
                    ap(qtmp, b * 256, [[4096, 128], [64, 4], [1, 64]]),
                    ap(PB[obank], 0, [[512, 128], [66, 4], [1, 64]]),
                    ap(den, (b % 2) * 8 + 4, [[16, 128], [1, 4], [0, 64]]),
                    op=ALU.mult), inc="vec")
                norm_done.append(p.c("vec"))

            emit_score(0)
            emit_score(1)
            for b in range(NB4):
                emit_attnv(b)
                if b + 2 < NB4:
                    emit_score(b + 2)
            p.mark("attcore")

            ofm_v = 0
            ofm_s = 0
            oq_dma = p.c("shuf")
            for b in range(NG // 4):
                bank = 6 + b % 2
                eng = "vector" if b % 2 == 0 else "scalar"
                bw = [("vec", norm_done[b])]
                if b >= 2:
                    bw += [(("vec", ofm_v) if b % 2 == 0 else ("sca", ofm_s))]
                for gl in range(4):
                    g = 4 * b + gl
                    p.emit("tensor", lambda e, g=g, gl=gl, bank=bank: e.matmul(
                        bass.AP(PB[bank], gl * 128, [[512, 64], [1, 128]]),
                        qtmp[:, g * 64:(g + 1) * 64],
                        ident[:], start=True, stop=True),
                        waits=(bw if gl == 0 else []), inc="tmm")
                t_oi = p.c("tmm")
                sw = [("tmm", t_oi)]
                if b % 4 in (0, 1) and b > 1:
                    sw.append(("shuf", oq_dma))
                for i in range(16):
                    if i % 2 == 0:
                        dst = ap(arena, (i // 2) * 512 + b * 32,
                                 [[4096, 64], [8, 4], [1, 8]])
                    else:
                        dst = ap(oddscr, (i // 2) * 128 + (b % 4) * 32,
                                 [[1024, 64], [8, 4], [1, 8]])
                    srcp = ap(PB[bank], i * 8, [[512, 64], [128, 4], [1, 8]])
                    if eng == "vector":
                        p.emit("vector", lambda e, d=dst, s_=srcp: e.tensor_copy(d, s_),
                               waits=(sw if i == 0 else []), inc="vec")
                    else:
                        p.emit("scalar", lambda e, d=dst, s_=srcp: e.copy(d, s_),
                               waits=(sw if i == 0 else []), inc="sca")
                if eng == "vector":
                    ofm_v = p.c("vec")
                else:
                    ofm_s = p.c("sca")
                if b % 4 == 3:
                    qb = b // 4
                    p.dma("sync",
                          ap(arena, 64 * 4096 + qb * 128, [[4096, 64], [512, 8], [1, 128]]),
                          ap(oddscr, 0, [[1024, 64], [128, 8], [1, 128]]),
                          waits=[("sca", p.c("sca")), ("vec", p.c("vec"))], inc="shuf")
                    oq_dma = p.c("shuf")
            ofm_dma = p.c("shuf")
            p.mark("attn")

            for cc in range(8):
                p.emit("vector", lambda e, cc=cc: e.tensor_scalar_mul(
                    eop[:, cc * 128:(cc + 1) * 128], ident[:], modc[:, 16 + cc:17 + cc]),
                    waits=([("tmm", attnv_done[-1])] if cc == 0 else []), inc="vec")
            dga_v = p.c("vec")
            res_v = 0
            res_v_s = []
            for s in range(NSUB):
                b2k = (s % 2) * 2
                for cc in range(8):
                    waits = []
                    if s == 0 and cc == 0:
                        waits = [("vec", ofm_v), ("sca", ofm_s), ("shuf", ofm_dma),
                                 ("vec", dga_v)]
                    if s >= 2 and cc == 0:
                        waits = [("vec", res_v)]
                    p.emit("tensor", lambda e, s=s, cc=cc, b2k=b2k: e.matmul(
                        bass.AP(PB[b2k + cc // 4], (cc % 4) * 128, [[512, 128], [1, 128]]),
                        arena[:, cc * 512 + s * 128: cc * 512 + s * 128 + 128],
                        eop[:, cc * 128:(cc + 1) * 128],
                        start=True, stop=True),
                        waits=waits, inc="tmm")
                t_ga = p.c("tmm")
                p.emit("vector", lambda e, s=s, b2k=b2k: e.scalar_tensor_tensor(
                    x2_t[:, s * 1024:s * 1024 + 512], x_t[:, s * 1024:s * 1024 + 512],
                    1.0, PB[b2k][:, 0:512], ALU.mult, ALU.add,
                    accum_out=small[:, 16 + s:17 + s]),
                    waits=[("tmm", t_ga)] +
                          ([("xout", x2_free_out)] if s == 0 and tile > 0 else []),
                    inc="vec")
                p.emit("vector", lambda e, s=s, b2k=b2k: e.scalar_tensor_tensor(
                    x2_t[:, s * 1024 + 512:(s + 1) * 1024],
                    x_t[:, s * 1024 + 512:(s + 1) * 1024],
                    1.0, PB[b2k + 1][:, 0:512], ALU.mult, ALU.add,
                    accum_out=small[:, 20 + s:21 + s]),
                    inc="vec")
                res_v = p.c("vec")
                res_v_s.append(res_v)
            x_free = res_v
            p.mark("res1")

            for s in range(NSUB):
                p.emit("scalar", lambda e, s=s: e.activation(
                    pb6[:, 0:512], x2_t[:, s * 1024:s * 1024 + 512], AF.Square,
                    accum_out=small[:, 24 + s:25 + s]),
                    waits=[("vec", res_v_s[s])], inc="sca")
                p.emit("scalar", lambda e, s=s: e.activation(
                    pb7[:, 0:512], x2_t[:, s * 1024 + 512:(s + 1) * 1024], AF.Square,
                    accum_out=small[:, 28 + s:29 + s]),
                    inc="sca")
            p.emit("vector", lambda e: e.tensor_add(small[:, 16:20], small[:, 16:20],
                                                    small[:, 20:24]),
                   waits=[("vec", res_v)], inc="vec")
            v_nmr2 = emit_ln_chain(16, 24, 28, True)
            s_xhat2_s = []
            for s in range(NSUB):
                p.emit("scalar", lambda e, s=s: e.activation(
                    arena[:, s * 1024:(s + 1) * 1024], x2_t[:, s * 1024:(s + 1) * 1024],
                    AF.Identity, bias=small[:, 8 + s:9 + s], scale=small[:, s:s + 1]),
                    waits=([("vec", v_nmr2)] if s == 0 else []), inc="sca")
                s_xhat2_s.append(p.c("sca"))

            xT2_v, xT2_s = emit_transposes(arena, s_xhat2_s, [])

            for c2 in range(2):
                bank = 4 + c2
                for kc in range(8):
                    p.emit("tensor", lambda e, c2=c2, kc=kc, bank=bank: e.matmul(
                        PB[bank][:, 0:512],
                        u1_sb[:, kc * 256 + c2 * 128: kc * 256 + (c2 + 1) * 128],
                        xhatT[:, kc * 512:(kc + 1) * 512],
                        start=(kc == 0), stop=(kc == 7)),
                        waits=([("vec", xT2_v), ("sca", xT2_s)]
                               if (c2 == 0 and kc == 0) else []),
                        inc="tmm")
                t_pt = p.c("tmm")
                p.emit("vector", lambda e, c2=c2, bank=bank: e.tensor_scalar_add(
                    ptsb[:, c2 * 512:(c2 + 1) * 512], PB[bank][:, 0:512],
                    bcols[:, 24 + c2:25 + c2]),
                    waits=[("tmm", t_pt)], inc="vec")
            pt_v = p.c("vec")
            p.mark("pt")

            gelu_done = []

            def emit_h(fc):
                bank = fc % 4
                for rc in range(2):
                    waits = []
                    if fc == 0 and rc == 0:
                        waits = [("vec", pt_v)]
                    if fc >= 4 and rc == 0:
                        waits = [("sca", gelu_done[fc - 4])]
                    p.emit("tensor", lambda e, fc=fc, rc=rc, bank=bank: e.matmul(
                        PB[bank][:, 0:512],
                        v1_sb[:, rc * 4096 + fc * 128: rc * 4096 + (fc + 1) * 128],
                        ptsb[:, rc * 512:(rc + 1) * 512],
                        start=(rc == 0), stop=(rc == 1)),
                        waits=waits, inc="tmm")
                t_h = p.c("tmm")
                p.emit("scalar", lambda e, fc=fc, bank=bank: e.activation(
                    htring[:, (fc % 2) * 512:(fc % 2) * 512 + 512], PB[bank][:, 0:512],
                    AF.Gelu_apprx_tanh, bias=bcols[:, 32 + fc:33 + fc]),
                    waits=[("tmm", t_h)], inc="sca")
                gelu_done.append(p.c("sca"))

            def emit_u2(fc):
                for c2 in range(2):
                    p.emit("tensor", lambda e, fc=fc, c2=c2: e.matmul(
                        PB[6 + c2][:, 0:512],
                        u2_sb[:, fc * 256 + c2 * 128: fc * 256 + (c2 + 1) * 128],
                        htring[:, (fc % 2) * 512:(fc % 2) * 512 + 512],
                        start=(fc == 0), stop=(fc == 31), skip_group_check=True),
                        waits=([("sca", gelu_done[fc])] if c2 == 0 else []), inc="tmm")

            emit_h(0)
            emit_h(1)
            for fc in range(32):
                emit_u2(fc)
                if fc + 2 < 32:
                    emit_h(fc + 2)
            t_q2 = p.c("tmm")
            for c2 in range(2):
                p.emit("vector", lambda e, c2=c2: e.tensor_copy(
                    q2sb[:, c2 * 512:(c2 + 1) * 512], PB[6 + c2][:, 0:512]),
                    waits=[("tmm", t_q2)], inc="vec")
            q2_v = p.c("vec")
            p.mark("q2")

            fprime_s = []
            for cc in range(8):
                bank = 4 + cc % 2
                for rc in range(2):
                    waits = []
                    if cc == 0 and rc == 0:
                        waits = [("vec", q2_v)]
                    if cc >= 2 and rc == 0:
                        waits = [("sca", fprime_s[cc - 2])]
                    p.emit("tensor", lambda e, cc=cc, rc=rc, bank=bank: e.matmul(
                        PB[bank][:, 0:512],
                        v2_sb[:, rc * 1024 + cc * 128: rc * 1024 + (cc + 1) * 128],
                        q2sb[:, rc * 512:(rc + 1) * 512],
                        start=(rc == 0), stop=(rc == 1)),
                        waits=waits, inc="tmm")
                t_f = p.c("tmm")
                p.emit("scalar", lambda e, cc=cc, bank=bank: e.activation(
                    vint[:, cc * 512:(cc + 1) * 512], PB[bank][:, 0:512],
                    AF.Identity, bias=bcols[:, 64 + cc:65 + cc]),
                    waits=[("tmm", t_f)], inc="sca")
                fprime_s.append(p.c("sca"))
            fp_s = p.c("sca")
            p.mark("fp")

            if tile + 1 < n_tiles:
                t_ga_all = t_ga
                s_xhat_s_next = emit_ln1(tile + 1,
                                         [("vec", x_free), ("tmm", t_ga_all)])
            for cc in range(8):
                p.emit("vector", lambda e, cc=cc: e.tensor_scalar_mul(
                    eop[:, cc * 128:(cc + 1) * 128], ident[:], modc[:, 40 + cc:41 + cc]),
                    waits=([("tmm", p.c("tmm"))] if cc == 0 else []), inc="vec")
            dgb_v = p.c("vec")
            fin_v = 0
            for s in range(NSUB):
                b2k = (s % 2) * 2
                for cc in range(8):
                    waits = []
                    if s == 0 and cc == 0:
                        waits = [("sca", fp_s), ("vec", dgb_v)]
                    if s >= 2 and cc == 0:
                        waits = [("vec", fin_v)]
                    p.emit("tensor", lambda e, s=s, cc=cc, b2k=b2k: e.matmul(
                        bass.AP(PB[b2k + cc // 4], (cc % 4) * 128, [[512, 128], [1, 128]]),
                        vint[:, cc * 512 + s * 128: cc * 512 + s * 128 + 128],
                        eop[:, cc * 128:(cc + 1) * 128],
                        start=True, stop=True),
                        waits=waits, inc="tmm")
                t_gb = p.c("tmm")
                p.emit("vector", lambda e, s=s, b2k=b2k: e.tensor_add(
                    x2_t[:, s * 1024:s * 1024 + 512], x2_t[:, s * 1024:s * 1024 + 512],
                    PB[b2k][:, 0:512]),
                    waits=[("tmm", t_gb)], inc="vec")
                p.emit("vector", lambda e, s=s, b2k=b2k: e.tensor_add(
                    x2_t[:, s * 1024 + 512:(s + 1) * 1024],
                    x2_t[:, s * 1024 + 512:(s + 1) * 1024],
                    PB[b2k + 1][:, 0:512]),
                    inc="vec")
                fin_v = p.c("vec")
                p.dma("sync",
                      ap(y_h, tile * T * D + s * 128 * D, [[D, 128], [1, D]]),
                      x2_t[:, s * 1024:(s + 1) * 1024],
                      waits=[("vec", fin_v)], inc="xout")
            fin_v_prev = fin_v
            x2_free_out = p.c("xout")
            if tile + 1 < n_tiles:
                s_xhat_s = s_xhat_s_next

        if phase is not None:
            print("PLAN: truncating at", phase, "=", p.marks[phase], "of", p.seq)
            p.truncate(p.marks[phase])
        if dbg:
            last = p.wall("vec", "sca", "tmm", "xout", "shuf")
            dump_map = {
                "d_qop": qop[:], "d_kop": kop[:], "d_vop": vop[:], "d_eop": eop[:],
                "d_otmp": qtmp[:], "d_ofm": arena[:], "d_xhatT": xhatT[:],
                "d_x2": x2_t[:], "d_modc": modc[:], "d_bcols": bcols[:],
                "d_vint": vint[:], "d_ptsb": ptsb[:], "d_q2sb": q2sb[:],
                "d_small": small[:], "d_scolb": scolb[:],
            }
            first = True
            for name in dbg:
                p.dma("gpsimd", dbg[name], dump_map[name],
                      waits=(last if first else []), inc="dma")
                first = False

        p.emit("sync", lambda e: e.nop(),
               waits=p.wall("xout", "dma", "shuf", "cast"))

        with nc.Block() as block:
            p.realize(block, sems)

    return nc


def host_consts():
    ident = np.eye(128, dtype=BF)
    i64d = np.zeros((128, 64), dtype=BF)
    i64d[0:64] = np.eye(64, dtype=BF)
    i64d[64:128] = np.eye(64, dtype=BF)
    base = np.zeros((9, 128), dtype=np.float32)
    for r in range(8):
        for t in range(8):
            for h in range(16):
                base[r, h * 8 + t] = 32.0 if t == r else 0.0
    maskk = np.tile(base, (1, 64)).astype(BF)
    maskq = np.tile(base, (1, 64)).astype(BF)
    maskk[8, :] = np.float32(-32.0)
    maskq[8, :] = np.float32(32.0)
    return ident, i64d, maskq, maskk


_NC_CACHE = {}


def kernel(L0_attn_x, L0_attn_Wq, L0_attn_bq, L0_attn_Wk, L0_attn_bk, L0_attn_Wv,
           L0_attn_bv, L0_ffn_U1, L0_ffn_V1, L0_ffn_b1, L0_ffn_U2, L0_ffn_V2,
           L0_ffn_b2, L0_c, L0_wmod):
    x = np.ascontiguousarray(np.asarray(L0_attn_x, dtype=np.float32))
    B, S, Dm = x.shape
    half = S // 2
    ident, i64d, maskq, maskk = host_consts()
    shared = {
        "wq": np.ascontiguousarray(L0_attn_Wq, dtype=np.float32),
        "wk": np.ascontiguousarray(L0_attn_Wk, dtype=np.float32),
        "wv": np.ascontiguousarray(L0_attn_Wv, dtype=np.float32),
        "bq": np.asarray(L0_attn_bq, dtype=np.float32).reshape(1, Dm),
        "bk": np.asarray(L0_attn_bk, dtype=np.float32).reshape(1, Dm),
        "bv": np.asarray(L0_attn_bv, dtype=np.float32).reshape(1, Dm),
        "u1": np.ascontiguousarray(L0_ffn_U1, dtype=np.float32),
        "v1": np.ascontiguousarray(L0_ffn_V1, dtype=np.float32),
        "b1": np.asarray(L0_ffn_b1, dtype=np.float32).reshape(1, DFF),
        "u2": np.ascontiguousarray(L0_ffn_U2, dtype=np.float32),
        "v2": np.ascontiguousarray(L0_ffn_V2, dtype=np.float32),
        "b2": np.asarray(L0_ffn_b2, dtype=np.float32).reshape(1, Dm),
        "wmod": np.ascontiguousarray(L0_wmod, dtype=np.float32),
        "ident": ident, "i64d": i64d, "maskq": maskq, "maskk": maskk,
    }
    cvec = np.asarray(L0_c, dtype=np.float32)
    n_tiles = (half + T - 1) // T
    if n_tiles not in _NC_CACHE:
        _NC_CACHE[n_tiles] = build_core_program(n_tiles)
    nc = _NC_CACHE[n_tiles]
    in_maps = []
    for core in range(8):
        b = core // 2
        h0 = (core % 2) * half
        m = dict(shared)
        m["x"] = np.ascontiguousarray(x[b, h0:h0 + half])
        m["cvec"] = np.ascontiguousarray(cvec[b:b + 1])
        in_maps.append(m)
    res = run_bass_kernel_spmd(nc, in_maps, core_ids=list(range(8)))
    out = np.empty((B, S, Dm), dtype=np.float32)
    for core in range(8):
        b = core // 2
        h0 = (core % 2) * half
        out[b, h0:h0 + half] = res.results[core]["y"]
    return out


# version 05
# speedup vs baseline: 1.0434x; 1.0155x over previous
import numpy as np
import ml_dtypes
import concourse.bass as bass
import concourse.mybir as mybir
from concourse.bass_utils import run_bass_kernel_spmd

F32 = mybir.dt.float32
BF16 = mybir.dt.bfloat16
AF = mybir.ActivationFunctionType
ALU = mybir.AluOpType
BF = ml_dtypes.bfloat16

D = 1024
NH = 16
DK = 64
DFF = 4096
RANK = 256
T = 512
NSUB = T // 128
NG = T // 8
EPS = 1e-5

ENGS = ("sync", "gpsimd", "tensor", "vector", "scalar")


class Prog:
    def __init__(self):
        self.ops = {e: [] for e in ENGS}
        self.ctr = {}
        self.seq = 0
        self.marks = {}

    def mark(self, name):
        self.marks[name] = self.seq

    def c(self, sem):
        return self.ctr.get(sem, 0)

    def truncate(self, limit_seq):
        self.ctr = {}
        for e in ENGS:
            kept = [op for op in self.ops[e] if op[3] <= limit_seq]
            self.ops[e] = kept
            for fn, waits, tok, sq in kept:
                if tok is not None:
                    self.ctr[tok[0]] = self.ctr.get(tok[0], 0) + tok[1]

    def emit(self, eng, fn, waits=(), inc=None, inc_k=1):
        tok = None
        if inc is not None:
            self.ctr[inc] = self.ctr.get(inc, 0) + inc_k
            tok = (inc, inc_k)
        self.seq += 1
        self.ops[eng].append((fn, [w for w in waits if w[1] > 0], tok, self.seq))

    def dma(self, eng, out_ap, in_ap, waits=(), inc="dma", slow=False):
        self.emit(eng, lambda e, o=out_ap, i=in_ap, sl=slow: e.dma_start(
            o, i, allow_slow_non_contiguous=sl),
            waits=waits, inc=inc, inc_k=16)

    def drain(self, eng):
        self.ops[eng].append((lambda e: e.drain(), [], None, self.seq))

    def wall(self, *sems):
        return [(s, self.c(s)) for s in sems]

    def realize(self, block, sems):
        for eng_name in ENGS:
            seq = self.ops[eng_name]
            if not seq:
                continue
            def mk(seq=seq):
                def body(eng):
                    for fn, waits, tok, _sq in seq:
                        for s, v in waits:
                            eng.wait_ge(sems[s], v)
                        instr = fn(eng)
                        if tok is not None:
                            instr.then_inc(sems[tok[0]], tok[1])
                return body
            getattr(block, eng_name)(mk())


def ap(t, offset, dims):
    if isinstance(t, bass.AP):
        t = t.tensor
    return bass.AP(t, offset, [list(d) for d in dims])


def build_core_program(n_tiles, debug=(), phase=None, skip=()):
    nc = bass.Bass("TRN2", target_bir_lowering=False)
    NTOK = n_tiles * T

    x_h = nc.dram_tensor("x", [NTOK, D], F32, kind="ExternalInput").ap()
    y_h = nc.dram_tensor("y", [NTOK, D], F32, kind="ExternalOutput").ap()
    wq_h = nc.dram_tensor("wq", [D, D], F32, kind="ExternalInput").ap()
    wk_h = nc.dram_tensor("wk", [D, D], F32, kind="ExternalInput").ap()
    wv_h = nc.dram_tensor("wv", [D, D], F32, kind="ExternalInput").ap()
    bq_h = nc.dram_tensor("bq", [1, D], F32, kind="ExternalInput").ap()
    bk_h = nc.dram_tensor("bk", [1, D], F32, kind="ExternalInput").ap()
    bv_h = nc.dram_tensor("bv", [1, D], F32, kind="ExternalInput").ap()
    u1_h = nc.dram_tensor("u1", [D, RANK], F32, kind="ExternalInput").ap()
    v1_h = nc.dram_tensor("v1", [RANK, DFF], F32, kind="ExternalInput").ap()
    b1_h = nc.dram_tensor("b1", [1, DFF], F32, kind="ExternalInput").ap()
    u2_h = nc.dram_tensor("u2", [DFF, RANK], F32, kind="ExternalInput").ap()
    v2_h = nc.dram_tensor("v2", [RANK, D], F32, kind="ExternalInput").ap()
    b2_h = nc.dram_tensor("b2", [1, D], F32, kind="ExternalInput").ap()
    c_h = nc.dram_tensor("cvec", [1, D], F32, kind="ExternalInput").ap()
    wmod_h = nc.dram_tensor("wmod", [D, 6 * D], F32, kind="ExternalInput").ap()
    ident_h = nc.dram_tensor("ident", [128, 128], BF16, kind="ExternalInput").ap()
    i64d_h = nc.dram_tensor("i64d", [128, 64], BF16, kind="ExternalInput").ap()
    maskq_h = nc.dram_tensor("maskq", [9, 8192], BF16, kind="ExternalInput").ap()
    maskk_h = nc.dram_tensor("maskk", [9, 8192], BF16, kind="ExternalInput").ap()
    dbg = {}
    for name, shape, dt in debug:
        dbg[name] = nc.dram_tensor(name, list(shape), dt, kind="ExternalOutput").ap()

    p = Prog()
    SEMS = ["dma", "cast", "tmm", "vec", "sca", "xin", "xout", "shuf"]

    from contextlib import ExitStack
    with ExitStack() as ctx:
        wq_sb = ctx.enter_context(nc.sbuf_tensor("wq_sb", [128, 8192], BF16))
        wk_sb = ctx.enter_context(nc.sbuf_tensor("wk_sb", [128, 8192], BF16))
        wv_sb = ctx.enter_context(nc.sbuf_tensor("wv_sb", [128, 8192], BF16))
        u1_sb = ctx.enter_context(nc.sbuf_tensor("u1_sb", [128, 2048], BF16))
        v1_sb = ctx.enter_context(nc.sbuf_tensor("v1_sb", [128, 8192], BF16))
        u2_sb = ctx.enter_context(nc.sbuf_tensor("u2_sb", [128, 8192], BF16))
        v2_sb = ctx.enter_context(nc.sbuf_tensor("v2_sb", [128, 2048], BF16))
        ident = ctx.enter_context(nc.sbuf_tensor("ident_sb", [128, 128], BF16))
        i64d = ctx.enter_context(nc.sbuf_tensor("i64d_sb", [128, 64], BF16))
        qop = ctx.enter_context(nc.sbuf_tensor("qop", [73, 8192], BF16))
        kop = ctx.enter_context(nc.sbuf_tensor("kop", [73, 8192], BF16))
        vop = ctx.enter_context(nc.sbuf_tensor("vop", [128, 4160], BF16))
        bcols = ctx.enter_context(nc.sbuf_tensor("bcols", [128, 80], F32))
        modc = ctx.enter_context(nc.sbuf_tensor("modc", [128, 56], F32))
        rowbuf = ctx.enter_context(nc.sbuf_tensor("rowbuf", [1, 512], F32))
        rowbf = ctx.enter_context(nc.sbuf_tensor("rowbf", [1, 1024], BF16))
        scolb = ctx.enter_context(nc.sbuf_tensor("scolb", [128, 24], BF16))
        onescal = ctx.enter_context(nc.sbuf_tensor("onescal", [1, 8], BF16))
        small = ctx.enter_context(nc.sbuf_tensor("small", [128, 64], F32))
        x_t = ctx.enter_context(nc.sbuf_tensor("x_t", [128, 4096], F32))
        x2_t = ctx.enter_context(nc.sbuf_tensor("x2_t", [128, 4096], F32))
        arena = ctx.enter_context(nc.sbuf_tensor("arena", [128, 4096], BF16))
        xhatT = ctx.enter_context(nc.sbuf_tensor("xhatT", [128, 4096], BF16))
        qtmp = ctx.enter_context(nc.sbuf_tensor("qtmp", [128, 4096], BF16))
        oddscr = ctx.enter_context(nc.sbuf_tensor("oddscr", [64, 1024], BF16))
        vint = ctx.enter_context(nc.sbuf_tensor("vint", [128, 4096], BF16))
        eop = ctx.enter_context(nc.sbuf_tensor("eop", [128, 1024], BF16))
        den = ctx.enter_context(nc.sbuf_tensor("den", [128, 16], F32))
        ptsb = ctx.enter_context(nc.sbuf_tensor("ptsb", [128, 1024], BF16))
        htring = ctx.enter_context(nc.sbuf_tensor("htring", [128, 1024], BF16))
        q2sb = ctx.enter_context(nc.sbuf_tensor("q2sb", [128, 1024], BF16))
        pb0 = ctx.enter_context(nc.psum_tensor("pb0", [128, 512], F32))
        pb1 = ctx.enter_context(nc.psum_tensor("pb1", [128, 512], F32))
        pb2 = ctx.enter_context(nc.psum_tensor("pb2", [128, 512], F32))
        pb3 = ctx.enter_context(nc.psum_tensor("pb3", [128, 512], F32))
        pb4 = ctx.enter_context(nc.psum_tensor("pb4", [128, 512], F32))
        pb5 = ctx.enter_context(nc.psum_tensor("pb5", [128, 512], F32))
        pb6 = ctx.enter_context(nc.psum_tensor("pb6", [128, 512], F32))
        pb7 = ctx.enter_context(nc.psum_tensor("pb7", [128, 512], F32))
        s_dma = ctx.enter_context(nc.semaphore("s_dma"))
        s_cast = ctx.enter_context(nc.semaphore("s_cast"))
        s_tmm = ctx.enter_context(nc.semaphore("s_tmm"))
        s_vec = ctx.enter_context(nc.semaphore("s_vec"))
        s_sca = ctx.enter_context(nc.semaphore("s_sca"))
        s_xin = ctx.enter_context(nc.semaphore("s_xin"))
        s_xout = ctx.enter_context(nc.semaphore("s_xout"))
        s_shuf = ctx.enter_context(nc.semaphore("s_shuf"))
        sems = {"dma": s_dma, "cast": s_cast, "tmm": s_tmm, "vec": s_vec,
                "sca": s_sca, "xin": s_xin, "xout": s_xout, "shuf": s_shuf}
        PB = [pb0, pb1, pb2, pb3, pb4, pb5, pb6, pb7]

        p.dma("sync", ident[:], ident_h)
        p.dma("sync", i64d[:], i64d_h)
        p.dma("sync", qop[64:73, :], maskq_h)
        p.dma("sync", kop[64:73, :], maskk_h)
        p.dma("sync", rowbuf[0:1, 0:512], ap(c_h, 0, [[1, 1], [1, 512]]))
        p.dma("sync", ap(bcols, 32, [[80, 128], [1, 32]]),
              ap(b1_h, 0, [[1, 128], [128, 32]]), slow=True)
        p.dma("sync", ap(bcols, 64, [[80, 128], [1, 8]]),
              ap(b2_h, 0, [[1, 128], [128, 8]]), slow=True)
        p.dma("sync", ap(small, 0, [[64, 128], [1, 8]]),
              ap(bq_h, 0, [[1, 128], [128, 8]]), slow=True)
        p.dma("sync", ap(small, 8, [[64, 128], [1, 8]]),
              ap(bk_h, 0, [[1, 128], [128, 8]]), slow=True)
        p.dma("sync", ap(small, 16, [[64, 128], [1, 8]]),
              ap(bv_h, 0, [[1, 128], [128, 8]]), slow=True)
        consts_done = p.c("dma")

        for w_sb, w_hh in ((wq_sb, wq_h), (wk_sb, wk_h), (wv_sb, wv_h)):
            p.dma("gpsimd", w_sb[:],
                  ap(w_hh, 0, [[1024, 128], [128 * 1024, 8], [1, 1024]]), inc="cast")
        p.dma("gpsimd", u1_sb[:],
              ap(u1_h, 0, [[256, 128], [128 * 256, 8], [1, 256]]), inc="cast")
        p.dma("gpsimd", v1_sb[:],
              ap(v1_h, 0, [[4096, 128], [128 * 4096, 2], [1, 4096]]), inc="cast")
        p.dma("gpsimd", u2_sb[:],
              ap(u2_h, 0, [[256, 128], [128 * 256, 32], [1, 256]]), inc="cast")
        p.dma("gpsimd", v2_sb[:],
              ap(v2_h, 0, [[1024, 128], [128 * 1024, 2], [1, 1024]]), inc="cast")
        w_done = p.c("cast")

        p.emit("vector", lambda e: e.memset(onescal[:], 1.0), inc="vec")
        p.emit("vector", lambda e: e.memset(
            ap(vop, 64, [[4160, 128], [65, 64]]), 1.0), inc="vec")
        v_ones = p.c("vec")

        p.emit("scalar", lambda e: e.activation(rowbf[0:1, 0:512], rowbuf[0:1, 0:512],
                                                AF.Silu),
               waits=[("dma", consts_done)], inc="sca")
        s_h1 = p.c("sca")
        p.dma("sync", rowbuf[0:1, 0:512], ap(c_h, 512, [[1, 1], [1, 512]]),
              waits=[("sca", s_h1)])
        c2_done = p.c("dma")
        p.emit("scalar", lambda e: e.activation(rowbf[0:1, 512:1024], rowbuf[0:1, 0:512],
                                                AF.Silu),
               waits=[("dma", c2_done)], inc="sca")
        s_silu = p.c("sca")
        for k in range(8):
            p.emit("tensor", lambda e, k=k: e.matmul(
                pb6[:, k:k + 1], rowbf[0:1, k * 128:(k + 1) * 128], onescal[0:1, 0:1],
                start=True, stop=True),
                waits=(p.wall("sca", "vec") if k == 0 else []), inc="tmm")
        t_scol = p.c("tmm")
        p.emit("vector", lambda e: e.tensor_copy(scolb[:, 0:8], pb6[:, 0:8]),
               waits=[("tmm", t_scol)], inc="vec")
        v_scolb = p.c("vec")

        for n in range(12):
            ring = vint if n % 2 == 0 else qtmp
            p.dma("gpsimd", ap(ring, 0, [[4096, 128], [512, 8], [1, 512]]),
                  ap(wmod_h, n * 512, [[6144, 128], [128 * 6144, 8], [1, 512]]),
                  waits=(p.wall("tmm") if n >= 2 else []),
                  inc="cast")
            cd = p.c("cast")
            for k in range(8):
                p.emit("tensor", lambda e, k=k, ring=ring: e.matmul(
                    pb7[0:1, 0:512], scolb[:, k:k + 1],
                    ring[:, k * 512:(k + 1) * 512],
                    start=(k == 0), stop=(k == 7)),
                    waits=(([("cast", cd), ("vec", v_scolb)] + p.wall("sca"))
                           if k == 0 else []),
                    inc="tmm")
            t_mod = p.c("tmm")
            p.emit("scalar", lambda e: e.copy(rowbuf[0:1, 0:512], pb7[0:1, 0:512]),
                   waits=[("tmm", t_mod)], inc="sca")
            p.drain("scalar")
            p.emit("scalar", lambda e: e.copy(rowbf[0:1, 0:512], rowbuf[0:1, 0:512]),
                   inc="sca")
            s_rb = p.c("sca")
            for j in range(4):
                col = n * 4 + j
                p.emit("tensor", lambda e, j=j: e.matmul(
                    pb6[:, 8:9], rowbf[0:1, j * 128:(j + 1) * 128],
                    onescal[0:1, 0:1], start=True, stop=True),
                    waits=[("sca", s_rb)] + p.wall("vec"), inc="tmm")
                t_c = p.c("tmm")
                p.emit("vector", lambda e, col=col: e.tensor_copy(
                    modc[:, col:col + 1], pb6[:, 8:9]),
                    waits=[("tmm", t_c)], inc="vec")
        v_modc = p.c("vec")

        def emit_gate(c0, sg0, mg0, prev_v):
            p.emit("scalar", lambda e, c0=c0, sg0=sg0: e.sign(
                small[:, sg0:sg0 + 8], modc[:, c0:c0 + 8]),
                waits=[("vec", v_modc), ("vec", prev_v)], inc="sca")
            p.emit("scalar", lambda e, c0=c0, mg0=mg0: e.activation(
                small[:, mg0:mg0 + 8], modc[:, c0:c0 + 8], AF.Abs), inc="sca")
            s_g = p.c("sca")
            p.emit("vector", lambda e, mg0=mg0: e.tensor_scalar_max(
                small[:, mg0:mg0 + 8], small[:, mg0:mg0 + 8], 0.25),
                waits=[("sca", s_g), ("vec", v_modc)], inc="vec")
            p.drain("vector")
            p.emit("vector", lambda e, c0=c0, sg0=sg0, mg0=mg0: e.tensor_mul(
                modc[:, c0:c0 + 8], small[:, sg0:sg0 + 8], small[:, mg0:mg0 + 8]),
                inc="vec")
            return p.c("vec")
        print("PLAN: before gates seq=", p.seq)
        g1 = emit_gate(16, 32, 40, 0)
        emit_gate(40, 56, 32, g1)
        print("PLAN: after gates seq=", p.seq)
        p.emit("vector", lambda e: e.tensor_scalar_add(modc[:, 48:56], modc[:, 8:16], 1.0),
               inc="vec")
        p.emit("vector", lambda e: e.tensor_scalar_add(small[:, 48:56], modc[:, 32:40], 1.0),
               inc="vec")
        p.emit("vector", lambda e: e.tensor_copy(scolb[:, 8:16], modc[:, 0:8]), inc="vec")
        p.emit("vector", lambda e: e.tensor_copy(scolb[:, 16:24], modc[:, 24:32]), inc="vec")
        v_gates = p.c("vec")

        def emit_bprime(w_sb, scol_off, raw_off, dst_off, width, wstride):
            nch = max(1, width // 512)
            for n in range(nch):
                for k in range(8):
                    p.emit("tensor", lambda e, n=n, k=k, w_sb=w_sb, scol_off=scol_off,
                           wstride=wstride: e.matmul(
                        pb7[0:1, 0:512] if wstride >= 512 else pb7[0:1, 0:wstride],
                        scolb[:, scol_off + k:scol_off + k + 1],
                        w_sb[:, k * wstride + n * 512:
                             k * wstride + n * 512 + min(512, wstride)],
                        start=(k == 0), stop=(k == 7)),
                        waits=(([("vec", v_gates)] + p.wall("sca")) if (n == 0 and k == 0)
                               else []),
                        inc="tmm")
                t_b = p.c("tmm")
                p.emit("scalar", lambda e, wstride=wstride: e.copy(
                    rowbf[0:1, 0:min(512, wstride)], pb7[0:1, 0:min(512, wstride)]),
                    waits=[("tmm", t_b)], inc="sca")
                s_b = p.c("sca")
                ncols = min(512, wstride) // 128
                for j in range(ncols):
                    cc = n * 4 + j
                    p.emit("tensor", lambda e, j=j: e.matmul(
                        pb6[:, 8:9], rowbf[0:1, j * 128:(j + 1) * 128],
                        onescal[0:1, 0:1], start=True, stop=True),
                        waits=[("sca", s_b)] + p.wall("vec"), inc="tmm")
                    t_c = p.c("tmm")
                    p.emit("vector", lambda e, cc=cc, dst_off=dst_off: e.tensor_copy(
                        bcols[:, dst_off + cc:dst_off + cc + 1], pb6[:, 8:9]),
                        waits=[("tmm", t_c)], inc="vec")
            if raw_off is not None:
                p.drain("vector")
                p.emit("vector", lambda e, dst_off=dst_off, raw_off=raw_off: e.tensor_add(
                    bcols[:, dst_off:dst_off + width // 128],
                    bcols[:, dst_off:dst_off + width // 128],
                    small[:, raw_off:raw_off + width // 128]),
                    waits=[("dma", consts_done)], inc="vec")

        emit_bprime(wq_sb, 8, 0, 0, 1024, 1024)
        emit_bprime(wk_sb, 8, 8, 8, 1024, 1024)
        emit_bprime(wv_sb, 8, 16, 16, 1024, 1024)
        emit_bprime(u1_sb, 16, None, 24, 256, 256)
        v_bcols = p.c("vec")

        for w_sb in (wq_sb, wk_sb, wv_sb):
            for k in range(8):
                p.emit("vector", lambda e, w_sb=w_sb, k=k: e.tensor_scalar_mul(
                    w_sb[:, k * 1024:(k + 1) * 1024], w_sb[:, k * 1024:(k + 1) * 1024],
                    modc[:, 48 + k:49 + k]),
                    waits=([("vec", v_bcols)] if k == 0 and w_sb is wq_sb else []),
                    inc="vec")
        for k in range(8):
            p.emit("vector", lambda e, k=k: e.tensor_scalar_mul(
                u1_sb[:, k * 256:(k + 1) * 256], u1_sb[:, k * 256:(k + 1) * 256],
                small[:, 48 + k:49 + k]), inc="vec")
        p.emit("vector", lambda e: e.memset(small[:, 32:33], EPS), inc="vec")
        v_prep = p.c("vec")
        t_prep = p.c("tmm")
        s_prep = p.c("sca")
        p.mark("prep")

        x_free = 0
        x2_free_out = 0
        fin_v_prev = 0
        def emit_ln_chain(sum_off, sq1_off, sq2_off, two_sq):
            if two_sq:
                p.emit("vector", lambda e: e.tensor_add(
                    small[:, sq1_off:sq1_off + 4], small[:, sq1_off:sq1_off + 4],
                    small[:, sq2_off:sq2_off + 4]),
                    waits=[("sca", p.c("sca")), ("vec", p.c("vec"))], inc="vec")
            p.drain("vector")
            p.emit("vector", lambda e: e.tensor_scalar_mul(
                small[:, sum_off:sum_off + 4], small[:, sum_off:sum_off + 4],
                1.0 / 1024), inc="vec")
            p.drain("vector")
            p.emit("vector", lambda e: e.tensor_mul(
                small[:, 28:32], small[:, sum_off:sum_off + 4],
                small[:, sum_off:sum_off + 4]), inc="vec")
            p.emit("vector", lambda e: e.tensor_scalar_mul(
                small[:, sq1_off:sq1_off + 4], small[:, sq1_off:sq1_off + 4],
                1.0 / 1024), inc="vec")
            p.drain("vector")
            p.emit("vector", lambda e: e.tensor_sub(
                small[:, sq1_off:sq1_off + 4], small[:, sq1_off:sq1_off + 4],
                small[:, 28:32]), inc="vec")
            v_var = p.c("vec")
            p.emit("scalar", lambda e: e.activation(
                small[:, 28:32], small[:, sq1_off:sq1_off + 4], AF.Ln,
                bias=small[:, 32:33]),
                waits=[("vec", v_var)], inc="sca")
            p.drain("scalar")
            p.emit("scalar", lambda e: e.activation(
                small[:, 0:4], small[:, 28:32], AF.Exp, scale=-0.5), inc="sca")
            s_r = p.c("sca")
            p.emit("vector", lambda e: e.tensor_mul(
                small[:, 8:12], small[:, sum_off:sum_off + 4], small[:, 0:4]),
                waits=[("sca", s_r)], inc="vec")
            p.drain("vector")
            p.emit("vector", lambda e: e.tensor_scalar_mul(
                small[:, 8:12], small[:, 8:12], -1.0), inc="vec")
            return p.c("vec")

        def emit_ln1(tile, extra_w):
            p.dma("sync", x_t[:],
                  ap(x_h, tile * T * D, [[D, 128], [128 * D, NSUB], [1, D]]),
                  waits=list(extra_w), inc="xin")
            xin_done = p.c("xin")
            for s in range(NSUB):
                p.emit("vector", lambda e, s=s: e.reduce_sum(
                    small[:, 16 + s:17 + s], x_t[:, s * 1024:(s + 1) * 1024],
                    axis=mybir.AxisListType.X),
                    waits=([("xin", xin_done)] + (list(extra_w) if s == 0 else []))
                          if s == 0 else [],
                    inc="vec")
                p.emit("scalar", lambda e, s=s: e.activation(
                    qtmp[:, 0:512], x_t[:, s * 1024:s * 1024 + 512], AF.Square,
                    accum_out=small[:, 20 + s:21 + s]),
                    waits=(([("xin", xin_done)] + list(extra_w)) if s == 0 else []),
                    inc="sca")
                p.emit("scalar", lambda e, s=s: e.activation(
                    qtmp[:, 512:1024], x_t[:, s * 1024 + 512:(s + 1) * 1024], AF.Square,
                    accum_out=small[:, 24 + s:25 + s]),
                    inc="sca")
            v_nmr = emit_ln_chain(16, 20, 24, True)
            out = []
            for s in range(NSUB):
                p.emit("scalar", lambda e, s=s: e.activation(
                    arena[:, s * 1024:(s + 1) * 1024], x_t[:, s * 1024:(s + 1) * 1024],
                    AF.Identity, bias=small[:, 8 + s:9 + s], scale=small[:, s:s + 1]),
                    waits=([("vec", v_nmr)] if s == 0 else []), inc="sca")
                out.append(p.c("sca"))
            return out

        s_xhat_s = None
        for tile in range(n_tiles):
            if tile == 0:
                s_xhat_s = emit_ln1(0, [("vec", v_prep), ("tmm", t_prep)])
            p.mark("ln1")

            def emit_transposes(src_sb, s_ready, extra_first):
                ev = []
                for s in range(NSUB):
                    b = (s % 2) * 2
                    for fc in range(8):
                        waits = []
                        if fc == 0:
                            waits = [("sca", s_ready[s])]
                        if s == 0 and fc == 0:
                            waits += extra_first
                        if s >= 2 and fc == 0:
                            waits += list(ev[s - 2])
                        p.emit("tensor", lambda e, s=s, fc=fc, b=b, src_sb=src_sb: e.matmul(
                            bass.AP(PB[b + fc // 4], (fc % 4) * 128, [[512, 128], [1, 128]]),
                            src_sb[:, s * 1024 + fc * 128: s * 1024 + (fc + 1) * 128],
                            ident[:], start=True, stop=True),
                            waits=waits, inc="tmm")
                    t_tr = p.c("tmm")
                    if s % 2 == 0:
                        p.emit("vector", lambda e, s=s, b=b: e.tensor_copy(
                            ap(xhatT, s * 128, [[4096, 128], [512, 4], [1, 128]]),
                            ap(PB[b], 0, [[512, 128], [128, 4], [1, 128]])),
                            waits=[("tmm", t_tr)], inc="vec")
                        p.emit("vector", lambda e, s=s, b=b: e.tensor_copy(
                            ap(xhatT, 2048 + s * 128, [[4096, 128], [512, 4], [1, 128]]),
                            ap(PB[b + 1], 0, [[512, 128], [128, 4], [1, 128]])),
                            inc="vec")
                        ev.append([("vec", p.c("vec"))])
                    else:
                        p.emit("scalar", lambda e, s=s, b=b: e.copy(
                            ap(xhatT, s * 128, [[4096, 128], [512, 4], [1, 128]]),
                            ap(PB[b], 0, [[512, 128], [128, 4], [1, 128]])),
                            waits=[("tmm", t_tr)], inc="sca")
                        p.emit("scalar", lambda e, s=s, b=b: e.copy(
                            ap(xhatT, 2048 + s * 128, [[4096, 128], [512, 4], [1, 128]]),
                            ap(PB[b + 1], 0, [[512, 128], [128, 4], [1, 128]])),
                            inc="sca")
                        ev.append([("sca", p.c("sca"))])
                return p.c("vec"), p.c("sca")

            xT_v, xT_s = emit_transposes(arena, s_xhat_s,
                                         [("vec", fin_v_prev)] if tile > 0 else [])
            p.mark("xT")

            def proj_qk(w_sb, op, bcol_off, first_waits, stage, stage_w=()):
                ev = []
                for cc in range(8):
                    bank = cc % 4
                    for kc in range(8):
                        waits = []
                        if cc == 0 and kc == 0:
                            waits = list(first_waits)
                        if cc >= 4 and kc == 0:
                            waits = list(ev[cc - 4])
                        p.emit("tensor", lambda e, cc=cc, kc=kc, bank=bank, w_sb=w_sb: e.matmul(
                            PB[bank][:, 0:512],
                            w_sb[:, kc * 1024 + cc * 128: kc * 1024 + (cc + 1) * 128],
                            xhatT[:, kc * 512:(kc + 1) * 512],
                            start=(kc == 0), stop=(kc == 7)),
                            waits=waits, inc="tmm")
                    t_p = p.c("tmm")
                    even_dst = ap(op, 2 * cc * 8, [[8192, 64], [128, 64], [1, 8]])
                    even_src = bass.AP(PB[bank], 0, [[512, 64], [8, 64], [1, 8]])
                    if cc % 2 == 0:
                        p.emit("vector", lambda e, d=even_dst, s_=even_src, cc=cc:
                               e.tensor_scalar_add(
                            d, s_, bcols[0:64, bcol_off + cc:bcol_off + cc + 1]),
                            waits=[("tmm", t_p)], inc="vec")
                        p.emit("vector", lambda e, cc=cc, bank=bank, stage=stage:
                               e.tensor_scalar_add(
                            stage[64:128, cc * 512:(cc + 1) * 512],
                            PB[bank][64:128, 0:512],
                            bcols[64:128, bcol_off + cc:bcol_off + cc + 1]),
                            waits=(list(stage_w) if cc == 0 else []), inc="vec")
                        ev.append([("vec", p.c("vec"))])
                    else:
                        p.emit("scalar", lambda e, d=even_dst, s_=even_src, cc=cc:
                               e.activation(
                            d, s_, AF.Identity,
                            bias=bcols[0:64, bcol_off + cc:bcol_off + cc + 1]),
                            waits=[("tmm", t_p)], inc="sca")
                        p.emit("scalar", lambda e, cc=cc, bank=bank, stage=stage: e.activation(
                            stage[64:128, cc * 512:(cc + 1) * 512],
                            PB[bank][64:128, 0:512],
                            AF.Identity,
                            bias=bcols[64:128, bcol_off + cc:bcol_off + cc + 1]),
                            waits=(list(stage_w) if cc == 1 else []), inc="sca")
                        ev.append([("sca", p.c("sca"))])
                return p.c("vec"), p.c("sca")

            ev_v = []
            for cc in range(8):
                bank = cc % 4
                for kc in range(8):
                    waits = []
                    if cc == 0 and kc == 0:
                        waits = [("vec", xT_v), ("sca", xT_s)] + p.wall("shuf")
                    if cc >= 4 and kc == 0:
                        waits = list(ev_v[cc - 4])
                    p.emit("tensor", lambda e, cc=cc, kc=kc, bank=bank: e.matmul(
                        PB[bank][:, 0:512],
                        wv_sb[:, kc * 1024 + cc * 128: kc * 1024 + (cc + 1) * 128],
                        xhatT[:, kc * 512:(kc + 1) * 512],
                        start=(kc == 0), stop=(kc == 7)),
                        waits=waits, inc="tmm")
                t_v = p.c("tmm")
                if cc % 2 == 0:
                    p.emit("vector", lambda e, cc=cc, bank=bank: e.tensor_scalar_add(
                        qtmp[:, cc * 512:(cc + 1) * 512], PB[bank][:, 0:512],
                        bcols[:, 16 + cc:17 + cc]),
                        waits=[("tmm", t_v)], inc="vec")
                    ev_v.append([("vec", p.c("vec"))])
                else:
                    p.emit("scalar", lambda e, cc=cc, bank=bank: e.activation(
                        qtmp[:, cc * 512:(cc + 1) * 512], PB[bank][:, 0:512],
                        AF.Identity, bias=bcols[:, 16 + cc:17 + cc]),
                        waits=[("tmm", t_v)], inc="sca")
                    ev_v.append([("sca", p.c("sca"))])
            vfm_v, vfm_s = p.c("vec"), p.c("sca")
            p.mark("qkv")

            vint_w = [("vec", vfm_v), ("sca", vfm_s)]
            first_v = True
            first_d = True
            dq = 0
            for j in range(16):
                src_band = (j % 2) * 64
                for half in range(2):
                    dst_band = half * 64
                    src_off = src_band * 4096 + (j // 2) * 512 + half * 256
                    dst_off = dst_band * 4096 + j * 8
                    if src_band == dst_band:
                        p.emit("vector", lambda e, src_off=src_off, dst_off=dst_off:
                               e.tensor_copy(
                            ap(vint, dst_off, [[4096, 64], [128, 32], [1, 8]]),
                            ap(qtmp, src_off, [[4096, 64], [8, 32], [1, 8]])),
                            waits=(vint_w if first_v else []), inc="vec")
                        first_v = False
                    else:
                        qeng = "sync" if dq % 2 == 0 else "scalar"
                        p.dma(qeng,
                              ap(vint, dst_off, [[4096, 64], [128, 32], [1, 8]]),
                              ap(qtmp, src_off, [[4096, 64], [8, 32], [1, 8]]),
                              waits=(vint_w if first_d else []), inc="shuf")
                        first_d = False
                        dq += 1
            vint_v, vint_dma = p.c("vec"), p.c("shuf")

            vq, sq_ = proj_qk(wq_sb, qop, 0,
                              [("vec", vfm_v), ("sca", vfm_s)], qtmp,
                              stage_w=[("vec", vint_v), ("shuf", vint_dma)])
            for cc in range(8):
                p.dma("sync",
                      ap(qop, (2 * cc + 1) * 8, [[8192, 64], [128, 64], [1, 8]]),
                      ap(qtmp, 64 * 4096 + cc * 512, [[4096, 64], [8, 64], [1, 8]]),
                      waits=([("sca", sq_), ("vec", vq)] if cc == 0 else []), inc="shuf")
            q_odd = p.c("shuf")
            vk, sk_ = proj_qk(wk_sb, kop, 8, [("vec", vq), ("sca", sq_)], arena)
            for cc in range(8):
                p.dma("scalar",
                      ap(kop, (2 * cc + 1) * 8, [[8192, 64], [128, 64], [1, 8]]),
                      ap(arena, 64 * 4096 + cc * 512, [[4096, 64], [8, 64], [1, 8]]),
                      waits=([("sca", sk_), ("vec", vk)] if cc == 0 else []), inc="shuf")
            k_odd = p.c("shuf")

            ev_vt = []
            for gb in range(8):
                bank = 2 + gb % 2
                for gl in range(8):
                    g = gb * 8 + gl
                    half = g // 32
                    base = half * 64
                    waits = []
                    if gb == 0 and gl == 0:
                        waits = [("vec", vint_v), ("shuf", vint_dma),
                                 ("vec", vk), ("sca", sk_)]
                    if gb >= 2 and gl == 0:
                        waits = list(ev_vt[gb - 2])
                    p.emit("tensor", lambda e, g=g, gl=gl, bank=bank, base=base: e.matmul(
                        bass.AP(PB[bank], gl * 64, [[512, 128], [1, 64]]),
                        vint[base:base + 64, (g % 32) * 128:(g % 32) * 128 + 128],
                        i64d[base:base + 64, 0:64],
                        start=True, stop=True),
                        waits=waits, inc="tmm")
                t_vt = p.c("tmm")
                if gb % 2 == 0:
                    p.emit("vector", lambda e, gb=gb, bank=bank: e.tensor_copy(
                        ap(vop, gb * 8 * 65, [[4160, 128], [65, 8], [1, 64]]),
                        ap(PB[bank], 0, [[512, 128], [64, 8], [1, 64]])),
                        waits=[("tmm", t_vt)], inc="vec")
                    ev_vt.append([("vec", p.c("vec"))])
                else:
                    p.emit("scalar", lambda e, gb=gb, bank=bank: e.copy(
                        ap(vop, gb * 8 * 65, [[4160, 128], [65, 8], [1, 64]]),
                        ap(PB[bank], 0, [[512, 128], [64, 8], [1, 64]])),
                        waits=[("tmm", t_vt)], inc="sca")
                    ev_vt.append([("sca", p.c("sca"))])
            vop_v, vop_s = p.c("vec"), p.c("sca")
            p.mark("vop")

            att_w = [("shuf", q_odd), ("shuf", k_odd), ("vec", vq), ("vec", vk),
                     ("sca", sq_), ("sca", sk_), ("vec", vop_v), ("sca", vop_s)]
            exp_done = []
            attnv_done = []
            norm_done = []
            NB4 = NG // 4

            def emit_score(b):
                sbank = b % 2
                waits = (list(att_w) if b == 0 else [])
                if b >= 2:
                    waits = waits + [("sca", exp_done[b - 2])]
                for gl in range(4):
                    g = 4 * b + gl
                    p.emit("tensor", lambda e, g=g, gl=gl, sbank=sbank: e.matmul(
                        bass.AP(PB[sbank], gl * 128, [[512, 128], [1, 128]]),
                        kop[0:73, g * 128:(g + 1) * 128],
                        qop[0:73, g * 128:(g + 1) * 128],
                        start=True, stop=True),
                        waits=(waits if gl == 0 else []), inc="tmm")
                t_sc = p.c("tmm")
                ew = [("tmm", t_sc)]
                if b >= 2:
                    ew.append(("tmm", attnv_done[b - 2]))
                p.emit("scalar", lambda e, b=b, sbank=sbank: e.activation(
                    eop[:, (b % 2) * 512:(b % 2) * 512 + 512], PB[sbank][:, 0:512],
                    AF.Exp, scale=0.125),
                    waits=ew, inc="sca")
                exp_done.append(p.c("sca"))

            def emit_attnv(b):
                obank = 4 + b % 2
                aw = [("sca", exp_done[b])]
                if b >= 2:
                    aw.append(("vec", norm_done[b - 2]))
                for gl in range(4):
                    g = 4 * b + gl
                    p.emit("tensor", lambda e, g=g, gl=gl, b=b, obank=obank: e.matmul(
                        bass.AP(PB[obank], gl * 66, [[512, 128], [1, 65]]),
                        eop[:, (b % 2) * 512 + gl * 128:(b % 2) * 512 + (gl + 1) * 128],
                        vop[:, g * 65:(g + 1) * 65],
                        start=True, stop=True),
                        waits=(aw if gl == 0 else []), inc="tmm")
                attnv_done.append(p.c("tmm"))
                p.emit("vector", lambda e, b=b, obank=obank: e.tensor_copy(
                    den[:, (b % 2) * 8:(b % 2) * 8 + 4],
                    ap(PB[obank], 64, [[512, 128], [66, 4]])),
                    waits=[("tmm", attnv_done[b])], inc="vec")
                p.drain("vector")
                p.emit("vector", lambda e, b=b: e.reciprocal(
                    den[:, (b % 2) * 8 + 4:(b % 2) * 8 + 8],
                    den[:, (b % 2) * 8:(b % 2) * 8 + 4]), inc="vec")
                p.drain("vector")
                p.emit("vector", lambda e, b=b, obank=obank: e.tensor_tensor(
                    ap(qtmp, b * 256, [[4096, 128], [64, 4], [1, 64]]),
                    ap(PB[obank], 0, [[512, 128], [66, 4], [1, 64]]),
                    ap(den, (b % 2) * 8 + 4, [[16, 128], [1, 4], [0, 64]]),
                    op=ALU.mult), inc="vec")
                norm_done.append(p.c("vec"))

            emit_score(0)
            emit_score(1)
            for b in range(NB4):
                emit_attnv(b)
                if b + 2 < NB4:
                    emit_score(b + 2)
            p.mark("attcore")

            ofm_v = 0
            ofm_s = 0
            oq_dma = p.c("shuf")
            for b in range(NG // 4):
                bank = 6 + b % 2
                eng = "vector" if b % 2 == 0 else "scalar"
                bw = [("vec", norm_done[b])]
                if b >= 2:
                    bw += [(("vec", ofm_v) if b % 2 == 0 else ("sca", ofm_s))]
                for gl in range(4):
                    g = 4 * b + gl
                    p.emit("tensor", lambda e, g=g, gl=gl, bank=bank: e.matmul(
                        bass.AP(PB[bank], gl * 128, [[512, 64], [1, 128]]),
                        qtmp[:, g * 64:(g + 1) * 64],
                        ident[:], start=True, stop=True),
                        waits=(bw if gl == 0 else []), inc="tmm")
                t_oi = p.c("tmm")
                sw = [("tmm", t_oi)]
                if b % 4 in (0, 1) and b > 1:
                    sw.append(("shuf", oq_dma))
                for i in range(16):
                    if i % 2 == 0:
                        dst = ap(arena, (i // 2) * 512 + b * 32,
                                 [[4096, 64], [8, 4], [1, 8]])
                    else:
                        dst = ap(oddscr, (i // 2) * 128 + (b % 4) * 32,
                                 [[1024, 64], [8, 4], [1, 8]])
                    srcp = ap(PB[bank], i * 8, [[512, 64], [128, 4], [1, 8]])
                    if eng == "vector":
                        p.emit("vector", lambda e, d=dst, s_=srcp: e.tensor_copy(d, s_),
                               waits=(sw if i == 0 else []), inc="vec")
                    else:
                        p.emit("scalar", lambda e, d=dst, s_=srcp: e.copy(d, s_),
                               waits=(sw if i == 0 else []), inc="sca")
                if eng == "vector":
                    ofm_v = p.c("vec")
                else:
                    ofm_s = p.c("sca")
                if b % 4 == 3:
                    qb = b // 4
                    p.dma("sync",
                          ap(arena, 64 * 4096 + qb * 128, [[4096, 64], [512, 8], [1, 128]]),
                          ap(oddscr, 0, [[1024, 64], [128, 8], [1, 128]]),
                          waits=[("sca", p.c("sca")), ("vec", p.c("vec"))], inc="shuf")
                    oq_dma = p.c("shuf")
            ofm_dma = p.c("shuf")
            p.mark("attn")

            for cc in range(8):
                p.emit("vector", lambda e, cc=cc: e.tensor_scalar_mul(
                    eop[:, cc * 128:(cc + 1) * 128], ident[:], modc[:, 16 + cc:17 + cc]),
                    waits=([("tmm", attnv_done[-1])] if cc == 0 else []), inc="vec")
            dga_v = p.c("vec")
            res_v = 0
            res_v_s = []
            for s in range(NSUB):
                b2k = (s % 2) * 2
                for cc in range(8):
                    waits = []
                    if s == 0 and cc == 0:
                        waits = [("vec", ofm_v), ("sca", ofm_s), ("shuf", ofm_dma),
                                 ("vec", dga_v)]
                    if s >= 2 and cc == 0:
                        waits = [("vec", res_v)]
                    p.emit("tensor", lambda e, s=s, cc=cc, b2k=b2k: e.matmul(
                        bass.AP(PB[b2k + cc // 4], (cc % 4) * 128, [[512, 128], [1, 128]]),
                        arena[:, cc * 512 + s * 128: cc * 512 + s * 128 + 128],
                        eop[:, cc * 128:(cc + 1) * 128],
                        start=True, stop=True),
                        waits=waits, inc="tmm")
                t_ga = p.c("tmm")
                p.emit("vector", lambda e, s=s, b2k=b2k: e.scalar_tensor_tensor(
                    x2_t[:, s * 1024:s * 1024 + 512], x_t[:, s * 1024:s * 1024 + 512],
                    1.0, PB[b2k][:, 0:512], ALU.mult, ALU.add,
                    accum_out=small[:, 16 + s:17 + s]),
                    waits=[("tmm", t_ga)] +
                          ([("xout", x2_free_out)] if s == 0 and tile > 0 else []),
                    inc="vec")
                p.emit("vector", lambda e, s=s, b2k=b2k: e.scalar_tensor_tensor(
                    x2_t[:, s * 1024 + 512:(s + 1) * 1024],
                    x_t[:, s * 1024 + 512:(s + 1) * 1024],
                    1.0, PB[b2k + 1][:, 0:512], ALU.mult, ALU.add,
                    accum_out=small[:, 20 + s:21 + s]),
                    inc="vec")
                res_v = p.c("vec")
                res_v_s.append(res_v)
            x_free = res_v
            for cc in range(8):
                p.emit("vector", lambda e, cc=cc: e.tensor_scalar_mul(
                    eop[:, cc * 128:(cc + 1) * 128], ident[:], modc[:, 40 + cc:41 + cc]),
                    waits=([("tmm", t_ga)] if cc == 0 else []), inc="vec")
            dgb_v = p.c("vec")
            p.mark("res1")

            for s in range(NSUB):
                p.emit("scalar", lambda e, s=s: e.activation(
                    pb6[:, 0:512], x2_t[:, s * 1024:s * 1024 + 512], AF.Square,
                    accum_out=small[:, 24 + s:25 + s]),
                    waits=[("vec", res_v_s[s])], inc="sca")
                p.emit("scalar", lambda e, s=s: e.activation(
                    pb7[:, 0:512], x2_t[:, s * 1024 + 512:(s + 1) * 1024], AF.Square,
                    accum_out=small[:, 28 + s:29 + s]),
                    inc="sca")
            p.emit("vector", lambda e: e.tensor_add(small[:, 16:20], small[:, 16:20],
                                                    small[:, 20:24]),
                   waits=[("vec", res_v)], inc="vec")
            v_nmr2 = emit_ln_chain(16, 24, 28, True)
            s_xhat2_s = []
            for s in range(NSUB):
                p.emit("scalar", lambda e, s=s: e.activation(
                    arena[:, s * 1024:(s + 1) * 1024], x2_t[:, s * 1024:(s + 1) * 1024],
                    AF.Identity, bias=small[:, 8 + s:9 + s], scale=small[:, s:s + 1]),
                    waits=([("vec", v_nmr2)] if s == 0 else []), inc="sca")
                s_xhat2_s.append(p.c("sca"))

            xT2_v, xT2_s = emit_transposes(arena, s_xhat2_s, [])

            for c2 in range(2):
                bank = 4 + c2
                for kc in range(8):
                    p.emit("tensor", lambda e, c2=c2, kc=kc, bank=bank: e.matmul(
                        PB[bank][:, 0:512],
                        u1_sb[:, kc * 256 + c2 * 128: kc * 256 + (c2 + 1) * 128],
                        xhatT[:, kc * 512:(kc + 1) * 512],
                        start=(kc == 0), stop=(kc == 7)),
                        waits=([("vec", xT2_v), ("sca", xT2_s)]
                               if (c2 == 0 and kc == 0) else []),
                        inc="tmm")
                t_pt = p.c("tmm")
                p.emit("vector", lambda e, c2=c2, bank=bank: e.tensor_scalar_add(
                    ptsb[:, c2 * 512:(c2 + 1) * 512], PB[bank][:, 0:512],
                    bcols[:, 24 + c2:25 + c2]),
                    waits=[("tmm", t_pt)], inc="vec")
            pt_v = p.c("vec")
            p.mark("pt")

            gelu_done = []

            def emit_h(fc):
                bank = fc % 4
                for rc in range(2):
                    waits = []
                    if fc == 0 and rc == 0:
                        waits = [("vec", pt_v)]
                    if fc >= 4 and rc == 0:
                        waits = [("sca", gelu_done[fc - 4])]
                    p.emit("tensor", lambda e, fc=fc, rc=rc, bank=bank: e.matmul(
                        PB[bank][:, 0:512],
                        v1_sb[:, rc * 4096 + fc * 128: rc * 4096 + (fc + 1) * 128],
                        ptsb[:, rc * 512:(rc + 1) * 512],
                        start=(rc == 0), stop=(rc == 1)),
                        waits=waits, inc="tmm")
                t_h = p.c("tmm")
                p.emit("scalar", lambda e, fc=fc, bank=bank: e.activation(
                    htring[:, (fc % 2) * 512:(fc % 2) * 512 + 512], PB[bank][:, 0:512],
                    AF.Gelu_apprx_tanh, bias=bcols[:, 32 + fc:33 + fc]),
                    waits=[("tmm", t_h)], inc="sca")
                gelu_done.append(p.c("sca"))

            def emit_u2(fc):
                for c2 in range(2):
                    p.emit("tensor", lambda e, fc=fc, c2=c2: e.matmul(
                        PB[6 + c2][:, 0:512],
                        u2_sb[:, fc * 256 + c2 * 128: fc * 256 + (c2 + 1) * 128],
                        htring[:, (fc % 2) * 512:(fc % 2) * 512 + 512],
                        start=(fc == 0), stop=(fc == 31), skip_group_check=True),
                        waits=([("sca", gelu_done[fc])] if c2 == 0 else []), inc="tmm")

            emit_h(0)
            emit_h(1)
            for fc in range(32):
                emit_u2(fc)
                if fc + 2 < 32:
                    emit_h(fc + 2)
            t_q2 = p.c("tmm")
            for c2 in range(2):
                p.emit("vector", lambda e, c2=c2: e.tensor_copy(
                    q2sb[:, c2 * 512:(c2 + 1) * 512], PB[6 + c2][:, 0:512]),
                    waits=[("tmm", t_q2)], inc="vec")
            q2_v = p.c("vec")
            p.mark("q2")

            fprime_tok = []
            for cc in range(8):
                bank = 4 + cc % 2
                for rc in range(2):
                    waits = []
                    if cc == 0 and rc == 0:
                        waits = [("vec", q2_v)]
                    if cc >= 2 and rc == 0:
                        waits = list(fprime_tok[cc - 2])
                    p.emit("tensor", lambda e, cc=cc, rc=rc, bank=bank: e.matmul(
                        PB[bank][:, 0:512],
                        v2_sb[:, rc * 1024 + cc * 128: rc * 1024 + (cc + 1) * 128],
                        q2sb[:, rc * 512:(rc + 1) * 512],
                        start=(rc == 0), stop=(rc == 1)),
                        waits=waits, inc="tmm")
                t_f = p.c("tmm")
                if cc % 2 == 0:
                    p.emit("vector", lambda e, cc=cc, bank=bank: e.tensor_scalar_add(
                        vint[:, cc * 512:(cc + 1) * 512], PB[bank][:, 0:512],
                        bcols[:, 64 + cc:65 + cc]),
                        waits=[("tmm", t_f)], inc="vec")
                    fprime_tok.append([("vec", p.c("vec"))])
                else:
                    p.emit("scalar", lambda e, cc=cc, bank=bank: e.activation(
                        vint[:, cc * 512:(cc + 1) * 512], PB[bank][:, 0:512],
                        AF.Identity, bias=bcols[:, 64 + cc:65 + cc]),
                        waits=[("tmm", t_f)], inc="sca")
                    fprime_tok.append([("sca", p.c("sca"))])
            fp_s = p.c("sca")
            fp_v = p.c("vec")
            p.mark("fp")

            if tile + 1 < n_tiles:
                t_ga_all = t_ga
                s_xhat_s_next = emit_ln1(tile + 1,
                                         [("vec", x_free), ("tmm", t_ga_all)])
            fin_v = 0
            for s in range(NSUB):
                b2k = (s % 2) * 2
                for cc in range(8):
                    waits = []
                    if s == 0 and cc == 0:
                        waits = [("sca", fp_s), ("vec", fp_v), ("vec", dgb_v)]
                    if s >= 2 and cc == 0:
                        waits = [("vec", fin_v)]
                    p.emit("tensor", lambda e, s=s, cc=cc, b2k=b2k: e.matmul(
                        bass.AP(PB[b2k + cc // 4], (cc % 4) * 128, [[512, 128], [1, 128]]),
                        vint[:, cc * 512 + s * 128: cc * 512 + s * 128 + 128],
                        eop[:, cc * 128:(cc + 1) * 128],
                        start=True, stop=True),
                        waits=waits, inc="tmm")
                t_gb = p.c("tmm")
                p.emit("vector", lambda e, s=s, b2k=b2k: e.tensor_add(
                    x2_t[:, s * 1024:s * 1024 + 512], x2_t[:, s * 1024:s * 1024 + 512],
                    PB[b2k][:, 0:512]),
                    waits=[("tmm", t_gb)], inc="vec")
                p.emit("vector", lambda e, s=s, b2k=b2k: e.tensor_add(
                    x2_t[:, s * 1024 + 512:(s + 1) * 1024],
                    x2_t[:, s * 1024 + 512:(s + 1) * 1024],
                    PB[b2k + 1][:, 0:512]),
                    inc="vec")
                fin_v = p.c("vec")
                p.dma("sync",
                      ap(y_h, tile * T * D + s * 128 * D, [[D, 128], [1, D]]),
                      x2_t[:, s * 1024:(s + 1) * 1024],
                      waits=[("vec", fin_v)], inc="xout")
            fin_v_prev = fin_v
            x2_free_out = p.c("xout")
            if tile + 1 < n_tiles:
                s_xhat_s = s_xhat_s_next

        if phase is not None:
            print("PLAN: truncating at", phase, "=", p.marks[phase], "of", p.seq)
            p.truncate(p.marks[phase])
        if dbg:
            last = p.wall("vec", "sca", "tmm", "xout", "shuf")
            dump_map = {
                "d_qop": qop[:], "d_kop": kop[:], "d_vop": vop[:], "d_eop": eop[:],
                "d_otmp": qtmp[:], "d_ofm": arena[:], "d_xhatT": xhatT[:],
                "d_x2": x2_t[:], "d_modc": modc[:], "d_bcols": bcols[:],
                "d_vint": vint[:], "d_ptsb": ptsb[:], "d_q2sb": q2sb[:],
                "d_small": small[:], "d_scolb": scolb[:],
            }
            first = True
            for name in dbg:
                p.dma("gpsimd", dbg[name], dump_map[name],
                      waits=(last if first else []), inc="dma")
                first = False

        p.emit("sync", lambda e: e.nop(),
               waits=p.wall("xout", "dma", "shuf", "cast"))

        with nc.Block() as block:
            p.realize(block, sems)

    return nc


def host_consts():
    ident = np.eye(128, dtype=BF)
    i64d = np.zeros((128, 64), dtype=BF)
    i64d[0:64] = np.eye(64, dtype=BF)
    i64d[64:128] = np.eye(64, dtype=BF)
    base = np.zeros((9, 128), dtype=np.float32)
    for r in range(8):
        for t in range(8):
            for h in range(16):
                base[r, h * 8 + t] = 32.0 if t == r else 0.0
    maskk = np.tile(base, (1, 64)).astype(BF)
    maskq = np.tile(base, (1, 64)).astype(BF)
    maskk[8, :] = np.float32(-32.0)
    maskq[8, :] = np.float32(32.0)
    return ident, i64d, maskq, maskk


_NC_CACHE = {}


def kernel(L0_attn_x, L0_attn_Wq, L0_attn_bq, L0_attn_Wk, L0_attn_bk, L0_attn_Wv,
           L0_attn_bv, L0_ffn_U1, L0_ffn_V1, L0_ffn_b1, L0_ffn_U2, L0_ffn_V2,
           L0_ffn_b2, L0_c, L0_wmod):
    x = np.ascontiguousarray(np.asarray(L0_attn_x, dtype=np.float32))
    B, S, Dm = x.shape
    half = S // 2
    ident, i64d, maskq, maskk = host_consts()
    shared = {
        "wq": np.ascontiguousarray(L0_attn_Wq, dtype=np.float32),
        "wk": np.ascontiguousarray(L0_attn_Wk, dtype=np.float32),
        "wv": np.ascontiguousarray(L0_attn_Wv, dtype=np.float32),
        "bq": np.asarray(L0_attn_bq, dtype=np.float32).reshape(1, Dm),
        "bk": np.asarray(L0_attn_bk, dtype=np.float32).reshape(1, Dm),
        "bv": np.asarray(L0_attn_bv, dtype=np.float32).reshape(1, Dm),
        "u1": np.ascontiguousarray(L0_ffn_U1, dtype=np.float32),
        "v1": np.ascontiguousarray(L0_ffn_V1, dtype=np.float32),
        "b1": np.asarray(L0_ffn_b1, dtype=np.float32).reshape(1, DFF),
        "u2": np.ascontiguousarray(L0_ffn_U2, dtype=np.float32),
        "v2": np.ascontiguousarray(L0_ffn_V2, dtype=np.float32),
        "b2": np.asarray(L0_ffn_b2, dtype=np.float32).reshape(1, Dm),
        "wmod": np.ascontiguousarray(L0_wmod, dtype=np.float32),
        "ident": ident, "i64d": i64d, "maskq": maskq, "maskk": maskk,
    }
    cvec = np.asarray(L0_c, dtype=np.float32)
    n_tiles = (half + T - 1) // T
    if n_tiles not in _NC_CACHE:
        _NC_CACHE[n_tiles] = build_core_program(n_tiles)
    nc = _NC_CACHE[n_tiles]
    in_maps = []
    for core in range(8):
        b = core // 2
        h0 = (core % 2) * half
        m = dict(shared)
        m["x"] = np.ascontiguousarray(x[b, h0:h0 + half])
        m["cvec"] = np.ascontiguousarray(cvec[b:b + 1])
        in_maps.append(m)
    res = run_bass_kernel_spmd(nc, in_maps, core_ids=list(range(8)))
    out = np.empty((B, S, Dm), dtype=np.float32)
    for core in range(8):
        b = core // 2
        h0 = (core % 2) * half
        out[b, h0:h0 + half] = res.results[core]["y"]
    return out


# version 06 (final)
# speedup vs baseline: 1.0562x; 1.0123x over previous
import numpy as np
import ml_dtypes
import concourse.bass as bass
import concourse.mybir as mybir
from concourse.bass_utils import run_bass_kernel_spmd

F32 = mybir.dt.float32
BF16 = mybir.dt.bfloat16
AF = mybir.ActivationFunctionType
ALU = mybir.AluOpType
BF = ml_dtypes.bfloat16

D = 1024
NH = 16
DK = 64
DFF = 4096
RANK = 256
T = 512
NSUB = T // 128
NG = T // 8
EPS = 1e-5

ENGS = ("sync", "gpsimd", "tensor", "vector", "scalar")


class Prog:
    def __init__(self):
        self.ops = {e: [] for e in ENGS}
        self.ctr = {}
        self.seq = 0
        self.marks = {}

    def mark(self, name):
        self.marks[name] = self.seq

    def c(self, sem):
        return self.ctr.get(sem, 0)

    def truncate(self, limit_seq):
        self.ctr = {}
        for e in ENGS:
            kept = [op for op in self.ops[e] if op[3] <= limit_seq]
            self.ops[e] = kept
            for fn, waits, tok, sq in kept:
                if tok is not None:
                    self.ctr[tok[0]] = self.ctr.get(tok[0], 0) + tok[1]

    def emit(self, eng, fn, waits=(), inc=None, inc_k=1):
        tok = None
        if inc is not None:
            self.ctr[inc] = self.ctr.get(inc, 0) + inc_k
            tok = (inc, inc_k)
        self.seq += 1
        self.ops[eng].append((fn, [w for w in waits if w[1] > 0], tok, self.seq))

    def dma(self, eng, out_ap, in_ap, waits=(), inc="dma", slow=False):
        self.emit(eng, lambda e, o=out_ap, i=in_ap, sl=slow: e.dma_start(
            o, i, allow_slow_non_contiguous=sl),
            waits=waits, inc=inc, inc_k=16)

    def drain(self, eng):
        self.ops[eng].append((lambda e: e.drain(), [], None, self.seq))

    def wall(self, *sems):
        return [(s, self.c(s)) for s in sems]

    def realize(self, block, sems):
        for eng_name in ENGS:
            seq = self.ops[eng_name]
            if not seq:
                continue
            def mk(seq=seq):
                def body(eng):
                    for fn, waits, tok, _sq in seq:
                        for s, v in waits:
                            eng.wait_ge(sems[s], v)
                        instr = fn(eng)
                        if tok is not None:
                            instr.then_inc(sems[tok[0]], tok[1])
                return body
            getattr(block, eng_name)(mk())


def ap(t, offset, dims):
    if isinstance(t, bass.AP):
        t = t.tensor
    return bass.AP(t, offset, [list(d) for d in dims])


def build_core_program(n_tiles, debug=(), phase=None, skip=()):
    nc = bass.Bass("TRN2", target_bir_lowering=False)
    NTOK = n_tiles * T

    x_h = nc.dram_tensor("x", [NTOK, D], F32, kind="ExternalInput").ap()
    y_h = nc.dram_tensor("y", [NTOK, D], F32, kind="ExternalOutput").ap()
    wq_h = nc.dram_tensor("wq", [D, D], F32, kind="ExternalInput").ap()
    wk_h = nc.dram_tensor("wk", [D, D], F32, kind="ExternalInput").ap()
    wv_h = nc.dram_tensor("wv", [D, D], F32, kind="ExternalInput").ap()
    bq_h = nc.dram_tensor("bq", [1, D], F32, kind="ExternalInput").ap()
    bk_h = nc.dram_tensor("bk", [1, D], F32, kind="ExternalInput").ap()
    bv_h = nc.dram_tensor("bv", [1, D], F32, kind="ExternalInput").ap()
    u1_h = nc.dram_tensor("u1", [D, RANK], F32, kind="ExternalInput").ap()
    v1_h = nc.dram_tensor("v1", [RANK, DFF], F32, kind="ExternalInput").ap()
    b1_h = nc.dram_tensor("b1", [1, DFF], F32, kind="ExternalInput").ap()
    u2_h = nc.dram_tensor("u2", [DFF, RANK], F32, kind="ExternalInput").ap()
    v2_h = nc.dram_tensor("v2", [RANK, D], F32, kind="ExternalInput").ap()
    b2_h = nc.dram_tensor("b2", [1, D], F32, kind="ExternalInput").ap()
    c_h = nc.dram_tensor("cvec", [1, D], F32, kind="ExternalInput").ap()
    wmod_h = nc.dram_tensor("wmod", [D, 6 * D], F32, kind="ExternalInput").ap()
    ident_h = nc.dram_tensor("ident", [128, 128], BF16, kind="ExternalInput").ap()
    i64d_h = nc.dram_tensor("i64d", [128, 64], BF16, kind="ExternalInput").ap()
    maskq_h = nc.dram_tensor("maskq", [9, 8192], BF16, kind="ExternalInput").ap()
    maskk_h = nc.dram_tensor("maskk", [9, 8192], BF16, kind="ExternalInput").ap()
    pswap_h = nc.dram_tensor("pswap", [128, 128], BF16, kind="ExternalInput").ap()
    dbg = {}
    for name, shape, dt in debug:
        dbg[name] = nc.dram_tensor(name, list(shape), dt, kind="ExternalOutput").ap()

    p = Prog()
    SEMS = ["dma", "cast", "tmm", "vec", "sca", "xin", "xout", "shuf"]

    from contextlib import ExitStack
    with ExitStack() as ctx:
        wq_sb = ctx.enter_context(nc.sbuf_tensor("wq_sb", [128, 8192], BF16))
        wk_sb = ctx.enter_context(nc.sbuf_tensor("wk_sb", [128, 8192], BF16))
        wv_sb = ctx.enter_context(nc.sbuf_tensor("wv_sb", [128, 8192], BF16))
        u1_sb = ctx.enter_context(nc.sbuf_tensor("u1_sb", [128, 2048], BF16))
        v1_sb = ctx.enter_context(nc.sbuf_tensor("v1_sb", [128, 8192], BF16))
        u2_sb = ctx.enter_context(nc.sbuf_tensor("u2_sb", [128, 8192], BF16))
        v2_sb = ctx.enter_context(nc.sbuf_tensor("v2_sb", [128, 2048], BF16))
        ident = ctx.enter_context(nc.sbuf_tensor("ident_sb", [128, 128], BF16))
        i64d = ctx.enter_context(nc.sbuf_tensor("i64d_sb", [128, 64], BF16))
        pswap = ctx.enter_context(nc.sbuf_tensor("pswap_sb", [128, 128], BF16))
        qop = ctx.enter_context(nc.sbuf_tensor("qop", [73, 8192], BF16))
        kop = ctx.enter_context(nc.sbuf_tensor("kop", [73, 8192], BF16))
        vop = ctx.enter_context(nc.sbuf_tensor("vop", [128, 4160], BF16))
        bcols = ctx.enter_context(nc.sbuf_tensor("bcols", [128, 80], F32))
        modc = ctx.enter_context(nc.sbuf_tensor("modc", [128, 56], F32))
        rowbuf = ctx.enter_context(nc.sbuf_tensor("rowbuf", [1, 512], F32))
        rowbf = ctx.enter_context(nc.sbuf_tensor("rowbf", [1, 1024], BF16))
        scolb = ctx.enter_context(nc.sbuf_tensor("scolb", [128, 24], BF16))
        onescal = ctx.enter_context(nc.sbuf_tensor("onescal", [1, 8], BF16))
        small = ctx.enter_context(nc.sbuf_tensor("small", [128, 64], F32))
        x_t = ctx.enter_context(nc.sbuf_tensor("x_t", [128, 4096], F32))
        x2_t = ctx.enter_context(nc.sbuf_tensor("x2_t", [128, 4096], F32))
        arena = ctx.enter_context(nc.sbuf_tensor("arena", [128, 4096], BF16))
        xhatT = ctx.enter_context(nc.sbuf_tensor("xhatT", [128, 4096], BF16))
        qtmp = ctx.enter_context(nc.sbuf_tensor("qtmp", [128, 4096], BF16))
        oddscr = ctx.enter_context(nc.sbuf_tensor("oddscr", [64, 1024], BF16))
        vint = ctx.enter_context(nc.sbuf_tensor("vint", [128, 4096], BF16))
        eop = ctx.enter_context(nc.sbuf_tensor("eop", [128, 1024], BF16))
        den = ctx.enter_context(nc.sbuf_tensor("den", [128, 16], F32))
        ptsb = ctx.enter_context(nc.sbuf_tensor("ptsb", [128, 1024], BF16))
        htring = ctx.enter_context(nc.sbuf_tensor("htring", [128, 1024], BF16))
        q2sb = ctx.enter_context(nc.sbuf_tensor("q2sb", [128, 1024], BF16))
        pb0 = ctx.enter_context(nc.psum_tensor("pb0", [128, 512], F32))
        pb1 = ctx.enter_context(nc.psum_tensor("pb1", [128, 512], F32))
        pb2 = ctx.enter_context(nc.psum_tensor("pb2", [128, 512], F32))
        pb3 = ctx.enter_context(nc.psum_tensor("pb3", [128, 512], F32))
        pb4 = ctx.enter_context(nc.psum_tensor("pb4", [128, 512], F32))
        pb5 = ctx.enter_context(nc.psum_tensor("pb5", [128, 512], F32))
        pb6 = ctx.enter_context(nc.psum_tensor("pb6", [128, 512], F32))
        pb7 = ctx.enter_context(nc.psum_tensor("pb7", [128, 512], F32))
        s_dma = ctx.enter_context(nc.semaphore("s_dma"))
        s_cast = ctx.enter_context(nc.semaphore("s_cast"))
        s_tmm = ctx.enter_context(nc.semaphore("s_tmm"))
        s_vec = ctx.enter_context(nc.semaphore("s_vec"))
        s_sca = ctx.enter_context(nc.semaphore("s_sca"))
        s_xin = ctx.enter_context(nc.semaphore("s_xin"))
        s_xout = ctx.enter_context(nc.semaphore("s_xout"))
        s_shuf = ctx.enter_context(nc.semaphore("s_shuf"))
        sems = {"dma": s_dma, "cast": s_cast, "tmm": s_tmm, "vec": s_vec,
                "sca": s_sca, "xin": s_xin, "xout": s_xout, "shuf": s_shuf}
        PB = [pb0, pb1, pb2, pb3, pb4, pb5, pb6, pb7]

        p.dma("sync", ident[:], ident_h)
        p.dma("sync", i64d[:], i64d_h)
        p.dma("sync", pswap[:], pswap_h)
        p.dma("sync", qop[64:73, :], maskq_h)
        p.dma("sync", kop[64:73, :], maskk_h)
        p.dma("sync", rowbuf[0:1, 0:512], ap(c_h, 0, [[1, 1], [1, 512]]))
        p.dma("sync", ap(bcols, 32, [[80, 128], [1, 32]]),
              ap(b1_h, 0, [[1, 128], [128, 32]]), slow=True)
        p.dma("sync", ap(bcols, 64, [[80, 128], [1, 8]]),
              ap(b2_h, 0, [[1, 128], [128, 8]]), slow=True)
        p.dma("sync", ap(small, 0, [[64, 128], [1, 8]]),
              ap(bq_h, 0, [[1, 128], [128, 8]]), slow=True)
        p.dma("sync", ap(small, 8, [[64, 128], [1, 8]]),
              ap(bk_h, 0, [[1, 128], [128, 8]]), slow=True)
        p.dma("sync", ap(small, 16, [[64, 128], [1, 8]]),
              ap(bv_h, 0, [[1, 128], [128, 8]]), slow=True)
        consts_done = p.c("dma")

        for w_sb, w_hh in ((wq_sb, wq_h), (wk_sb, wk_h), (wv_sb, wv_h)):
            p.dma("gpsimd", w_sb[:],
                  ap(w_hh, 0, [[1024, 128], [128 * 1024, 8], [1, 1024]]), inc="cast")
        p.dma("gpsimd", u1_sb[:],
              ap(u1_h, 0, [[256, 128], [128 * 256, 8], [1, 256]]), inc="cast")
        p.dma("gpsimd", v1_sb[:],
              ap(v1_h, 0, [[4096, 128], [128 * 4096, 2], [1, 4096]]), inc="cast")
        p.dma("gpsimd", u2_sb[:],
              ap(u2_h, 0, [[256, 128], [128 * 256, 32], [1, 256]]), inc="cast")
        p.dma("gpsimd", v2_sb[:],
              ap(v2_h, 0, [[1024, 128], [128 * 1024, 2], [1, 1024]]), inc="cast")
        w_done = p.c("cast")

        p.emit("vector", lambda e: e.memset(onescal[:], 1.0), inc="vec")
        p.emit("vector", lambda e: e.memset(
            ap(vop, 64, [[4160, 128], [65, 64]]), 1.0), inc="vec")
        v_ones = p.c("vec")

        p.emit("scalar", lambda e: e.activation(rowbf[0:1, 0:512], rowbuf[0:1, 0:512],
                                                AF.Silu),
               waits=[("dma", consts_done)], inc="sca")
        s_h1 = p.c("sca")
        p.dma("sync", rowbuf[0:1, 0:512], ap(c_h, 512, [[1, 1], [1, 512]]),
              waits=[("sca", s_h1)])
        c2_done = p.c("dma")
        p.emit("scalar", lambda e: e.activation(rowbf[0:1, 512:1024], rowbuf[0:1, 0:512],
                                                AF.Silu),
               waits=[("dma", c2_done)], inc="sca")
        s_silu = p.c("sca")
        for k in range(8):
            p.emit("tensor", lambda e, k=k: e.matmul(
                pb6[:, k:k + 1], rowbf[0:1, k * 128:(k + 1) * 128], onescal[0:1, 0:1],
                start=True, stop=True),
                waits=(p.wall("sca", "vec") if k == 0 else []), inc="tmm")
        t_scol = p.c("tmm")
        p.emit("vector", lambda e: e.tensor_copy(scolb[:, 0:8], pb6[:, 0:8]),
               waits=[("tmm", t_scol)], inc="vec")
        v_scolb = p.c("vec")

        for n in range(12):
            ring = vint if n % 2 == 0 else qtmp
            p.dma("gpsimd", ap(ring, 0, [[4096, 128], [512, 8], [1, 512]]),
                  ap(wmod_h, n * 512, [[6144, 128], [128 * 6144, 8], [1, 512]]),
                  waits=(p.wall("tmm") if n >= 2 else []),
                  inc="cast")
            cd = p.c("cast")
            for k in range(8):
                p.emit("tensor", lambda e, k=k, ring=ring: e.matmul(
                    pb7[0:1, 0:512], scolb[:, k:k + 1],
                    ring[:, k * 512:(k + 1) * 512],
                    start=(k == 0), stop=(k == 7)),
                    waits=(([("cast", cd), ("vec", v_scolb)] + p.wall("sca"))
                           if k == 0 else []),
                    inc="tmm")
            t_mod = p.c("tmm")
            p.emit("scalar", lambda e: e.copy(rowbuf[0:1, 0:512], pb7[0:1, 0:512]),
                   waits=[("tmm", t_mod)], inc="sca")
            p.drain("scalar")
            p.emit("scalar", lambda e: e.copy(rowbf[0:1, 0:512], rowbuf[0:1, 0:512]),
                   inc="sca")
            s_rb = p.c("sca")
            for j in range(4):
                col = n * 4 + j
                p.emit("tensor", lambda e, j=j: e.matmul(
                    pb6[:, 8:9], rowbf[0:1, j * 128:(j + 1) * 128],
                    onescal[0:1, 0:1], start=True, stop=True),
                    waits=[("sca", s_rb)] + p.wall("vec"), inc="tmm")
                t_c = p.c("tmm")
                p.emit("vector", lambda e, col=col: e.tensor_copy(
                    modc[:, col:col + 1], pb6[:, 8:9]),
                    waits=[("tmm", t_c)], inc="vec")
        v_modc = p.c("vec")

        def emit_gate(c0, sg0, mg0, prev_v):
            p.emit("scalar", lambda e, c0=c0, sg0=sg0: e.sign(
                small[:, sg0:sg0 + 8], modc[:, c0:c0 + 8]),
                waits=[("vec", v_modc), ("vec", prev_v)], inc="sca")
            p.emit("scalar", lambda e, c0=c0, mg0=mg0: e.activation(
                small[:, mg0:mg0 + 8], modc[:, c0:c0 + 8], AF.Abs), inc="sca")
            s_g = p.c("sca")
            p.emit("vector", lambda e, mg0=mg0: e.tensor_scalar_max(
                small[:, mg0:mg0 + 8], small[:, mg0:mg0 + 8], 0.25),
                waits=[("sca", s_g), ("vec", v_modc)], inc="vec")
            p.drain("vector")
            p.emit("vector", lambda e, c0=c0, sg0=sg0, mg0=mg0: e.tensor_mul(
                modc[:, c0:c0 + 8], small[:, sg0:sg0 + 8], small[:, mg0:mg0 + 8]),
                inc="vec")
            return p.c("vec")
        print("PLAN: before gates seq=", p.seq)
        g1 = emit_gate(16, 32, 40, 0)
        emit_gate(40, 56, 32, g1)
        print("PLAN: after gates seq=", p.seq)
        p.emit("vector", lambda e: e.tensor_scalar_add(modc[:, 48:56], modc[:, 8:16], 1.0),
               inc="vec")
        p.emit("vector", lambda e: e.tensor_scalar_add(small[:, 48:56], modc[:, 32:40], 1.0),
               inc="vec")
        p.emit("vector", lambda e: e.tensor_copy(scolb[:, 8:16], modc[:, 0:8]), inc="vec")
        p.emit("vector", lambda e: e.tensor_copy(scolb[:, 16:24], modc[:, 24:32]), inc="vec")
        v_gates = p.c("vec")

        def emit_bprime(w_sb, scol_off, raw_off, dst_off, width, wstride):
            nch = max(1, width // 512)
            for n in range(nch):
                for k in range(8):
                    p.emit("tensor", lambda e, n=n, k=k, w_sb=w_sb, scol_off=scol_off,
                           wstride=wstride: e.matmul(
                        pb7[0:1, 0:512] if wstride >= 512 else pb7[0:1, 0:wstride],
                        scolb[:, scol_off + k:scol_off + k + 1],
                        w_sb[:, k * wstride + n * 512:
                             k * wstride + n * 512 + min(512, wstride)],
                        start=(k == 0), stop=(k == 7)),
                        waits=(([("vec", v_gates)] + p.wall("sca")) if (n == 0 and k == 0)
                               else []),
                        inc="tmm")
                t_b = p.c("tmm")
                p.emit("scalar", lambda e, wstride=wstride: e.copy(
                    rowbf[0:1, 0:min(512, wstride)], pb7[0:1, 0:min(512, wstride)]),
                    waits=[("tmm", t_b)], inc="sca")
                s_b = p.c("sca")
                ncols = min(512, wstride) // 128
                for j in range(ncols):
                    cc = n * 4 + j
                    p.emit("tensor", lambda e, j=j: e.matmul(
                        pb6[:, 8:9], rowbf[0:1, j * 128:(j + 1) * 128],
                        onescal[0:1, 0:1], start=True, stop=True),
                        waits=[("sca", s_b)] + p.wall("vec"), inc="tmm")
                    t_c = p.c("tmm")
                    p.emit("vector", lambda e, cc=cc, dst_off=dst_off: e.tensor_copy(
                        bcols[:, dst_off + cc:dst_off + cc + 1], pb6[:, 8:9]),
                        waits=[("tmm", t_c)], inc="vec")
            if raw_off is not None:
                p.drain("vector")
                p.emit("vector", lambda e, dst_off=dst_off, raw_off=raw_off: e.tensor_add(
                    bcols[:, dst_off:dst_off + width // 128],
                    bcols[:, dst_off:dst_off + width // 128],
                    small[:, raw_off:raw_off + width // 128]),
                    waits=[("dma", consts_done)], inc="vec")

        emit_bprime(wq_sb, 8, 0, 0, 1024, 1024)
        emit_bprime(wk_sb, 8, 8, 8, 1024, 1024)
        emit_bprime(wv_sb, 8, 16, 16, 1024, 1024)
        emit_bprime(u1_sb, 16, None, 24, 256, 256)
        v_bcols = p.c("vec")

        for w_sb in (wq_sb, wk_sb, wv_sb):
            for k in range(8):
                p.emit("vector", lambda e, w_sb=w_sb, k=k: e.tensor_scalar_mul(
                    w_sb[:, k * 1024:(k + 1) * 1024], w_sb[:, k * 1024:(k + 1) * 1024],
                    modc[:, 48 + k:49 + k]),
                    waits=([("vec", v_bcols)] if k == 0 and w_sb is wq_sb else []),
                    inc="vec")
        for k in range(8):
            p.emit("vector", lambda e, k=k: e.tensor_scalar_mul(
                u1_sb[:, k * 256:(k + 1) * 256], u1_sb[:, k * 256:(k + 1) * 256],
                small[:, 48 + k:49 + k]), inc="vec")
        p.emit("vector", lambda e: e.memset(small[:, 32:33], EPS), inc="vec")
        v_prep = p.c("vec")
        t_prep = p.c("tmm")
        s_prep = p.c("sca")
        p.mark("prep")

        x_free = 0
        x2_free_out = 0
        fin_v_prev = 0
        def emit_ln_chain(sum_off, sq1_off, sq2_off, two_sq):
            if two_sq:
                p.emit("vector", lambda e: e.tensor_add(
                    small[:, sq1_off:sq1_off + 4], small[:, sq1_off:sq1_off + 4],
                    small[:, sq2_off:sq2_off + 4]),
                    waits=[("sca", p.c("sca")), ("vec", p.c("vec"))], inc="vec")
            p.drain("vector")
            p.emit("vector", lambda e: e.tensor_scalar_mul(
                small[:, sum_off:sum_off + 4], small[:, sum_off:sum_off + 4],
                1.0 / 1024), inc="vec")
            p.drain("vector")
            p.emit("vector", lambda e: e.tensor_mul(
                small[:, 28:32], small[:, sum_off:sum_off + 4],
                small[:, sum_off:sum_off + 4]), inc="vec")
            p.emit("vector", lambda e: e.tensor_scalar_mul(
                small[:, sq1_off:sq1_off + 4], small[:, sq1_off:sq1_off + 4],
                1.0 / 1024), inc="vec")
            p.drain("vector")
            p.emit("vector", lambda e: e.tensor_sub(
                small[:, sq1_off:sq1_off + 4], small[:, sq1_off:sq1_off + 4],
                small[:, 28:32]), inc="vec")
            v_var = p.c("vec")
            p.emit("scalar", lambda e: e.activation(
                small[:, 28:32], small[:, sq1_off:sq1_off + 4], AF.Ln,
                bias=small[:, 32:33]),
                waits=[("vec", v_var)], inc="sca")
            p.drain("scalar")
            p.emit("scalar", lambda e: e.activation(
                small[:, 0:4], small[:, 28:32], AF.Exp, scale=-0.5), inc="sca")
            s_r = p.c("sca")
            p.emit("vector", lambda e: e.tensor_mul(
                small[:, 8:12], small[:, sum_off:sum_off + 4], small[:, 0:4]),
                waits=[("sca", s_r)], inc="vec")
            p.drain("vector")
            p.emit("vector", lambda e: e.tensor_scalar_mul(
                small[:, 8:12], small[:, 8:12], -1.0), inc="vec")
            return p.c("vec")

        def emit_ln1(tile, extra_w):
            p.dma("sync", x_t[:],
                  ap(x_h, tile * T * D, [[D, 128], [128 * D, NSUB], [1, D]]),
                  waits=list(extra_w), inc="xin")
            xin_done = p.c("xin")
            for s in range(NSUB):
                p.emit("vector", lambda e, s=s: e.reduce_sum(
                    small[:, 16 + s:17 + s], x_t[:, s * 1024:(s + 1) * 1024],
                    axis=mybir.AxisListType.X),
                    waits=([("xin", xin_done)] + (list(extra_w) if s == 0 else []))
                          if s == 0 else [],
                    inc="vec")
                p.emit("scalar", lambda e, s=s: e.activation(
                    qtmp[:, 0:512], x_t[:, s * 1024:s * 1024 + 512], AF.Square,
                    accum_out=small[:, 20 + s:21 + s]),
                    waits=(([("xin", xin_done)] + list(extra_w)) if s == 0 else []),
                    inc="sca")
                p.emit("scalar", lambda e, s=s: e.activation(
                    qtmp[:, 512:1024], x_t[:, s * 1024 + 512:(s + 1) * 1024], AF.Square,
                    accum_out=small[:, 24 + s:25 + s]),
                    inc="sca")
            v_nmr = emit_ln_chain(16, 20, 24, True)
            out = []
            for s in range(NSUB):
                p.emit("scalar", lambda e, s=s: e.activation(
                    arena[:, s * 1024:(s + 1) * 1024], x_t[:, s * 1024:(s + 1) * 1024],
                    AF.Identity, bias=small[:, 8 + s:9 + s], scale=small[:, s:s + 1]),
                    waits=([("vec", v_nmr)] if s == 0 else []), inc="sca")
                out.append(p.c("sca"))
            return out

        s_xhat_s = None
        for tile in range(n_tiles):
            if tile == 0:
                s_xhat_s = emit_ln1(0, [("vec", v_prep), ("tmm", t_prep)])
            p.mark("ln1")

            def emit_transposes(src_sb, s_ready, extra_first):
                ev = []
                for s in range(NSUB):
                    b = (s % 2) * 2
                    for fc in range(8):
                        waits = []
                        if fc == 0:
                            waits = [("sca", s_ready[s])]
                        if s == 0 and fc == 0:
                            waits += extra_first
                        if s >= 2 and fc == 0:
                            waits += list(ev[s - 2])
                        p.emit("tensor", lambda e, s=s, fc=fc, b=b, src_sb=src_sb: e.matmul(
                            bass.AP(PB[b + fc // 4], (fc % 4) * 128, [[512, 128], [1, 128]]),
                            src_sb[:, s * 1024 + fc * 128: s * 1024 + (fc + 1) * 128],
                            ident[:], start=True, stop=True),
                            waits=waits, inc="tmm")
                    t_tr = p.c("tmm")
                    if s % 2 == 0:
                        p.emit("vector", lambda e, s=s, b=b: e.tensor_copy(
                            ap(xhatT, s * 128, [[4096, 128], [512, 4], [1, 128]]),
                            ap(PB[b], 0, [[512, 128], [128, 4], [1, 128]])),
                            waits=[("tmm", t_tr)], inc="vec")
                        p.emit("vector", lambda e, s=s, b=b: e.tensor_copy(
                            ap(xhatT, 2048 + s * 128, [[4096, 128], [512, 4], [1, 128]]),
                            ap(PB[b + 1], 0, [[512, 128], [128, 4], [1, 128]])),
                            inc="vec")
                        ev.append([("vec", p.c("vec"))])
                    else:
                        p.emit("scalar", lambda e, s=s, b=b: e.copy(
                            ap(xhatT, s * 128, [[4096, 128], [512, 4], [1, 128]]),
                            ap(PB[b], 0, [[512, 128], [128, 4], [1, 128]])),
                            waits=[("tmm", t_tr)], inc="sca")
                        p.emit("scalar", lambda e, s=s, b=b: e.copy(
                            ap(xhatT, 2048 + s * 128, [[4096, 128], [512, 4], [1, 128]]),
                            ap(PB[b + 1], 0, [[512, 128], [128, 4], [1, 128]])),
                            inc="sca")
                        ev.append([("sca", p.c("sca"))])
                return p.c("vec"), p.c("sca")

            xT_v, xT_s = emit_transposes(arena, s_xhat_s,
                                         [("vec", fin_v_prev)] if tile > 0 else [])
            p.mark("xT")

            def proj_qk(w_sb, op, bcol_off, first_waits, stage, stage_w=()):
                ev = []
                for cc in range(8):
                    bank = cc % 4
                    for kc in range(8):
                        waits = []
                        if cc == 0 and kc == 0:
                            waits = list(first_waits)
                        if cc >= 4 and kc == 0:
                            waits = list(ev[cc - 4])
                        p.emit("tensor", lambda e, cc=cc, kc=kc, bank=bank, w_sb=w_sb: e.matmul(
                            PB[bank][:, 0:512],
                            w_sb[:, kc * 1024 + cc * 128: kc * 1024 + (cc + 1) * 128],
                            xhatT[:, kc * 512:(kc + 1) * 512],
                            start=(kc == 0), stop=(kc == 7)),
                            waits=waits, inc="tmm")
                    t_p = p.c("tmm")
                    even_dst = ap(op, 2 * cc * 8, [[8192, 64], [128, 64], [1, 8]])
                    even_src = bass.AP(PB[bank], 0, [[512, 64], [8, 64], [1, 8]])
                    if cc % 2 == 0:
                        p.emit("vector", lambda e, d=even_dst, s_=even_src, cc=cc:
                               e.tensor_scalar_add(
                            d, s_, bcols[0:64, bcol_off + cc:bcol_off + cc + 1]),
                            waits=[("tmm", t_p)], inc="vec")
                        p.emit("vector", lambda e, cc=cc, bank=bank, stage=stage:
                               e.tensor_scalar_add(
                            stage[64:128, cc * 512:(cc + 1) * 512],
                            PB[bank][64:128, 0:512],
                            bcols[64:128, bcol_off + cc:bcol_off + cc + 1]),
                            waits=(list(stage_w) if cc == 0 else []), inc="vec")
                        ev.append([("vec", p.c("vec"))])
                    else:
                        p.emit("scalar", lambda e, d=even_dst, s_=even_src, cc=cc:
                               e.activation(
                            d, s_, AF.Identity,
                            bias=bcols[0:64, bcol_off + cc:bcol_off + cc + 1]),
                            waits=[("tmm", t_p)], inc="sca")
                        p.emit("scalar", lambda e, cc=cc, bank=bank, stage=stage: e.activation(
                            stage[64:128, cc * 512:(cc + 1) * 512],
                            PB[bank][64:128, 0:512],
                            AF.Identity,
                            bias=bcols[64:128, bcol_off + cc:bcol_off + cc + 1]),
                            waits=(list(stage_w) if cc == 1 else []), inc="sca")
                        ev.append([("sca", p.c("sca"))])
                return p.c("vec"), p.c("sca")

            ev_v = []
            for cc in range(8):
                bank = cc % 4
                for kc in range(8):
                    waits = []
                    if cc == 0 and kc == 0:
                        waits = [("vec", xT_v), ("sca", xT_s)] + p.wall("shuf")
                    if cc >= 4 and kc == 0:
                        waits = list(ev_v[cc - 4])
                    p.emit("tensor", lambda e, cc=cc, kc=kc, bank=bank: e.matmul(
                        PB[bank][:, 0:512],
                        wv_sb[:, kc * 1024 + cc * 128: kc * 1024 + (cc + 1) * 128],
                        xhatT[:, kc * 512:(kc + 1) * 512],
                        start=(kc == 0), stop=(kc == 7)),
                        waits=waits, inc="tmm")
                t_v = p.c("tmm")
                if cc % 2 == 0:
                    p.emit("vector", lambda e, cc=cc, bank=bank: e.tensor_scalar_add(
                        qtmp[:, cc * 512:(cc + 1) * 512], PB[bank][:, 0:512],
                        bcols[:, 16 + cc:17 + cc]),
                        waits=[("tmm", t_v)], inc="vec")
                    ev_v.append([("vec", p.c("vec"))])
                else:
                    p.emit("scalar", lambda e, cc=cc, bank=bank: e.activation(
                        qtmp[:, cc * 512:(cc + 1) * 512], PB[bank][:, 0:512],
                        AF.Identity, bias=bcols[:, 16 + cc:17 + cc]),
                        waits=[("tmm", t_v)], inc="sca")
                    ev_v.append([("sca", p.c("sca"))])
            vfm_v, vfm_s = p.c("vec"), p.c("sca")
            p.mark("qkv")

            vint_w = [("vec", vfm_v), ("sca", vfm_s)]
            first_v = True
            for j in range(16):
                src_band = (j % 2) * 64
                for half in range(2):
                    dst_band = half * 64
                    if src_band != dst_band:
                        continue
                    src_off = src_band * 4096 + (j // 2) * 512 + half * 256
                    dst_off = dst_band * 4096 + j * 8
                    p.emit("vector", lambda e, src_off=src_off, dst_off=dst_off:
                           e.tensor_copy(
                        ap(vint, dst_off, [[4096, 64], [128, 32], [1, 8]]),
                        ap(qtmp, src_off, [[4096, 64], [8, 32], [1, 8]])),
                        waits=(vint_w if first_v else []), inc="vec")
                    first_v = False
            swap_ev = []
            for c in range(8):
                bank = 4 + c % 2
                waits = []
                if c == 0:
                    waits = list(vint_w) + p.wall("vec", "sca")
                if c >= 2:
                    waits = list(swap_ev[c - 2])
                p.emit("tensor", lambda e, c=c, bank=bank: e.matmul(
                    PB[bank][:, 0:512], pswap[:], qtmp[:, c * 512:(c + 1) * 512],
                    start=True, stop=True),
                    waits=waits, inc="tmm")
                t_sw = p.c("tmm")
                if c % 2 == 0:
                    p.emit("vector", lambda e, c=c, bank=bank: e.tensor_copy(
                        ap(vint, (2 * c + 1) * 8, [[4096, 64], [128, 32], [1, 8]]),
                        ap(PB[bank], 0, [[512, 64], [8, 32], [1, 8]])),
                        waits=[("tmm", t_sw)], inc="vec")
                    p.emit("vector", lambda e, c=c, bank=bank: e.tensor_copy(
                        ap(vint, 64 * 4096 + 2 * c * 8, [[4096, 64], [128, 32], [1, 8]]),
                        ap(PB[bank], 64 * 512 + 256, [[512, 64], [8, 32], [1, 8]])),
                        inc="vec")
                    swap_ev.append([("vec", p.c("vec"))])
                else:
                    p.emit("scalar", lambda e, c=c, bank=bank: e.copy(
                        ap(vint, (2 * c + 1) * 8, [[4096, 64], [128, 32], [1, 8]]),
                        ap(PB[bank], 0, [[512, 64], [8, 32], [1, 8]])),
                        waits=[("tmm", t_sw)], inc="sca")
                    p.emit("scalar", lambda e, c=c, bank=bank: e.copy(
                        ap(vint, 64 * 4096 + 2 * c * 8, [[4096, 64], [128, 32], [1, 8]]),
                        ap(PB[bank], 64 * 512 + 256, [[512, 64], [8, 32], [1, 8]])),
                        inc="sca")
                    swap_ev.append([("sca", p.c("sca"))])
            vint_v, vint_s2 = p.c("vec"), p.c("sca")
            vint_dma = p.c("shuf")

            vq, sq_ = proj_qk(wq_sb, qop, 0,
                              [("vec", vfm_v), ("sca", vfm_s)], qtmp,
                              stage_w=[("vec", vint_v), ("sca", vint_s2)])
            for cc in range(8):
                p.dma("sync",
                      ap(qop, (2 * cc + 1) * 8, [[8192, 64], [128, 64], [1, 8]]),
                      ap(qtmp, 64 * 4096 + cc * 512, [[4096, 64], [8, 64], [1, 8]]),
                      waits=([("sca", sq_), ("vec", vq)] if cc == 0 else []), inc="shuf")
            q_odd = p.c("shuf")
            vk, sk_ = proj_qk(wk_sb, kop, 8, [("vec", vq), ("sca", sq_)], arena)
            for cc in range(8):
                p.dma("scalar",
                      ap(kop, (2 * cc + 1) * 8, [[8192, 64], [128, 64], [1, 8]]),
                      ap(arena, 64 * 4096 + cc * 512, [[4096, 64], [8, 64], [1, 8]]),
                      waits=([("sca", sk_), ("vec", vk)] if cc == 0 else []), inc="shuf")
            k_odd = p.c("shuf")

            ev_vt = []
            for gb in range(8):
                bank = 2 + gb % 2
                for gl in range(8):
                    g = gb * 8 + gl
                    half = g // 32
                    base = half * 64
                    waits = []
                    if gb == 0 and gl == 0:
                        waits = [("vec", vint_v), ("sca", vint_s2),
                                 ("vec", vk), ("sca", sk_)]
                    if gb >= 2 and gl == 0:
                        waits = list(ev_vt[gb - 2])
                    p.emit("tensor", lambda e, g=g, gl=gl, bank=bank, base=base: e.matmul(
                        bass.AP(PB[bank], gl * 64, [[512, 128], [1, 64]]),
                        vint[base:base + 64, (g % 32) * 128:(g % 32) * 128 + 128],
                        i64d[base:base + 64, 0:64],
                        start=True, stop=True),
                        waits=waits, inc="tmm")
                t_vt = p.c("tmm")
                if gb % 2 == 0:
                    p.emit("vector", lambda e, gb=gb, bank=bank: e.tensor_copy(
                        ap(vop, gb * 8 * 65, [[4160, 128], [65, 8], [1, 64]]),
                        ap(PB[bank], 0, [[512, 128], [64, 8], [1, 64]])),
                        waits=[("tmm", t_vt)], inc="vec")
                    ev_vt.append([("vec", p.c("vec"))])
                else:
                    p.emit("scalar", lambda e, gb=gb, bank=bank: e.copy(
                        ap(vop, gb * 8 * 65, [[4160, 128], [65, 8], [1, 64]]),
                        ap(PB[bank], 0, [[512, 128], [64, 8], [1, 64]])),
                        waits=[("tmm", t_vt)], inc="sca")
                    ev_vt.append([("sca", p.c("sca"))])
            vop_v, vop_s = p.c("vec"), p.c("sca")
            p.mark("vop")

            att_w = [("shuf", q_odd), ("shuf", k_odd), ("vec", vq), ("vec", vk),
                     ("sca", sq_), ("sca", sk_), ("vec", vop_v), ("sca", vop_s)]
            exp_done = []
            attnv_done = []
            norm_done = []
            NB4 = NG // 4

            def emit_score(b):
                sbank = b % 2
                waits = (list(att_w) if b == 0 else [])
                if b >= 2:
                    waits = waits + [("sca", exp_done[b - 2])]
                for gl in range(4):
                    g = 4 * b + gl
                    p.emit("tensor", lambda e, g=g, gl=gl, sbank=sbank: e.matmul(
                        bass.AP(PB[sbank], gl * 128, [[512, 128], [1, 128]]),
                        kop[0:73, g * 128:(g + 1) * 128],
                        qop[0:73, g * 128:(g + 1) * 128],
                        start=True, stop=True),
                        waits=(waits if gl == 0 else []), inc="tmm")
                t_sc = p.c("tmm")
                ew = [("tmm", t_sc)]
                if b >= 2:
                    ew.append(("tmm", attnv_done[b - 2]))
                p.emit("scalar", lambda e, b=b, sbank=sbank: e.activation(
                    eop[:, (b % 2) * 512:(b % 2) * 512 + 512], PB[sbank][:, 0:512],
                    AF.Exp, scale=0.125),
                    waits=ew, inc="sca")
                exp_done.append(p.c("sca"))

            def emit_attnv(b):
                obank = 4 + b % 2
                aw = [("sca", exp_done[b])]
                if b >= 2:
                    aw.append(("vec", norm_done[b - 2]))
                for gl in range(4):
                    g = 4 * b + gl
                    p.emit("tensor", lambda e, g=g, gl=gl, b=b, obank=obank: e.matmul(
                        bass.AP(PB[obank], gl * 66, [[512, 128], [1, 65]]),
                        eop[:, (b % 2) * 512 + gl * 128:(b % 2) * 512 + (gl + 1) * 128],
                        vop[:, g * 65:(g + 1) * 65],
                        start=True, stop=True),
                        waits=(aw if gl == 0 else []), inc="tmm")
                attnv_done.append(p.c("tmm"))
                p.emit("vector", lambda e, b=b, obank=obank: e.tensor_copy(
                    den[:, (b % 2) * 8:(b % 2) * 8 + 4],
                    ap(PB[obank], 64, [[512, 128], [66, 4]])),
                    waits=[("tmm", attnv_done[b])], inc="vec")
                p.drain("vector")
                p.emit("vector", lambda e, b=b: e.reciprocal(
                    den[:, (b % 2) * 8 + 4:(b % 2) * 8 + 8],
                    den[:, (b % 2) * 8:(b % 2) * 8 + 4]), inc="vec")
                p.drain("vector")
                p.emit("vector", lambda e, b=b, obank=obank: e.tensor_tensor(
                    ap(qtmp, b * 256, [[4096, 128], [64, 4], [1, 64]]),
                    ap(PB[obank], 0, [[512, 128], [66, 4], [1, 64]]),
                    ap(den, (b % 2) * 8 + 4, [[16, 128], [1, 4], [0, 64]]),
                    op=ALU.mult), inc="vec")
                norm_done.append(p.c("vec"))

            emit_score(0)
            emit_score(1)
            for b in range(NB4):
                emit_attnv(b)
                if b + 2 < NB4:
                    emit_score(b + 2)
            p.mark("attcore")

            ofm_v = 0
            ofm_s = 0
            oq_dma = p.c("shuf")
            for b in range(NG // 4):
                bank = 6 + b % 2
                eng = "vector" if b % 2 == 0 else "scalar"
                bw = [("vec", norm_done[b])]
                if b >= 2:
                    bw += [(("vec", ofm_v) if b % 2 == 0 else ("sca", ofm_s))]
                for gl in range(4):
                    g = 4 * b + gl
                    p.emit("tensor", lambda e, g=g, gl=gl, bank=bank: e.matmul(
                        bass.AP(PB[bank], gl * 128, [[512, 64], [1, 128]]),
                        qtmp[:, g * 64:(g + 1) * 64],
                        ident[:], start=True, stop=True),
                        waits=(bw if gl == 0 else []), inc="tmm")
                t_oi = p.c("tmm")
                sw = [("tmm", t_oi)]
                if b % 4 in (0, 1) and b > 1:
                    sw.append(("shuf", oq_dma))
                for i in range(16):
                    if i % 2 == 0:
                        dst = ap(arena, (i // 2) * 512 + b * 32,
                                 [[4096, 64], [8, 4], [1, 8]])
                    else:
                        dst = ap(oddscr, (i // 2) * 128 + (b % 4) * 32,
                                 [[1024, 64], [8, 4], [1, 8]])
                    srcp = ap(PB[bank], i * 8, [[512, 64], [128, 4], [1, 8]])
                    if eng == "vector":
                        p.emit("vector", lambda e, d=dst, s_=srcp: e.tensor_copy(d, s_),
                               waits=(sw if i == 0 else []), inc="vec")
                    else:
                        p.emit("scalar", lambda e, d=dst, s_=srcp: e.copy(d, s_),
                               waits=(sw if i == 0 else []), inc="sca")
                if eng == "vector":
                    ofm_v = p.c("vec")
                else:
                    ofm_s = p.c("sca")
                if b % 4 == 3:
                    qb = b // 4
                    p.dma("sync",
                          ap(arena, 64 * 4096 + qb * 128, [[4096, 64], [512, 8], [1, 128]]),
                          ap(oddscr, 0, [[1024, 64], [128, 8], [1, 128]]),
                          waits=[("sca", p.c("sca")), ("vec", p.c("vec"))], inc="shuf")
                    oq_dma = p.c("shuf")
            ofm_dma = p.c("shuf")
            p.mark("attn")

            for cc in range(8):
                p.emit("vector", lambda e, cc=cc: e.tensor_scalar_mul(
                    eop[:, cc * 128:(cc + 1) * 128], ident[:], modc[:, 16 + cc:17 + cc]),
                    waits=([("tmm", attnv_done[-1])] if cc == 0 else []), inc="vec")
            dga_v = p.c("vec")
            res_v = 0
            res_v_s = []
            for s in range(NSUB):
                b2k = (s % 2) * 2
                for cc in range(8):
                    waits = []
                    if s == 0 and cc == 0:
                        waits = [("vec", ofm_v), ("sca", ofm_s), ("shuf", ofm_dma),
                                 ("vec", dga_v)]
                    if s >= 2 and cc == 0:
                        waits = [("vec", res_v)]
                    p.emit("tensor", lambda e, s=s, cc=cc, b2k=b2k: e.matmul(
                        bass.AP(PB[b2k + cc // 4], (cc % 4) * 128, [[512, 128], [1, 128]]),
                        arena[:, cc * 512 + s * 128: cc * 512 + s * 128 + 128],
                        eop[:, cc * 128:(cc + 1) * 128],
                        start=True, stop=True),
                        waits=waits, inc="tmm")
                t_ga = p.c("tmm")
                p.emit("vector", lambda e, s=s, b2k=b2k: e.scalar_tensor_tensor(
                    x2_t[:, s * 1024:s * 1024 + 512], x_t[:, s * 1024:s * 1024 + 512],
                    1.0, PB[b2k][:, 0:512], ALU.mult, ALU.add,
                    accum_out=small[:, 16 + s:17 + s]),
                    waits=[("tmm", t_ga)] +
                          ([("xout", x2_free_out)] if s == 0 and tile > 0 else []),
                    inc="vec")
                p.emit("vector", lambda e, s=s, b2k=b2k: e.scalar_tensor_tensor(
                    x2_t[:, s * 1024 + 512:(s + 1) * 1024],
                    x_t[:, s * 1024 + 512:(s + 1) * 1024],
                    1.0, PB[b2k + 1][:, 0:512], ALU.mult, ALU.add,
                    accum_out=small[:, 20 + s:21 + s]),
                    inc="vec")
                res_v = p.c("vec")
                res_v_s.append(res_v)
            x_free = res_v
            for cc in range(8):
                p.emit("vector", lambda e, cc=cc: e.tensor_scalar_mul(
                    eop[:, cc * 128:(cc + 1) * 128], ident[:], modc[:, 40 + cc:41 + cc]),
                    waits=([("tmm", t_ga)] if cc == 0 else []), inc="vec")
            dgb_v = p.c("vec")
            p.mark("res1")

            for s in range(NSUB):
                p.emit("scalar", lambda e, s=s: e.activation(
                    pb6[:, 0:512], x2_t[:, s * 1024:s * 1024 + 512], AF.Square,
                    accum_out=small[:, 24 + s:25 + s]),
                    waits=[("vec", res_v_s[s])], inc="sca")
                p.emit("scalar", lambda e, s=s: e.activation(
                    pb7[:, 0:512], x2_t[:, s * 1024 + 512:(s + 1) * 1024], AF.Square,
                    accum_out=small[:, 28 + s:29 + s]),
                    inc="sca")
            p.emit("vector", lambda e: e.tensor_add(small[:, 16:20], small[:, 16:20],
                                                    small[:, 20:24]),
                   waits=[("vec", res_v)], inc="vec")
            v_nmr2 = emit_ln_chain(16, 24, 28, True)
            s_xhat2_s = []
            for s in range(NSUB):
                p.emit("scalar", lambda e, s=s: e.activation(
                    arena[:, s * 1024:(s + 1) * 1024], x2_t[:, s * 1024:(s + 1) * 1024],
                    AF.Identity, bias=small[:, 8 + s:9 + s], scale=small[:, s:s + 1]),
                    waits=([("vec", v_nmr2)] if s == 0 else []), inc="sca")
                s_xhat2_s.append(p.c("sca"))

            xT2_v, xT2_s = emit_transposes(arena, s_xhat2_s, [])

            for c2 in range(2):
                bank = 4 + c2
                for kc in range(8):
                    p.emit("tensor", lambda e, c2=c2, kc=kc, bank=bank: e.matmul(
                        PB[bank][:, 0:512],
                        u1_sb[:, kc * 256 + c2 * 128: kc * 256 + (c2 + 1) * 128],
                        xhatT[:, kc * 512:(kc + 1) * 512],
                        start=(kc == 0), stop=(kc == 7)),
                        waits=([("vec", xT2_v), ("sca", xT2_s)]
                               if (c2 == 0 and kc == 0) else []),
                        inc="tmm")
                t_pt = p.c("tmm")
                p.emit("vector", lambda e, c2=c2, bank=bank: e.tensor_scalar_add(
                    ptsb[:, c2 * 512:(c2 + 1) * 512], PB[bank][:, 0:512],
                    bcols[:, 24 + c2:25 + c2]),
                    waits=[("tmm", t_pt)], inc="vec")
            pt_v = p.c("vec")
            p.mark("pt")

            gelu_done = []

            def emit_h(fc):
                bank = fc % 4
                for rc in range(2):
                    waits = []
                    if fc == 0 and rc == 0:
                        waits = [("vec", pt_v)]
                    if fc >= 4 and rc == 0:
                        waits = [("sca", gelu_done[fc - 4])]
                    p.emit("tensor", lambda e, fc=fc, rc=rc, bank=bank: e.matmul(
                        PB[bank][:, 0:512],
                        v1_sb[:, rc * 4096 + fc * 128: rc * 4096 + (fc + 1) * 128],
                        ptsb[:, rc * 512:(rc + 1) * 512],
                        start=(rc == 0), stop=(rc == 1)),
                        waits=waits, inc="tmm")
                t_h = p.c("tmm")
                p.emit("scalar", lambda e, fc=fc, bank=bank: e.activation(
                    htring[:, (fc % 2) * 512:(fc % 2) * 512 + 512], PB[bank][:, 0:512],
                    AF.Gelu_apprx_tanh, bias=bcols[:, 32 + fc:33 + fc]),
                    waits=[("tmm", t_h)], inc="sca")
                gelu_done.append(p.c("sca"))

            def emit_u2(fc):
                for c2 in range(2):
                    p.emit("tensor", lambda e, fc=fc, c2=c2: e.matmul(
                        PB[6 + c2][:, 0:512],
                        u2_sb[:, fc * 256 + c2 * 128: fc * 256 + (c2 + 1) * 128],
                        htring[:, (fc % 2) * 512:(fc % 2) * 512 + 512],
                        start=(fc == 0), stop=(fc == 31), skip_group_check=True),
                        waits=([("sca", gelu_done[fc])] if c2 == 0 else []), inc="tmm")

            emit_h(0)
            emit_h(1)
            for fc in range(32):
                emit_u2(fc)
                if fc + 2 < 32:
                    emit_h(fc + 2)
            t_q2 = p.c("tmm")
            for c2 in range(2):
                p.emit("vector", lambda e, c2=c2: e.tensor_copy(
                    q2sb[:, c2 * 512:(c2 + 1) * 512], PB[6 + c2][:, 0:512]),
                    waits=[("tmm", t_q2)], inc="vec")
            q2_v = p.c("vec")
            p.mark("q2")

            fprime_tok = []
            for cc in range(8):
                bank = 4 + cc % 2
                for rc in range(2):
                    waits = []
                    if cc == 0 and rc == 0:
                        waits = [("vec", q2_v)]
                    if cc >= 2 and rc == 0:
                        waits = list(fprime_tok[cc - 2])
                    p.emit("tensor", lambda e, cc=cc, rc=rc, bank=bank: e.matmul(
                        PB[bank][:, 0:512],
                        v2_sb[:, rc * 1024 + cc * 128: rc * 1024 + (cc + 1) * 128],
                        q2sb[:, rc * 512:(rc + 1) * 512],
                        start=(rc == 0), stop=(rc == 1)),
                        waits=waits, inc="tmm")
                t_f = p.c("tmm")
                if cc % 2 == 0:
                    p.emit("vector", lambda e, cc=cc, bank=bank: e.tensor_scalar_add(
                        vint[:, cc * 512:(cc + 1) * 512], PB[bank][:, 0:512],
                        bcols[:, 64 + cc:65 + cc]),
                        waits=[("tmm", t_f)], inc="vec")
                    fprime_tok.append([("vec", p.c("vec"))])
                else:
                    p.emit("scalar", lambda e, cc=cc, bank=bank: e.activation(
                        vint[:, cc * 512:(cc + 1) * 512], PB[bank][:, 0:512],
                        AF.Identity, bias=bcols[:, 64 + cc:65 + cc]),
                        waits=[("tmm", t_f)], inc="sca")
                    fprime_tok.append([("sca", p.c("sca"))])
            fp_s = p.c("sca")
            fp_v = p.c("vec")
            p.mark("fp")

            if tile + 1 < n_tiles:
                t_ga_all = t_ga
                s_xhat_s_next = emit_ln1(tile + 1,
                                         [("vec", x_free), ("tmm", t_ga_all)])
            fin_v = 0
            for s in range(NSUB):
                b2k = (s % 2) * 2
                for cc in range(8):
                    waits = []
                    if s == 0 and cc == 0:
                        waits = [("sca", fp_s), ("vec", fp_v), ("vec", dgb_v)]
                    if s >= 2 and cc == 0:
                        waits = [("vec", fin_v)]
                    p.emit("tensor", lambda e, s=s, cc=cc, b2k=b2k: e.matmul(
                        bass.AP(PB[b2k + cc // 4], (cc % 4) * 128, [[512, 128], [1, 128]]),
                        vint[:, cc * 512 + s * 128: cc * 512 + s * 128 + 128],
                        eop[:, cc * 128:(cc + 1) * 128],
                        start=True, stop=True),
                        waits=waits, inc="tmm")
                t_gb = p.c("tmm")
                p.emit("vector", lambda e, s=s, b2k=b2k: e.tensor_add(
                    x2_t[:, s * 1024:s * 1024 + 512], x2_t[:, s * 1024:s * 1024 + 512],
                    PB[b2k][:, 0:512]),
                    waits=[("tmm", t_gb)], inc="vec")
                p.emit("vector", lambda e, s=s, b2k=b2k: e.tensor_add(
                    x2_t[:, s * 1024 + 512:(s + 1) * 1024],
                    x2_t[:, s * 1024 + 512:(s + 1) * 1024],
                    PB[b2k + 1][:, 0:512]),
                    inc="vec")
                fin_v = p.c("vec")
                p.dma("sync",
                      ap(y_h, tile * T * D + s * 128 * D, [[D, 128], [1, D]]),
                      x2_t[:, s * 1024:(s + 1) * 1024],
                      waits=[("vec", fin_v)], inc="xout")
            fin_v_prev = fin_v
            x2_free_out = p.c("xout")
            if tile + 1 < n_tiles:
                s_xhat_s = s_xhat_s_next

        if phase is not None:
            print("PLAN: truncating at", phase, "=", p.marks[phase], "of", p.seq)
            p.truncate(p.marks[phase])
        if dbg:
            last = p.wall("vec", "sca", "tmm", "xout", "shuf")
            dump_map = {
                "d_qop": qop[:], "d_kop": kop[:], "d_vop": vop[:], "d_eop": eop[:],
                "d_otmp": qtmp[:], "d_ofm": arena[:], "d_xhatT": xhatT[:],
                "d_x2": x2_t[:], "d_modc": modc[:], "d_bcols": bcols[:],
                "d_vint": vint[:], "d_ptsb": ptsb[:], "d_q2sb": q2sb[:],
                "d_small": small[:], "d_scolb": scolb[:],
            }
            first = True
            for name in dbg:
                p.dma("gpsimd", dbg[name], dump_map[name],
                      waits=(last if first else []), inc="dma")
                first = False

        p.emit("sync", lambda e: e.nop(),
               waits=p.wall("xout", "dma", "shuf", "cast"))

        with nc.Block() as block:
            p.realize(block, sems)

    return nc


def host_consts():
    ident = np.eye(128, dtype=BF)
    i64d = np.zeros((128, 64), dtype=BF)
    i64d[0:64] = np.eye(64, dtype=BF)
    i64d[64:128] = np.eye(64, dtype=BF)
    base = np.zeros((9, 128), dtype=np.float32)
    for r in range(8):
        for t in range(8):
            for h in range(16):
                base[r, h * 8 + t] = 32.0 if t == r else 0.0
    maskk = np.tile(base, (1, 64)).astype(BF)
    maskq = np.tile(base, (1, 64)).astype(BF)
    maskk[8, :] = np.float32(-32.0)
    maskq[8, :] = np.float32(32.0)
    pswap = np.roll(np.eye(128, dtype=np.float32), 64, axis=0).astype(BF)
    return ident, i64d, maskq, maskk, pswap


_NC_CACHE = {}


def kernel(L0_attn_x, L0_attn_Wq, L0_attn_bq, L0_attn_Wk, L0_attn_bk, L0_attn_Wv,
           L0_attn_bv, L0_ffn_U1, L0_ffn_V1, L0_ffn_b1, L0_ffn_U2, L0_ffn_V2,
           L0_ffn_b2, L0_c, L0_wmod):
    x = np.ascontiguousarray(np.asarray(L0_attn_x, dtype=np.float32))
    B, S, Dm = x.shape
    half = S // 2
    ident, i64d, maskq, maskk, pswap = host_consts()
    shared = {
        "wq": np.ascontiguousarray(L0_attn_Wq, dtype=np.float32),
        "wk": np.ascontiguousarray(L0_attn_Wk, dtype=np.float32),
        "wv": np.ascontiguousarray(L0_attn_Wv, dtype=np.float32),
        "bq": np.asarray(L0_attn_bq, dtype=np.float32).reshape(1, Dm),
        "bk": np.asarray(L0_attn_bk, dtype=np.float32).reshape(1, Dm),
        "bv": np.asarray(L0_attn_bv, dtype=np.float32).reshape(1, Dm),
        "u1": np.ascontiguousarray(L0_ffn_U1, dtype=np.float32),
        "v1": np.ascontiguousarray(L0_ffn_V1, dtype=np.float32),
        "b1": np.asarray(L0_ffn_b1, dtype=np.float32).reshape(1, DFF),
        "u2": np.ascontiguousarray(L0_ffn_U2, dtype=np.float32),
        "v2": np.ascontiguousarray(L0_ffn_V2, dtype=np.float32),
        "b2": np.asarray(L0_ffn_b2, dtype=np.float32).reshape(1, Dm),
        "wmod": np.ascontiguousarray(L0_wmod, dtype=np.float32),
        "ident": ident, "i64d": i64d, "maskq": maskq, "maskk": maskk,
        "pswap": pswap,
    }
    cvec = np.asarray(L0_c, dtype=np.float32)
    n_tiles = (half + T - 1) // T
    if n_tiles not in _NC_CACHE:
        _NC_CACHE[n_tiles] = build_core_program(n_tiles)
    nc = _NC_CACHE[n_tiles]
    in_maps = []
    for core in range(8):
        b = core // 2
        h0 = (core % 2) * half
        m = dict(shared)
        m["x"] = np.ascontiguousarray(x[b, h0:h0 + half])
        m["cvec"] = np.ascontiguousarray(cvec[b:b + 1])
        in_maps.append(m)
    res = run_bass_kernel_spmd(nc, in_maps, core_ids=list(range(8)))
    out = np.empty((B, S, Dm), dtype=np.float32)
    for core in range(8):
        b = core // 2
        h0 = (core % 2) * half
        out[b, h0:h0 + half] = res.results[core]["y"]
    return out
